# version 16
# baseline (speedup 1.0000x reference)
import contextlib

import numpy as np
import ml_dtypes

import concourse.bass as bass
import concourse.mybir as mybir
from concourse.bass_utils import run_bass_kernel_spmd

F32 = mybir.dt.float32
BF16 = mybir.dt.bfloat16
I32 = mybir.dt.int32
AF = mybir.ActivationFunctionType
ALU = mybir.AluOpType
AX = mybir.AxisListType

DM = 2048
DI = 4096
NST = 16
DTR = 128
L = 2048
LC = 512
NCH = L // LC
NDB = DM // 128
NIB = DI // 128
IFF = 5632
NIC = IFF // 512
RT = LC // 128
LN2C = float(np.log(2.0))
IVL2 = float(1.0 / np.log(2.0))
EPS = 1e-5


class Ctr:
    def __init__(self, sem, step=1):
        self.sem = sem
        self.n = 0
        self.step = step

    def bump(self, instr):
        instr.then_inc(self.sem, self.step)
        self.n += self.step
        return (self.sem, self.n)

    def bumpw(self, eng, instr):
        tk = self.bump(instr)
        eng.wait_ge(*tk)
        return tk

    def tok(self):
        return (self.sem, self.n)


def W(eng, *toks):
    for tk in toks:
        if tk is None:
            continue
        sem, v = tk
        if v > 0:
            eng.wait_ge(sem, v)


def emit_quant(blk, sV, sA, *, src, scale_ap, dq_ap, scr1, scr2, scr3, iscr,
               out_bf, cstref, pre_toks=(), scalar_scales=False):
    @blk.vector
    def _(v):
        W(v, *pre_toks)
        if scalar_scales:
            sV.bumpw(v, v.tensor_scalar(out=scr1, in0=src, scalar1=scale_ap, scalar2=None, op0=ALU.mult))
        else:
            sV.bumpw(v, v.tensor_tensor(out=scr1, in0=src, in1=scale_ap, op=ALU.mult))
    t1 = sV.tok()

    @blk.scalar
    def _(s):
        W(s, t1)
        sA.bumpw(s, s.activation(out=scr2, in_=scr1, func=AF.Abs, bias=0.0, scale=1.0))
        sA.bumpw(s, s.activation(out=scr2, in_=scr2, func=AF.Ln, bias=cstref[:, 4:5], scale=1.0))
    t2 = sA.tok()

    @blk.vector
    def _(v):
        W(v, t2)
        sV.bumpw(v, v.tensor_scalar(out=scr2, in0=scr2, scalar1=IVL2, scalar2=None,
                                    op0=ALU.mult))
        sV.bumpw(v, v.tensor_copy(out=iscr, in_=scr2))
        sV.bumpw(v, v.tensor_copy(out=scr3, in_=iscr))
        sV.bumpw(v, v.tensor_tensor(out=scr2, in0=scr2, in1=scr3, op=ALU.subtract))
        sV.bumpw(v, v.tensor_scalar(out=out_bf, in0=scr2, scalar1=0.0, scalar2=None, op0=ALU.is_lt))
        sV.bumpw(v, v.tensor_scalar(out=scr2, in0=scr2, scalar1=1.0, scalar2=None, op0=ALU.is_ge))
        sV.bumpw(v, v.tensor_tensor(out=scr3, in0=scr3, in1=out_bf, op=ALU.subtract))
        sV.bumpw(v, v.tensor_tensor(out=scr3, in0=scr3, in1=scr2, op=ALU.add))
        sV.bumpw(v, v.tensor_copy(out=iscr, in_=scr3))
        sV.bumpw(v, v.tensor_copy(out=scr3, in_=iscr))
        sV.bumpw(v, v.tensor_scalar(out=scr3, in0=scr3, scalar1=-3.0, scalar2=-9.0,
                                    op0=ALU.add, op1=ALU.max))
        sV.bumpw(v, v.tensor_scalar(out=scr3, in0=scr3, scalar1=LN2C, scalar2=None, op0=ALU.mult))
    t3 = sV.tok()

    @blk.scalar
    def _(s):
        W(s, t3)
        sA.bumpw(s, s.activation(out=out_bf, in_=scr3, func=AF.Exp, bias=0.0, scale=cstref[:, 2:3]))
        sA.bumpw(s, s.activation(out=scr3, in_=scr3, func=AF.Exp, bias=0.0, scale=1.0))
    t4 = sA.tok()

    @blk.vector
    def _(v):
        W(v, t4)
        sV.bumpw(v, v.tensor_tensor(out=scr1, in0=scr1, in1=out_bf, op=ALU.mult))
        sV.bumpw(v, v.tensor_copy(out=iscr, in_=scr1))
        sV.bumpw(v, v.tensor_copy(out=scr2, in_=iscr))
        sV.bumpw(v, v.tensor_tensor(out=scr1, in0=scr1, in1=scr2, op=ALU.subtract))
        sV.bumpw(v, v.tensor_scalar(out=out_bf, in0=scr1, scalar1=0.5, scalar2=None, op0=ALU.is_gt))
        sV.bumpw(v, v.tensor_scalar(out=scr1, in0=scr1, scalar1=-0.5, scalar2=None, op0=ALU.is_lt))
        sV.bumpw(v, v.tensor_tensor(out=scr2, in0=scr2, in1=out_bf, op=ALU.add))
        sV.bumpw(v, v.tensor_tensor(out=scr2, in0=scr2, in1=scr1, op=ALU.subtract))
        if scalar_scales:
            sV.bumpw(v, v.tensor_scalar(out=scr3, in0=scr3, scalar1=dq_ap, scalar2=None, op0=ALU.mult))
        else:
            sV.bumpw(v, v.tensor_tensor(out=scr3, in0=scr3, in1=dq_ap, op=ALU.mult))
        sV.bumpw(v, v.tensor_tensor(out=out_bf, in0=scr2, in1=scr3, op=ALU.mult))
    return sV.tok()


def emit_quant_pure(blk, sV, sA, *, src, scale_ap, dq_ap, scr1, scr2, scr3, scr4,
                    out_bf, cstref, pre_toks=(), tt_scales=False):
    @blk.vector
    def _(v):
        W(v, *pre_toks)
        if tt_scales:
            sV.bumpw(v, v.tensor_tensor(out=scr1, in0=src, in1=scale_ap, op=ALU.mult))
        else:
            sV.bumpw(v, v.tensor_scalar(out=scr1, in0=src, scalar1=scale_ap, scalar2=None, op0=ALU.mult))
    t1 = sV.tok()

    @blk.scalar
    def _(s):
        W(s, t1)
        sA.bumpw(s, s.activation(out=scr2, in_=scr1, func=AF.Abs, bias=0.0, scale=1.0))
        sA.bumpw(s, s.activation(out=scr2, in_=scr2, func=AF.Ln, bias=cstref[:, 4:5], scale=1.0))
    t2 = sA.tok()

    @blk.vector
    def _(v):
        W(v, t2)
        sV.bumpw(v, v.tensor_scalar(out=scr2, in0=scr2, scalar1=IVL2, scalar2=-6.99,
                                    op0=ALU.mult, op1=ALU.max))
        sV.bumpw(v, v.tensor_scalar(out=scr3, in0=scr2, scalar1=0.0, scalar2=-7.0,
                                    op0=ALU.mult, op1=ALU.add))
        for dl in (8.0, 4.0, 2.0, 1.0):
            sV.bumpw(v, v.scalar_tensor_tensor(out=scr4, in0=scr2, scalar=-dl, in1=scr3,
                                               op0=ALU.add, op1=ALU.is_ge))
            sV.bumpw(v, v.scalar_tensor_tensor(out=scr3, in0=scr4, scalar=dl, in1=scr3,
                                               op0=ALU.mult, op1=ALU.add))
        sV.bumpw(v, v.tensor_scalar(out=scr3, in0=scr3, scalar1=-3.0, scalar2=-9.0,
                                    op0=ALU.add, op1=ALU.max))
        sV.bumpw(v, v.tensor_scalar(out=scr3, in0=scr3, scalar1=LN2C, scalar2=None, op0=ALU.mult))
    t3 = sV.tok()

    @blk.scalar
    def _(s):
        W(s, t3)
        sA.bumpw(s, s.activation(out=out_bf, in_=scr3, func=AF.Exp, bias=0.0, scale=cstref[:, 2:3]))
        sA.bumpw(s, s.activation(out=scr3, in_=scr3, func=AF.Exp, bias=0.0, scale=1.0))
    t4 = sA.tok()

    @blk.vector
    def _(v):
        W(v, t4)
        sV.bumpw(v, v.tensor_tensor(out=scr1, in0=scr1, in1=out_bf, op=ALU.mult))
        sV.bumpw(v, v.tensor_scalar(out=scr2, in0=scr1, scalar1=0.0, scalar2=-16.0,
                                    op0=ALU.mult, op1=ALU.add))
        for dl in (16.0, 8.0, 4.0, 2.0, 1.0):
            sV.bumpw(v, v.scalar_tensor_tensor(out=scr4, in0=scr1, scalar=-(dl - 0.5), in1=scr2,
                                               op0=ALU.add, op1=ALU.is_ge))
            sV.bumpw(v, v.scalar_tensor_tensor(out=scr2, in0=scr4, scalar=dl, in1=scr2,
                                               op0=ALU.mult, op1=ALU.add))
        sV.bumpw(v, v.scalar_tensor_tensor(out=scr4, in0=scr1, scalar=-0.5, in1=scr2,
                                           op0=ALU.add, op1=ALU.is_ge))
        sV.bumpw(v, v.tensor_tensor(out=scr2, in0=scr2, in1=scr4, op=ALU.add))
        if tt_scales:
            sV.bumpw(v, v.tensor_tensor(out=scr3, in0=scr3, in1=dq_ap, op=ALU.mult))
        else:
            sV.bumpw(v, v.tensor_scalar(out=scr3, in0=scr3, scalar1=dq_ap, scalar2=None, op0=ALU.mult))
        sV.bumpw(v, v.tensor_tensor(out=out_bf, in0=scr2, in1=scr3, op=ALU.mult))
    return sV.tok()


def build_program():
    nc = bass.Bass("TRN2", target_bir_lowering=False, debug=False)

    d_xT = nc.dram_tensor("xT", [DM, L], F32, kind="ExternalInput").ap()
    d_cv = nc.dram_tensor("cvecT", [128, 16], F32, kind="ExternalInput").ap()
    d_wmod = nc.dram_tensor("wmod", [DM, 6 * DM], F32, kind="ExternalInput").ap()
    d_WinT = nc.dram_tensor("WinT", [DM, 2 * DI], F32, kind="ExternalInput").ap()
    d_convw = nc.dram_tensor("convw", [128, 128], F32, kind="ExternalInput").ap()
    d_convb = nc.dram_tensor("convb", [128, 32], F32, kind="ExternalInput").ap()
    d_bdt = nc.dram_tensor("bdt", [128, 32], F32, kind="ExternalInput").ap()
    d_Alog = nc.dram_tensor("Alog", [128, 512], F32, kind="ExternalInput").ap()
    d_Dt = nc.dram_tensor("Dt", [128, 32], F32, kind="ExternalInput").ap()
    d_bout = nc.dram_tensor("bout", [128, 16], F32, kind="ExternalInput").ap()
    d_WxT = nc.dram_tensor("WxT", [DI, 160], F32, kind="ExternalInput").ap()
    d_WdtT = nc.dram_tensor("WdtT", [128, DI], F32, kind="ExternalInput").ap()
    d_WoutT = nc.dram_tensor("WoutT", [DI, DM], F32, kind="ExternalInput").ap()
    d_wgS = nc.dram_tensor("wgS", [DM // 8, IFF], F32, kind="ExternalInput").ap()
    d_wuS = nc.dram_tensor("wuS", [DM // 8, IFF], F32, kind="ExternalInput").ap()
    d_wdS = nc.dram_tensor("wdS", [IFF // 8, DM], F32, kind="ExternalInput").ap()
    d_out = nc.dram_tensor("out", [L, DM], F32, kind="ExternalOutput").ap()

    d_WinB = nc.dram_tensor("WinB", [DM, 2 * DI], BF16, kind="Internal").ap()
    d_WoutB = nc.dram_tensor("WoutB", [DI, DM], BF16, kind="Internal").ap()
    d_WxB = nc.dram_tensor("WxB_d", [128, 32 * 160], BF16, kind="Internal").ap()
    d_WdtB = nc.dram_tensor("WdtB_d", [128, DI], BF16, kind="Internal").ap()
    t_QG = nc.dram_tensor("QG", [DM, IFF], BF16, kind="Internal", addr_space="Shared"); d_QG = t_QG.ap()
    t_QU = nc.dram_tensor("QU", [DM, IFF], BF16, kind="Internal", addr_space="Shared"); d_QU = t_QU.ap()
    t_QD = nc.dram_tensor("QD", [IFF, DM], BF16, kind="Internal", addr_space="Shared"); d_QD = t_QD.ap()
    t_qgS = nc.dram_tensor("qgS", [DM // 8, IFF], BF16, kind="Internal"); d_qgS = t_qgS.ap()
    t_quS = nc.dram_tensor("quS", [DM // 8, IFF], BF16, kind="Internal"); d_quS = t_quS.ap()
    t_qdS = nc.dram_tensor("qdS", [IFF // 8, DM], BF16, kind="Internal"); d_qdS = t_qdS.ap()
    d_modrow = nc.dram_tensor("modrow", [1, 6 * DM], F32, kind="Internal").ap()
    t_amaxg = nc.dram_tensor("amaxg", [1, IFF], F32, kind="Internal", addr_space="Shared"); d_amaxg = t_amaxg.ap()
    t_amaxu = nc.dram_tensor("amaxu", [1, IFF], F32, kind="Internal", addr_space="Shared"); d_amaxu = t_amaxu.ap()
    t_amaxd = nc.dram_tensor("amaxd", [1, DM], F32, kind="Internal", addr_space="Shared"); d_amaxd = t_amaxd.ap()
    t_amgP = nc.dram_tensor("amgP", [1, IFF], F32, kind="Internal"); d_amgP = t_amgP.ap()
    t_amuP = nc.dram_tensor("amuP", [1, IFF], F32, kind="Internal"); d_amuP = t_amuP.ap()
    t_amdP = nc.dram_tensor("amdP", [1, DM], F32, kind="Internal"); d_amdP = t_amdP.ap()
    d_srow = nc.dram_tensor("srow", [NCH, 2 * LC], F32, kind="Internal").ap()
    d_bc = nc.dram_tensor("bcrows", [NCH, 32 * LC], BF16, kind="Internal").ap()
    d_x2 = nc.dram_tensor("x2sp", [L, DM], BF16, kind="Internal").ap()
    d_t0p = [nc.dram_tensor("t0sp0", [IFF, LC], BF16, kind="Internal").ap(),
             nc.dram_tensor("t0sp1", [IFF, LC], BF16, kind="Internal").ap()]
    d_qtp = [nc.dram_tensor("qtsp0", [IFF, LC], BF16, kind="Internal").ap(),
             nc.dram_tensor("qtsp1", [IFF, LC], BF16, kind="Internal").ap()]
    d_tamax = nc.dram_tensor("tamax", [NCH, LC], F32, kind="Internal").ap()

    ident = nc.inline_tensor(np.eye(128, dtype=ml_dtypes.bfloat16), name="identbf")

    es0 = contextlib.ExitStack()
    with es0:
        def SEM(name, step=1):
            return Ctr(es0.enter_context(nc.semaphore(name)), step)

        sV = SEM("sV"); sA = SEM("sA"); sT = SEM("sT"); sG = SEM("sG")
        dIn = SEM("dIn", 16); dW0 = SEM("dW0", 16); dW1 = SEM("dW1", 16)
        dCn = SEM("dCn", 16); dSp = SEM("dSp", 16); dBr = SEM("dBr", 16)
        dT0 = SEM("dT0", 16); dQt = SEM("dQt", 16); dOu = SEM("dOu", 16)
        dTq = SEM("dTq", 16); dQs = SEM("dQs", 16)
        ccRg = es0.enter_context(nc.semaphore("ccRg"))
        ccRu = es0.enter_context(nc.semaphore("ccRu"))
        ccRd = es0.enter_context(nc.semaphore("ccRd"))
        ccGg = es0.enter_context(nc.semaphore("ccGg"))
        ccGu = es0.enter_context(nc.semaphore("ccGu"))
        ccGd = es0.enter_context(nc.semaphore("ccGd"))

        def P0(name, shape, dt):
            return es0.enter_context(nc.sbuf_tensor(name, shape, dt))

        idn = P0("idn", [128, 128], BF16)
        convw = P0("convw_s", [128, 128], F32)
        convb = P0("convb_s", [128, 32], F32)
        bdt = P0("bdt_s", [128, 32], F32)
        Aneg = P0("Aneg", [128, 512], F32)
        Dt = P0("Dt_s", [128, 32], F32)
        bout = P0("bout_s", [128, 16], F32)
        ca1 = P0("ca1", [128, 16], F32)
        sa_t = P0("sa_t", [128, 16], F32)
        gaf = P0("gaf", [128, 16], F32)
        gab = P0("gab", [128, 16], F32)
        cst = P0("cst", [128, 8], F32)

        esA = contextlib.ExitStack()
        with esA:
            def SB(name, shape, dt):
                return esA.enter_context(nc.sbuf_tensor(name, shape, dt))

            wbuf = SB("wbufA", [128, 8192], F32)
            wb2 = SB("wb2A", [128, IFF], F32)
            qsl = SB("qslA", [128, IFF], F32)
            qs3 = SB("qs3A", [128, IFF], F32)
            qint = SB("qintA", [128, IFF], I32)
            qout = SB("qoutA", [128, 8192], BF16)
            sc_b = SB("scbA", [128, IFF], BF16)
            dq_b = SB("dqbA", [128, IFF], BF16)
            siluc = SB("siluc", [128, 16], BF16)
            mrow = SB("mrowA", [1, 512], F32)
            ps1 = esA.enter_context(nc.psum_tensor("psA1", [1, 512], F32))

            blk = esA.enter_context(nc.Block())

            @blk.sync
            def _(sync):
                sync.dma_start(out=convw[:], in_=d_convw).then_inc(dCn.sem, 16); dCn.n += 16
                sync.dma_start(out=convb[:], in_=d_convb).then_inc(dCn.sem, 16); dCn.n += 16
                sync.dma_start(out=bdt[:], in_=d_bdt).then_inc(dCn.sem, 16); dCn.n += 16
                sync.dma_start(out=Aneg[:], in_=d_Alog).then_inc(dCn.sem, 16); dCn.n += 16
                sync.dma_start(out=Dt[:], in_=d_Dt).then_inc(dCn.sem, 16); dCn.n += 16
                sync.dma_start(out=bout[:], in_=d_bout).then_inc(dCn.sem, 16); dCn.n += 16
                sync.dma_start(out=idn[:], in_=ident.ap()).then_inc(dCn.sem, 16); dCn.n += 16
                sync.dma_start(out=wbuf[:, 0:16], in_=d_cv).then_inc(dCn.sem, 16); dCn.n += 16
            t_cn = dCn.tok()

            @blk.vector
            def _(v):
                for col, val in ((0, EPS), (1, -0.5), (2, -1.0), (3, 1.0), (4, 1e-38), (5, 0.5)):
                    sV.bumpw(v, v.memset(cst[:, col:col + 1], val))
            t_cst = sV.tok()

            @blk.scalar
            def _(s):
                W(s, t_cn, t_cst)
                sA.bumpw(s, s.activation(out=Aneg[:], in_=Aneg[:], func=AF.Exp, bias=0.0, scale=1.0))
                sA.bumpw(s, s.activation(out=siluc[:], in_=wbuf[:, 0:16], func=AF.Silu, bias=0.0, scale=1.0))
            t_acts = sA.tok()

            @blk.vector
            def _(v):
                W(v, t_acts)
                sV.bumpw(v, v.tensor_scalar(out=Aneg[:], in0=Aneg[:], scalar1=-1.0, scalar2=None, op0=ALU.mult))

            @blk.sync
            def _(sync):
                W(sync, t_acts)
                sync.dma_start(out=wbuf[:, 0:5120],
                               in_=d_WxT.rearrange("(j p) r -> p j r", p=128)).then_inc(dCn.sem, 16); dCn.n += 16
                sync.dma_start(out=qsl[:, 0:DI], in_=d_WdtT).then_inc(dCn.sem, 16); dCn.n += 16
            t_wx = dCn.tok()

            @blk.vector
            def _(v):
                W(v, t_wx)
                sV.bumpw(v, v.tensor_copy(out=qout[:, 0:5120], in_=wbuf[:, 0:5120]))
                sV.bumpw(v, v.tensor_copy(out=qout[:, 5120:5120 + 2048], in_=qsl[:, 0:2048]))
                sV.bumpw(v, v.tensor_copy(out=sc_b[:, 0:2048], in_=qsl[:, 2048:DI]))
            t_wxc = sV.tok()

            @blk.sync
            def _(sync):
                W(sync, t_wxc)
                sync.dma_start(out=d_WxB, in_=qout[:, 0:5120]).then_inc(dSp.sem, 16); dSp.n += 16
                sync.dma_start(out=d_WdtB[:, 0:2048], in_=qout[:, 5120:5120 + 2048]).then_inc(dSp.sem, 16); dSp.n += 16
                sync.dma_start(out=d_WdtB[:, 2048:DI], in_=sc_b[:, 0:2048]).then_inc(dSp.sem, 16); dSp.n += 16

            for mg in range(24):
                @blk.sync
                def _(sync, mg=mg):
                    W(sync, sV.tok())
                    sync.dma_start(out=wbuf[:],
                                   in_=d_wmod[:, mg * 512:(mg + 1) * 512].rearrange("(j p) c -> p j c", p=128)
                                   ).then_inc(dCn.sem, 16); dCn.n += 16
                t_l = dCn.tok()

                @blk.vector
                def _(v, t_l=t_l):
                    W(v, t_l, sT.tok(), dSp.tok())
                    sV.bumpw(v, v.tensor_copy(out=qout[:], in_=wbuf[:]))
                t_c = sV.tok()

                @blk.tensor
                def _(t, t_c=t_c):
                    W(t, t_c, t_acts, sV.tok())
                    for kb in range(16):
                        ins = t.matmul(ps1[:], siluc[:, kb:kb + 1], qout[:, kb * 512:(kb + 1) * 512],
                                       start=(kb == 0), stop=(kb == 15))
                    sT.bump(ins)
                t_m = sT.tok()

                @blk.vector
                def _(v, t_m=t_m):
                    W(v, t_m, dSp.tok())
                    sV.bumpw(v, v.tensor_copy(out=mrow[:], in_=ps1[:]))
                t_e = sV.tok()

                @blk.sync
                def _(sync, mg=mg, t_e=t_e):
                    W(sync, t_e)
                    sync.dma_start(out=d_modrow[0:1, mg * 512:(mg + 1) * 512], in_=mrow[:]
                                   ).then_inc(dSp.sem, 16); dSp.n += 16
            t_modsp = dSp.tok()

            @blk.sync
            def _(sync):
                W(sync, t_modsp)
                with nc.allow_non_contiguous_dma(reason="tiny one-time per-partition mod vectors"):
                    for dst, off in ((sa_t, 0), (ca1, DM), (gaf, 2 * DM)):
                        sync.dma_start(out=dst[:], in_=d_modrow[0:1, off:off + DM].rearrange("a (j p) -> p a j", p=128)
                                       ).then_inc(dBr.sem, 16); dBr.n += 16
            t_br1 = dBr.tok()

            @blk.vector
            def _(v):
                W(v, t_br1)
                sV.bumpw(v, v.tensor_scalar(out=ca1[:], in0=ca1[:], scalar1=1.0, scalar2=None, op0=ALU.add))
                sV.bumpw(v, v.tensor_scalar(out=wb2[:, 0:16], in0=gaf[:], scalar1=0.0, scalar2=None, op0=ALU.is_ge))
                sV.bumpw(v, v.tensor_scalar(out=wb2[:, 0:16], in0=wb2[:, 0:16], scalar1=2.0, scalar2=-1.0,
                                        op0=ALU.mult, op1=ALU.add))
                sV.bumpw(v, v.tensor_scalar(out=wb2[:, 16:32], in0=gaf[:], scalar1=-1.0, scalar2=None, op0=ALU.mult))
                sV.bumpw(v, v.tensor_tensor(out=wb2[:, 16:32], in0=gaf[:], in1=wb2[:, 16:32], op=ALU.max))
                sV.bumpw(v, v.tensor_scalar(out=wb2[:, 16:32], in0=wb2[:, 16:32], scalar1=0.25, scalar2=None, op0=ALU.max))
                sV.bumpw(v, v.tensor_tensor(out=gaf[:], in0=wb2[:, 0:16], in1=wb2[:, 16:32], op=ALU.mult))
                sV.bumpw(v, v.tensor_tensor(out=gab[:], in0=gaf[:], in1=bout[:], op=ALU.mult))

            def cast_weight(dsrc, ddst, ncols):
                nslab = dsrc.shape[0] // 128
                for slx in range(nslab):
                    @blk.sync
                    def _(sync, slx=slx):
                        W(sync, dSp.tok())
                        sync.dma_start(out=wbuf[:, 0:ncols], in_=dsrc[slx * 128:(slx + 1) * 128, :]
                                       ).then_inc(dW0.sem, 16); dW0.n += 16
                    t_ld = dW0.tok()

                    @blk.vector
                    def _(v, t_ld=t_ld):
                        W(v, t_ld)
                        sV.bumpw(v, v.tensor_copy(out=qout[:, 0:ncols], in_=wbuf[:, 0:ncols]))
                    t_ca = sV.tok()

                    @blk.sync
                    def _(sync, slx=slx, t_ca=t_ca):
                        W(sync, t_ca)
                        sync.dma_start(out=ddst[slx * 128:(slx + 1) * 128, :], in_=qout[:, 0:ncols]
                                       ).then_inc(dSp.sem, 16); dSp.n += 16

            cast_weight(d_WinT, d_WinB, 2 * DI)
            cast_weight(d_WoutT, d_WoutB, DM)

            RG8 = [list(range(8))]

            def shard_slabs(nrows):
                out, r = [], 0
                while r < nrows:
                    p = min(128, nrows - r)
                    out.append((r, p))
                    r += p
                return out

            def shard_amax(dsrc_sh, nrows, ncols, d_amP, t_amP, t_amax, ccR):
                for slx, (r0, p) in enumerate(shard_slabs(nrows)):
                    @blk.sync
                    def _(sync, r0=r0, p=p):
                        W(sync, sV.tok(), sA.tok())
                        sync.dma_start(out=wbuf[0:p, 0:ncols], in_=dsrc_sh[r0:r0 + p, :]
                                       ).then_inc(dW0.sem, 16); dW0.n += 16
                    t_ld = dW0.tok()

                    @blk.scalar
                    def _(s, t_ld=t_ld, p=p):
                        W(s, t_ld, sV.tok())
                        sA.bumpw(s, s.activation(out=wb2[0:p, 0:ncols], in_=wbuf[0:p, 0:ncols], func=AF.Abs,
                                             bias=0.0, scale=1.0))
                    t_ab = sA.tok()

                    @blk.vector
                    def _(v, slx=slx, p=p, t_ab=t_ab):
                        W(v, t_ab)
                        if slx == 0:
                            sV.bumpw(v, v.tensor_copy(out=qsl[0:p, 0:ncols], in_=wb2[0:p, 0:ncols]))
                        else:
                            sV.bumpw(v, v.tensor_tensor(out=qsl[0:p, 0:ncols], in0=qsl[0:p, 0:ncols],
                                                    in1=wb2[0:p, 0:ncols], op=ALU.max))
                npar = min(128, nrows)
                while npar > 1:
                    half = (npar + 1) // 2

                    @blk.sync
                    def _(sync, half=half, npar=npar):
                        W(sync, sV.tok())
                        sync.dma_start(out=wb2[0:npar - half, 0:ncols], in_=qsl[half:npar, 0:ncols]
                                       ).then_inc(dBr.sem, 16); dBr.n += 16
                    t_mv = dBr.tok()

                    @blk.vector
                    def _(v, half=half, npar=npar, t_mv=t_mv):
                        W(v, t_mv)
                        sV.bumpw(v, v.tensor_tensor(out=qsl[0:npar - half, 0:ncols], in0=qsl[0:npar - half, 0:ncols],
                                                in1=wb2[0:npar - half, 0:ncols], op=ALU.max))
                    npar = half

                @blk.sync
                def _(sync):
                    W(sync, sV.tok())
                    sync.dma_start(out=d_amP[0:1, 0:ncols], in_=qsl[0:1, 0:ncols]).then_inc(dSp.sem, 16); dSp.n += 16
                t_am = dSp.tok()

                @blk.gpsimd
                def _(g, t_am=t_am):
                    W(g, t_am)
                    g.collective_compute("AllReduce", ALU.max, replica_groups=RG8,
                                         ins=[t_amP.ap().opt()], outs=[t_amax.ap().opt()]).then_inc(ccR)

            def quant_shard(dsrc_sh, d_qsh, t_qsh, t_QX, nrows, ncols, d_amax, ccR, ccG):
                @blk.sync
                def _(sync):
                    sync.wait_ge(ccR, 1)
                    W(sync, sV.tok(), sA.tok())
                    sync.dma_start(out=wb2[:, 0:ncols],
                                   in_=d_amax[0:1, 0:ncols].partition_broadcast(128)
                                   ).then_inc(dBr.sem, 16); dBr.n += 16
                t_bc = dBr.tok()

                @blk.vector
                def _(v, t_bc=t_bc):
                    W(v, t_bc)
                    sV.bumpw(v, v.tensor_scalar(out=dq_b[:, 0:ncols], in0=wb2[:, 0:ncols],
                                            scalar1=1.0 / 448.0, scalar2=None, op0=ALU.mult))
                    sV.bumpw(v, v.reciprocal(out=wb2[:, 0:ncols], in_=wb2[:, 0:ncols]))
                    sV.bumpw(v, v.tensor_scalar(out=sc_b[:, 0:ncols], in0=wb2[:, 0:ncols],
                                            scalar1=448.0, scalar2=None, op0=ALU.mult))
                t_sc = sV.tok()

                for r0, p in shard_slabs(nrows):
                    @blk.sync
                    def _(sync, r0=r0, p=p):
                        W(sync, sV.tok(), dSp.tok())
                        sync.dma_start(out=wbuf[0:p, 0:ncols], in_=dsrc_sh[r0:r0 + p, :]
                                       ).then_inc(dW1.sem, 16); dW1.n += 16
                    t_ld = dW1.tok()
                    emit_quant(blk, sV, sA,
                               src=wbuf[0:p, 0:ncols], scale_ap=sc_b[0:p, 0:ncols], dq_ap=dq_b[0:p, 0:ncols],
                               scr1=wb2[0:p, 0:ncols], scr2=qsl[0:p, 0:ncols], scr3=qs3[0:p, 0:ncols],
                               iscr=qint[0:p, 0:ncols],
                               out_bf=qout[0:p, 0:ncols], cstref=cst[0:p, :], pre_toks=(t_ld, t_sc))
                    t_q = sV.tok()

                    @blk.sync
                    def _(sync, r0=r0, p=p, t_q=t_q):
                        W(sync, t_q)
                        sync.dma_start(out=d_qsh[r0:r0 + p, :], in_=qout[0:p, 0:ncols]
                                       ).then_inc(dSp.sem, 16); dSp.n += 16
                t_spq = dSp.tok()

                @blk.gpsimd
                def _(g, t_spq=t_spq):
                    W(g, t_spq)
                    g.collective_compute("AllGather", ALU.bypass, replica_groups=RG8,
                                         ins=[t_qsh.ap().opt()], outs=[t_QX.ap().opt()]).then_inc(ccG)

            shard_amax(d_wgS, DM // 8, IFF, d_amgP, t_amgP, t_amaxg, ccRg)
            shard_amax(d_wuS, DM // 8, IFF, d_amuP, t_amuP, t_amaxu, ccRu)
            shard_amax(d_wdS, IFF // 8, DM, d_amdP, t_amdP, t_amaxd, ccRd)
            quant_shard(d_wgS, d_qgS, t_qgS, t_QG, DM // 8, IFF, d_amaxg, ccRg, ccGg)
            quant_shard(d_wuS, d_quS, t_quS, t_QU, DM // 8, IFF, d_amaxu, ccRu, ccGu)
            quant_shard(d_wdS, d_qdS, t_qdS, t_QD, IFF // 8, DM, d_amaxd, ccRd, ccGd)

            @blk.sync
            def _(sync):
                W(sync, dSp.tok(), sV.tok(), sA.tok(), sT.tok(), dBr.tok())

        esB = contextlib.ExitStack()
        with esB:
            def SB(name, shape, dt):
                return esB.enter_context(nc.sbuf_tensor(name, shape, dt))

            xs = SB("xs", [128, NDB * LC], BF16)
            bufA = SB("bufA", [128, NST * LC], BF16)
            bufB = SB("bufB", [128, NST * LC], BF16)
            bufH = SB("bufH", [128, NST * LC], BF16)
            wsl = [SB("wsl0", [128, 2048], BF16), SB("wsl1", [128, 2048], BF16)]
            xcv = SB("xcv0", [128, LC + 3], BF16)
            xc = SB("xc", [128, NIB * LC], BF16)
            sz = SB("szb", [128, NIB * LC], BF16)
            Bb = SB("Bb", [128, NST * LC], BF16)
            Cb = SB("Cb", [128, NST * LC], BF16)
            dtl = SB("dtl", [128, LC], BF16)
            dlt = SB("dlt", [128, LC], F32)
            dl2 = SB("dl2", [128, LC], F32)
            rs2 = SB("rs2", [128, LC], F32)
            du = SB("du", [128, LC], BF16)
            r16 = SB("r16", [128, 16], F32)
            t16 = SB("t16", [128, 16], F32)
            carry = SB("carry", [128, NIB * NST], F32)
            halo = SB("halo", [128, NIB * 3], BF16)
            mrb = SB("mrb", [128, LC], F32)
            rsb = SB("rsb", [128, LC], F32)
            bcsb = SB("bcsb", [32, LC], BF16)
            onesb = SB("onesb", [128, 1], BF16)
            xstg = SB("xstg", [128, 512], F32)
            WxB = SB("WxB", [128, 32 * 160], BF16)
            WdtB = SB("WdtB", [128, DI], BF16)

            pz0 = esB.enter_context(nc.psum_tensor("pz0", [128, 512], F32))
            pz1 = esB.enter_context(nc.psum_tensor("pz1", [128, 512], F32))
            pd1 = esB.enter_context(nc.psum_tensor("pd1", [128, 512], F32))
            pd2 = esB.enter_context(nc.psum_tensor("pd2", [32, 512], F32))
            pss = esB.enter_context(nc.psum_tensor("pss", [1, 512], F32))
            psq = esB.enter_context(nc.psum_tensor("psq", [1, 512], F32))
            ptr = esB.enter_context(nc.psum_tensor("ptr", [128, 128], BF16))

            blk = esB.enter_context(nc.Block())

            @blk.vector
            def _(v):
                sV.bumpw(v, v.memset(onesb[:], 1.0))
                sV.bumpw(v, v.memset(carry[:], 0.0))
                sV.bumpw(v, v.memset(halo[:], 0.0))
            t_init = sV.tok()

            @blk.sync
            def _(sync):
                sync.dma_start(out=WxB[:], in_=d_WxB).then_inc(dCn.sem, 16); dCn.n += 16
                sync.dma_start(out=WdtB[:], in_=d_WdtB).then_inc(dCn.sem, 16); dCn.n += 16
            t_wres = dCn.tok()

            for ch in range(NCH):
                for grp in range(16):
                    @blk.sync
                    def _(sync, ch=ch, grp=grp):
                        W(sync, sV.tok())
                        sync.dma_start(out=xstg[:],
                                       in_=d_xT[grp * 128:(grp + 1) * 128, ch * LC:(ch + 1) * LC]
                                       ).then_inc(dIn.sem, 16); dIn.n += 16
                    t_x = dIn.tok()

                    @blk.vector
                    def _(v, grp=grp, t_x=t_x):
                        W(v, t_x)
                        sV.bumpw(v, v.tensor_copy(out=xs[:, grp * LC:(grp + 1) * LC], in_=xstg[:]))
                t_xs = sV.tok()

                @blk.scalar
                def _(s):
                    W(s, t_xs)
                    sA.bumpw(s, s.activation(out=bufB[:], in_=xs[:], func=AF.Square, bias=0.0, scale=1.0))
                t_sq = sA.tok()

                @blk.tensor
                def _(t):
                    W(t, t_xs, t_sq, t_init, sV.tok())
                    for j in range(NDB):
                        i1 = t.matmul(pss[:], onesb[:], xs[:, j * LC:(j + 1) * LC],
                                      start=(j == 0), stop=(j == NDB - 1))
                    sT.bump(i1)
                    for j in range(NDB):
                        i2 = t.matmul(psq[:], onesb[:], bufB[:, j * LC:(j + 1) * LC],
                                      start=(j == 0), stop=(j == NDB - 1))
                    sT.bump(i2)
                t_st = sT.tok()

                @blk.vector
                def _(v):
                    W(v, t_st, dSp.tok())
                    sV.bumpw(v, v.tensor_scalar(out=rsb[0:1, 0:LC], in0=pss[:], scalar1=1.0 / DM, scalar2=None, op0=ALU.mult))
                    sV.bumpw(v, v.tensor_scalar(out=mrb[0:1, 0:LC], in0=psq[:], scalar1=1.0 / DM, scalar2=None, op0=ALU.mult))
                    sV.bumpw(v, v.tensor_tensor(out=dlt[0:1, 0:LC], in0=rsb[0:1, 0:LC], in1=rsb[0:1, 0:LC], op=ALU.mult))
                    sV.bumpw(v, v.tensor_tensor(out=mrb[0:1, 0:LC], in0=mrb[0:1, 0:LC], in1=dlt[0:1, 0:LC], op=ALU.subtract))
                t_va = sV.tok()

                @blk.scalar
                def _(s):
                    W(s, t_va)
                    sA.bumpw(s, s.activation(out=mrb[0:1, 0:LC], in_=mrb[0:1, 0:LC], func=AF.Ln,
                                         bias=cst[0:1, 0:1], scale=1.0))
                    sA.bumpw(s, s.activation(out=mrb[0:1, 0:LC], in_=mrb[0:1, 0:LC], func=AF.Exp,
                                         bias=0.0, scale=cst[0:1, 1:2]))
                t_rs = sA.tok()

                @blk.vector
                def _(v):
                    W(v, t_rs)
                    sV.bumpw(v, v.tensor_tensor(out=dlt[0:1, 0:LC], in0=rsb[0:1, 0:LC], in1=mrb[0:1, 0:LC], op=ALU.mult))
                t_mr = sV.tok()

                @blk.sync
                def _(sync, ch=ch, t_mr=t_mr):
                    W(sync, t_mr)
                    sync.dma_start(out=d_srow[ch:ch + 1, 0:LC], in_=dlt[0:1, 0:LC]).then_inc(dSp.sem, 16); dSp.n += 16
                    sync.dma_start(out=d_srow[ch:ch + 1, LC:2 * LC], in_=mrb[0:1, 0:LC]).then_inc(dSp.sem, 16); dSp.n += 16
                t_ssp = dSp.tok()

                @blk.sync
                def _(sync, ch=ch, t_ssp=t_ssp):
                    W(sync, t_ssp, sV.tok())
                    sync.dma_start(out=mrb[:], in_=d_srow[ch:ch + 1, 0:LC].partition_broadcast(128)
                                   ).then_inc(dBr.sem, 16); dBr.n += 16
                    sync.dma_start(out=rsb[:], in_=d_srow[ch:ch + 1, LC:2 * LC].partition_broadcast(128)
                                   ).then_inc(dBr.sem, 16); dBr.n += 16
                t_bcs = dBr.tok()

                for j in range(NDB):
                    @blk.vector
                    def _(v, j=j):
                        W(v, t_bcs)
                        sV.bumpw(v, v.tensor_tensor(out=dlt[:], in0=xs[:, j * LC:(j + 1) * LC], in1=rsb[:], op=ALU.mult))
                        sV.bumpw(v, v.tensor_tensor(out=dlt[:], in0=dlt[:], in1=mrb[:], op=ALU.subtract))
                        sV.bumpw(v, v.tensor_scalar(out=bufH[:, j * LC:(j + 1) * LC], in0=dlt[:],
                                                scalar1=ca1[:, j:j + 1], scalar2=sa_t[:, j:j + 1],
                                                op0=ALU.mult, op1=ALU.add))
                t_h = sV.tok()

                t_slab_cons = [sT.tok(), sT.tok()]
                t_bankfree = [sV.tok(), sV.tok()]
                t_mrbfree = sA.tok()
                for te in range(64):
                    par = te % 2
                    dWp = dW0 if par == 0 else dW1
                    pz = pz0 if te % 2 == 0 else pz1

                    @blk.sync
                    def _(sync, te=te, par=par, dWp=dWp, tc=t_slab_cons[par]):
                        W(sync, tc)
                        sync.dma_start(out=wsl[par][:],
                                       in_=d_WinB[:, te * 128:(te + 1) * 128].rearrange("(j p) c -> p j c", p=128)
                                       ).then_inc(dWp.sem, 16); dWp.n += 16
                    t_w = dWp.tok()

                    if True:
                        @blk.tensor
                        def _(t, par=par, t_w=t_w, tf=t_bankfree[te % 2], pz=pz):
                            W(t, t_w, tf, t_h)
                            for j in range(NDB):
                                ins = t.matmul(pz[:], wsl[par][:, j * 128:(j + 1) * 128],
                                               bufH[:, j * LC:(j + 1) * LC],
                                               start=(j == 0), stop=(j == NDB - 1))
                            sT.bump(ins)
                        t_mm = sT.tok()
                        t_slab_cons[par] = t_mm

                        if te < NIB:
                            @blk.vector
                            def _(v, te=te, t_mm=t_mm, pz=pz, tmf=t_mrbfree):
                                W(v, t_mm, tmf)
                                sV.bumpw(v, v.tensor_copy(out=xcv[:, 3:3 + LC], in_=pz[:]))
                                sV.bumpw(v, v.tensor_copy(out=xcv[:, 0:3], in_=halo[:, te * 3:te * 3 + 3]))
                                sV.bumpw(v, v.tensor_copy(out=halo[:, te * 3:te * 3 + 3], in_=xcv[:, LC:LC + 3]))
                                sV.bumpw(v, v.tensor_scalar(out=mrb[:], in0=xcv[:, 0:LC],
                                                        scalar1=convw[:, te * 4:te * 4 + 1], scalar2=None, op0=ALU.mult))
                                for k in range(1, 4):
                                    sV.bumpw(v, v.scalar_tensor_tensor(out=mrb[:], in0=xcv[:, k:k + LC],
                                                                   scalar=convw[:, te * 4 + k:te * 4 + k + 1],
                                                                   in1=mrb[:], op0=ALU.mult, op1=ALU.add))
                            t_cv = sV.tok()
                            t_bankfree[te % 2] = t_cv

                            @blk.scalar
                            def _(s, te=te, t_cv=t_cv):
                                W(s, t_cv)
                                sA.bumpw(s, s.activation(out=xc[:, te * LC:(te + 1) * LC], in_=mrb[:], func=AF.Silu,
                                                     bias=convb[:, te:te + 1], scale=1.0))
                            t_mrbfree = sA.tok()
                        else:
                            tz = te - NIB

                            @blk.scalar
                            def _(s, tz=tz, t_mm=t_mm, pz=pz):
                                W(s, t_mm)
                                sA.bumpw(s, s.activation(out=sz[:, tz * LC:(tz + 1) * LC], in_=pz[:], func=AF.Silu,
                                                     bias=0.0, scale=1.0))
                            t_bankfree[te % 2] = sA.tok()
                t_xcz = sA.tok()

                @blk.tensor
                def _(t):
                    W(t, t_xcz, t_wres, sV.tok())
                    for j in range(NIB):
                        i1 = t.matmul(pd1[:], WxB[:, j * 160:j * 160 + 128], xc[:, j * LC:(j + 1) * LC],
                                      start=(j == 0), stop=(j == NIB - 1))
                    sT.bump(i1)
                    for j in range(NIB):
                        i2 = t.matmul(pd2[:], WxB[:, j * 160 + 128:j * 160 + 160], xc[:, j * LC:(j + 1) * LC],
                                      start=(j == 0), stop=(j == NIB - 1))
                    sT.bump(i2)
                t_xd = sT.tok()

                @blk.vector
                def _(v):
                    W(v, t_xd, dSp.tok())
                    sV.bumpw(v, v.tensor_copy(out=dtl[:], in_=pd1[:]))
                    sV.bumpw(v, v.tensor_copy(out=bcsb[:], in_=pd2[:]))
                t_bcev = sV.tok()

                @blk.sync
                def _(sync, ch=ch, t_bcev=t_bcev):
                    W(sync, t_bcev)
                    sync.dma_start(out=d_bc[ch:ch + 1, :].rearrange("a (p c) -> (a p) c", p=32),
                                   in_=bcsb[:]).then_inc(dSp.sem, 16); dSp.n += 16
                t_bcsp = dSp.tok()

                @blk.sync
                def _(sync, ch=ch, t_bcsp=t_bcsp):
                    W(sync, t_bcsp, sV.tok())
                    sync.dma_start(out=Bb[:], in_=d_bc[ch:ch + 1, 0:NST * LC].partition_broadcast(128)
                                   ).then_inc(dBr.sem, 16); dBr.n += 16
                    sync.dma_start(out=Cb[:], in_=d_bc[ch:ch + 1, NST * LC:2 * NST * LC].partition_broadcast(128)
                                   ).then_inc(dBr.sem, 16); dBr.n += 16
                t_bcb = dBr.tok()

                dltP = [dlt, dl2]
                rsbP = [rsb, rs2]
                pzP = [pz0, pz1]
                t_b9v = sV.tok(); t_b9a = sA.tok(); t_b9sp = dSp.tok()
                tsc = [t_b9v, t_b9v]
                tpd = [t_b9a, t_b9a]
                tdafree = [t_b9a, t_b9a]
                tmm = [None, None]
                tsp = [None, None]
                tdl = [None, None]

                def emit_mm(j):
                    par = j % 2

                    @blk.tensor
                    def _(t, j=j, par=par, tw=tpd[par]):
                        W(t, tw, t_bcev, t_xcz)
                        sT.bump(t.matmul(pzP[par][:], WdtB[:, j * 128:(j + 1) * 128], dtl[:], start=True, stop=True))
                    tmm[par] = sT.tok()

                def emit_sp(j, tduj=None):
                    par = j % 2

                    @blk.scalar
                    def _(s, j=j, par=par, tw=tmm[par], tfree=tdl[par], tdaf=tdafree[par], tduj=tduj):
                        W(s, tw, tfree, tdaf, tduj, t_b9v)
                        sA.bumpw(s, s.activation(out=rsbP[par][:], in_=pzP[par][:], func=AF.Abs,
                                             bias=bdt[:, j:j + 1], scale=1.0))
                        sA.bumpw(s, s.activation(out=rsbP[par][:], in_=rsbP[par][:], func=AF.Exp,
                                             bias=0.0, scale=cst[:, 2:3]))
                        sA.bumpw(s, s.activation(out=rsbP[par][:], in_=rsbP[par][:], func=AF.Ln,
                                             bias=cst[:, 3:4], scale=1.0))
                        sA.bumpw(s, s.activation(out=dltP[par][:], in_=pzP[par][:], func=AF.Relu,
                                             bias=bdt[:, j:j + 1], scale=1.0))
                    tsp[par] = sA.tok()
                    tpd[par] = tsp[par]

                def emit_dlt(j):
                    par = j % 2

                    @blk.vector
                    def _(v, par=par, tw=tsp[par]):
                        W(v, tw)
                        sV.bumpw(v, v.tensor_tensor(out=dltP[par][:], in0=dltP[par][:], in1=rsbP[par][:], op=ALU.add))
                    tdl[par] = sV.tok()

                emit_mm(0); emit_sp(0); emit_dlt(0)
                emit_mm(1); emit_sp(1); emit_dlt(1)

                for j in range(NIB):
                    par = j % 2

                    @blk.vector
                    def _(v, j=j, par=par, tw=tdl[par]):
                        W(v, tw, t_xcz)
                        sV.bumpw(v, v.tensor_tensor(out=du[:], in0=dltP[par][:],
                                                in1=xc[:, j * LC:(j + 1) * LC], op=ALU.mult))
                    tdu_j = sV.tok()

                    for h in (0, 1):
                        o = h * 8 * LC

                        @blk.scalar
                        def _(s, j=j, par=par, o=o, tw=tsc[h], tdlj=tdl[par]):
                            W(s, tdlj, tw, t_b9sp)
                            for n in range(8):
                                k = j * 16 + (o // LC) + n
                                ins = s.activation(out=bufA[:, o + n * LC:o + (n + 1) * LC], in_=dltP[par][:],
                                                   func=AF.Exp, bias=0.0, scale=Aneg[:, k:k + 1])
                            sA.bumpw(s, ins)
                        tda_h = sA.tok()
                        if h == 1:
                            tdafree[par] = tda_h

                        @blk.vector
                        def _(v, j=j, h=h, o=o, tda_h=tda_h):
                            W(v, t_bcb)
                            for n in range(8):
                                ins = v.tensor_tensor(out=bufB[:, o + n * LC:o + (n + 1) * LC], in0=du[:],
                                                      in1=Bb[:, o + n * LC:o + (n + 1) * LC], op=ALU.mult)
                            sV.bumpw(v, ins)
                            W(v, tda_h)
                            dA3 = bufA[:, o:o + 8 * LC].rearrange("p (n l) -> p n l", n=8)
                            dB3 = bufB[:, o:o + 8 * LC].rearrange("p (n l) -> p n l", n=8)
                            hs = h * 8
                            sV.bumpw(v, v.tensor_copy(out=r16[:, hs:hs + 8],
                                                  in_=dA3[:, :, 0:1].rearrange("p n o -> p (n o)")))
                            sV.bumpw(v, v.memset(dA3[:, :, 0:1], 0.0))
                            sV.bumpw(v, v.tensor_tensor(out=t16[:, hs:hs + 8], in0=r16[:, hs:hs + 8],
                                                    in1=carry[:, j * NST + hs:j * NST + hs + 8], op=ALU.mult))
                            sV.bumpw(v, v.tensor_tensor(out=dB3[:, :, 0:1], in0=dB3[:, :, 0:1],
                                                    in1=t16[:, hs:hs + 8].rearrange("p (n o) -> p n o", o=1),
                                                    op=ALU.add))
                            sV.bumpw(v, v.tensor_tensor_scan(out=bufH[:, o:o + 8 * LC],
                                                         data0=bufA[:, o:o + 8 * LC],
                                                         data1=bufB[:, o:o + 8 * LC],
                                                         initial=0.0, op0=ALU.mult, op1=ALU.add))
                            st3 = bufH[:, o:o + 8 * LC].rearrange("p (n l) -> p n l", n=8)
                            sV.bumpw(v, v.tensor_copy(out=carry[:, j * NST + hs:j * NST + hs + 8],
                                                  in_=st3[:, :, LC - 1:LC].rearrange("p n o -> p (n o)")))
                        tsc[h] = sV.tok()

                    if j + 2 < NIB:
                        emit_mm(j + 2)
                        emit_sp(j + 2, tdu_j)
                        emit_dlt(j + 2)

                    @blk.vector
                    def _(v, j=j):
                        W(v, t_bcb, t_xcz)
                        sV.bumpw(v, v.tensor_tensor(out=bufH[:], in0=bufH[:], in1=Cb[:], op=ALU.mult))
                        sV.bumpw(v, v.tensor_tensor(out=bufH[:, 0:8 * LC], in0=bufH[:, 0:8 * LC],
                                                in1=bufH[:, 8 * LC:16 * LC], op=ALU.add))
                        sV.bumpw(v, v.tensor_tensor(out=bufH[:, 0:4 * LC], in0=bufH[:, 0:4 * LC],
                                                in1=bufH[:, 4 * LC:8 * LC], op=ALU.add))
                        sV.bumpw(v, v.tensor_tensor(out=bufH[:, 0:2 * LC], in0=bufH[:, 0:2 * LC],
                                                in1=bufH[:, 2 * LC:4 * LC], op=ALU.add))
                        sV.bumpw(v, v.tensor_tensor(out=bufH[:, 0:LC], in0=bufH[:, 0:LC],
                                                in1=bufH[:, LC:2 * LC], op=ALU.add))
                        sV.bumpw(v, v.scalar_tensor_tensor(out=xstg[:], in0=xc[:, j * LC:(j + 1) * LC],
                                                       scalar=Dt[:, j:j + 1], in1=bufH[:, 0:LC],
                                                       op0=ALU.mult, op1=ALU.add))
                        sV.bumpw(v, v.tensor_tensor(out=xc[:, j * LC:(j + 1) * LC], in0=xstg[:],
                                                in1=sz[:, j * LC:(j + 1) * LC], op=ALU.mult))
                t_scan = sV.tok()
                t_aend = sA.tok()

                t_wcons = [t_scan, t_scan]
                t_ofree = [t_scan, t_scan]
                for dm in range(NDB):
                    pz = pz0 if dm % 2 == 0 else pz1
                    t_ws = []
                    for hh in range(2):
                        par = (2 * dm + hh) % 2
                        dWp = dW0 if par == 0 else dW1

                        @blk.sync
                        def _(sync, dm=dm, hh=hh, par=par, dWp=dWp, tc=t_wcons[par]):
                            W(sync, tc)
                            sync.dma_start(out=wsl[par][:],
                                           in_=d_WoutB[hh * 2048:(hh + 1) * 2048, dm * 128:(dm + 1) * 128]
                                           .rearrange("(j p) c -> p j c", p=128)
                                           ).then_inc(dWp.sem, 16); dWp.n += 16
                        t_ws.append(dWp.tok())

                    @blk.tensor
                    def _(t, dm=dm, t_ws=t_ws, tf=t_ofree[dm % 2], pz=pz):
                        W(t, t_ws[0], t_ws[1], tf, t_scan, t_aend)
                        for hh in range(2):
                            par = (2 * dm + hh) % 2
                            for j in range(16):
                                jj = hh * 16 + j
                                ins = t.matmul(pz[:], wsl[par][:, j * 128:(j + 1) * 128],
                                               xc[:, jj * LC:(jj + 1) * LC],
                                               start=(hh == 0 and j == 0), stop=(hh == 1 and j == 15))
                        sT.bump(ins)
                    t_mm = sT.tok()
                    t_wcons[0] = t_mm
                    t_wcons[1] = t_mm

                    @blk.vector
                    def _(v, dm=dm, t_mm=t_mm, pz=pz):
                        W(v, t_mm)
                        sV.bumpw(v, v.tensor_scalar(out=mrb[:], in0=pz[:],
                                                scalar1=gaf[:, dm:dm + 1], scalar2=gab[:, dm:dm + 1],
                                                op0=ALU.mult, op1=ALU.add))
                        sV.bumpw(v, v.tensor_tensor(out=sz[:, dm * LC:(dm + 1) * LC], in0=mrb[:],
                                                in1=xs[:, dm * LC:(dm + 1) * LC], op=ALU.add))
                    t_ofree[dm % 2] = sV.tok()
                t_x2 = sV.tok()

                for dm in range(NDB):
                    for rt in range(RT):
                        @blk.tensor
                        def _(t, dm=dm, rt=rt):
                            W(t, t_x2, sV.tok())
                            sT.bump(t.transpose(ptr[:], sz[:, dm * LC + rt * 128:dm * LC + rt * 128 + 128], idn[:]))
                        t_tp = sT.tok()

                        @blk.vector
                        def _(v, dm=dm, rt=rt, t_tp=t_tp):
                            W(v, t_tp)
                            sV.bumpw(v, v.tensor_copy(out=bufA[:, rt * DM + dm * 128:rt * DM + dm * 128 + 128], in_=ptr[:]))
                t_x2t = sV.tok()

                @blk.sync
                def _(sync, ch=ch, t_x2t=t_x2t):
                    W(sync, t_x2t)
                    sync.dma_start(out=d_x2[ch * LC:(ch + 1) * LC, :].rearrange("(rt p) c -> p rt c", p=128),
                                   in_=bufA[:]).then_inc(dSp.sem, 16); dSp.n += 16

            @blk.sync
            def _(sync):
                W(sync, dSp.tok(), sV.tok(), sA.tok(), sT.tok(), dBr.tok())

        esC = contextlib.ExitStack()
        with esC:
            def SB(name, shape, dt):
                return esC.enter_context(nc.sbuf_tensor(name, shape, dt))

            x2tc = [SB("x2tc0", [128, RT * DM], BF16), SB("x2tc1", [128, RT * DM], BF16)]
            qx = SB("qx", [128, RT * DM], BF16)
            qxT = [SB("qxT0", [128, NDB * LC], BF16), SB("qxT1", [128, NDB * LC], BF16)]
            qgs = [SB("qgs0", [128, NDB * 128], BF16), SB("qgs1", [128, NDB * 128], BF16)]
            qus = [SB("qus0", [128, NDB * 128], BF16), SB("qus1", [128, NDB * 128], BF16)]
            qts = SB("qts", [128, 11 * 512], BF16)
            wds = SB("wds", [128, 11 * 512], BF16)
            h2 = SB("h2", [128, DM], F32)
            c1s = SB("c1s", [128, DM], F32)
            c2s = SB("c2s", [128, DM], F32)
            t0w = SB("t0w", [128, 1024], BF16)
            tqw = SB("tqw", [128, 1024], BF16)
            tqf = SB("tqf", [128, 1024], F32)
            tqf2 = SB("tqf2", [128, 1024], F32)
            tqf3 = SB("tqf3", [128, 1024], F32)
            tqf4 = SB("tqf4", [128, 1024], F32)
            tb = SB("tb", [128, 512], BF16)
            rmax_t = SB("rmax_t", [128, 512], F32)
            sc_bc = [SB("sc_bc0", [128, 1024], F32), SB("sc_bc1", [128, 1024], F32)]
            dq_bc = [SB("dq_bc0", [128, 1024], F32), SB("dq_bc1", [128, 1024], F32)]
            ost = [SB("ost0", [128, 512], F32), SB("ost1", [128, 512], F32)]
            rmax = SB("rmaxC", [128, 8], F32)
            sxr = SB("sxr", [128, RT], F32)
            dqr = SB("dqr", [128, RT], F32)
            t0t = SB("t0t", [128, 512], BF16)
            cb1T = SB("cb1T", [128, DM], BF16)
            sbT = SB("sbT", [128, DM], BF16)
            gbT = SB("gbT", [128, DM], BF16)

            pg = esC.enter_context(nc.psum_tensor("pgC", [128, 512], F32))
            pu = esC.enter_context(nc.psum_tensor("puC", [128, 512], F32))
            pdn = [esC.enter_context(nc.psum_tensor("pdC%d" % i, [128, 512], F32)) for i in range(4)]
            ptr2 = esC.enter_context(nc.psum_tensor("ptrC", [128, 128], BF16))

            blk = esC.enter_context(nc.Block())

            @blk.sync
            def _(sync):
                sync.dma_start(out=h2[:], in_=d_modrow[0:1, 3 * DM:4 * DM].partition_broadcast(128)
                               ).then_inc(dBr.sem, 16); dBr.n += 16
                sync.dma_start(out=c1s[:], in_=d_modrow[0:1, 4 * DM:5 * DM].partition_broadcast(128)
                               ).then_inc(dBr.sem, 16); dBr.n += 16
                sync.dma_start(out=c2s[:], in_=d_modrow[0:1, 5 * DM:6 * DM].partition_broadcast(128)
                               ).then_inc(dBr.sem, 16); dBr.n += 16
            t_mbr = dBr.tok()

            @blk.vector
            def _(v):
                W(v, t_mbr)
                sV.bumpw(v, v.tensor_copy(out=sbT[:], in_=h2[:]))
                sV.bumpw(v, v.tensor_scalar(out=cb1T[:], in0=c1s[:], scalar1=1.0, scalar2=None, op0=ALU.add))
                sV.bumpw(v, v.tensor_scalar(out=h2[:], in0=c2s[:], scalar1=0.0, scalar2=None, op0=ALU.is_ge))
                sV.bumpw(v, v.tensor_scalar(out=h2[:], in0=h2[:], scalar1=2.0, scalar2=-1.0, op0=ALU.mult, op1=ALU.add))
                sV.bumpw(v, v.tensor_scalar(out=c1s[:], in0=c2s[:], scalar1=-1.0, scalar2=None, op0=ALU.mult))
                sV.bumpw(v, v.tensor_tensor(out=c1s[:], in0=c2s[:], in1=c1s[:], op=ALU.max))
                sV.bumpw(v, v.tensor_scalar(out=c1s[:], in0=c1s[:], scalar1=0.25, scalar2=None, op0=ALU.max))
                sV.bumpw(v, v.tensor_tensor(out=gbT[:], in0=h2[:], in1=c1s[:], op=ALU.mult))
            t_modT = sV.tok()


            t_outprev = dOu.tok()
            ost_tok = [dOu.tok(), dOu.tok()]
            t_evprev = sV.tok()
            t_qgu = [sV.tok(), sV.tok()]
            t_x2free = [sV.tok(), sV.tok()]
            t_qxTf = [sT.tok(), sT.tok()]
            t_dnall = [dW0.tok(), dW0.tok()]
            t_tsc_p = [None, None]
            t_t0all_p = [None, None]
            t_pgu_init = [sA.tok(), sV.tok()]
            t_qx_p = [None, None]
            t_x2l_p = [None, None]

            def emit_ln2qx(ch):
                par = ch % 2

                @blk.sync
                def _(sync, ch=ch, par=par, tx=t_x2free[par]):
                    W(sync, tx, sV.tok())
                    if ch == 0:
                        sync.wait_ge(ccGg, 1)
                        sync.wait_ge(ccGu, 1)
                        sync.wait_ge(ccGd, 1)
                    sync.dma_start(out=x2tc[par][:],
                                   in_=d_x2[ch * LC:(ch + 1) * LC, :].rearrange("(rt p) c -> p rt c", p=128)
                                   ).then_inc(dIn.sem, 16); dIn.n += 16
                t_x2l = dIn.tok()
                t_x2l_p[par] = t_x2l

                for rt in range(RT):
                    @blk.vector
                    def _(v, rt=rt, par=par, t_x2l=t_x2l):
                        W(v, t_x2l, t_modT, sA.tok())
                        xrt = x2tc[par][:, rt * DM:(rt + 1) * DM]
                        sV.bumpw(v, v.tensor_reduce(out=rmax[:, 0:1], in_=xrt, axis=AX.X, op=ALU.add))
                        sV.bumpw(v, v.tensor_scalar(out=rmax[:, 0:1], in0=rmax[:, 0:1], scalar1=1.0 / DM, scalar2=None, op0=ALU.mult))
                    t_mean = sV.tok()

                    @blk.scalar
                    def _(s, rt=rt, par=par, t_mean=t_mean):
                        W(s, t_mean)
                        sA.bumpw(s, s.activation(out=c1s[:], in_=x2tc[par][:, rt * DM:(rt + 1) * DM], func=AF.Square,
                                             bias=0.0, scale=1.0))
                    t_sq2 = sA.tok()

                    @blk.vector
                    def _(v, rt=rt, t_sq2=t_sq2):
                        W(v, t_sq2)
                        sV.bumpw(v, v.tensor_reduce(out=rmax[:, 1:2], in_=c1s[:], axis=AX.X, op=ALU.add))
                        sV.bumpw(v, v.tensor_scalar(out=rmax[:, 1:2], in0=rmax[:, 1:2], scalar1=1.0 / DM, scalar2=None, op0=ALU.mult))
                        sV.bumpw(v, v.tensor_tensor(out=rmax[:, 2:3], in0=rmax[:, 0:1], in1=rmax[:, 0:1], op=ALU.mult))
                        sV.bumpw(v, v.tensor_tensor(out=rmax[:, 1:2], in0=rmax[:, 1:2], in1=rmax[:, 2:3], op=ALU.subtract))
                    t_va2 = sV.tok()

                    @blk.scalar
                    def _(s, rt=rt, t_va2=t_va2):
                        W(s, t_va2)
                        sA.bumpw(s, s.activation(out=rmax[:, 1:2], in_=rmax[:, 1:2], func=AF.Ln, bias=cst[:, 0:1], scale=1.0))
                        sA.bumpw(s, s.activation(out=rmax[:, 1:2], in_=rmax[:, 1:2], func=AF.Exp, bias=0.0, scale=cst[:, 1:2]))
                    t_rs2 = sA.tok()

                    @blk.vector
                    def _(v, rt=rt, par=par, t_rs2=t_rs2):
                        W(v, t_rs2)
                        xrt = x2tc[par][:, rt * DM:(rt + 1) * DM]
                        sV.bumpw(v, v.tensor_scalar(out=h2[:], in0=xrt, scalar1=rmax[:, 0:1], scalar2=rmax[:, 1:2],
                                                op0=ALU.subtract, op1=ALU.mult))
                        sV.bumpw(v, v.tensor_tensor(out=h2[:], in0=h2[:], in1=cb1T[:], op=ALU.mult))
                        sV.bumpw(v, v.tensor_tensor(out=h2[:], in0=h2[:], in1=sbT[:], op=ALU.add))
                        sV.bumpw(v, v.tensor_reduce(out=sxr[:, rt:rt + 1], in_=h2[:], axis=AX.X, op=ALU.max,
                                                apply_absolute_value=True))
                        sV.bumpw(v, v.reciprocal(out=dqr[:, rt:rt + 1], in_=sxr[:, rt:rt + 1]))
                        sV.bumpw(v, v.tensor_scalar(out=dqr[:, rt:rt + 1], in0=dqr[:, rt:rt + 1],
                                                scalar1=448.0, scalar2=None, op0=ALU.mult))
                        sV.bumpw(v, v.tensor_scalar(out=sxr[:, rt:rt + 1], in0=sxr[:, rt:rt + 1],
                                                scalar1=1.0 / 448.0, scalar2=None, op0=ALU.mult))
                    t_h2 = sV.tok()

                    for hw in range(2):
                        emit_quant_pure(blk, sV, sA,
                                   src=h2[:, hw * 1024:(hw + 1) * 1024],
                                   scale_ap=dqr[:, rt:rt + 1], dq_ap=sxr[:, rt:rt + 1],
                                   scr1=tqf[:], scr2=tqf2[:], scr3=tqf3[:], scr4=tqf4[:],
                                   out_bf=qx[:, rt * DM + hw * 1024:rt * DM + (hw + 1) * 1024],
                                   cstref=cst, pre_toks=(t_h2,))
                t_qx_p[par] = sV.tok()

                @blk.vector
                def _(v):
                    W(v, dBr.tok())
                    sV.bumpw(v, v.memset(rmax_t[:], 0.0))

            def emit_tquant(ch):
                par = ch % 2
                t_qt_g = [None] * 4
                gcut = {5: 0, 10: 1, 16: 2, 21: 3}
                for g in range(22):
                    @blk.gpsimd
                    def _(gq, g=g, par=par):
                        W(gq, t_t0all_p[par], sV.tok(), sA.tok(), dQs.tok())
                        gq.dma_start(out=t0w[:].rearrange("p (k c) -> p k c", k=2),
                                     in_=d_t0p[par][g * 256:(g + 1) * 256, :].rearrange("(k p) c -> p k c", p=128)
                                     ).then_inc(dTq.sem, 16); dTq.n += 16
                    t_tl = dTq.tok()
                    emit_quant_pure(blk, sV, sA,
                               src=t0w[:], scale_ap=sc_bc[par][:], dq_ap=dq_bc[par][:],
                               scr1=tqf[:], scr2=tqf2[:], scr3=tqf3[:], scr4=tqf4[:],
                               out_bf=tqw[:], cstref=cst, pre_toks=(t_tl, t_tsc_p[par]), tt_scales=True)
                    t_q2 = sV.tok()

                    @blk.gpsimd
                    def _(gq, g=g, par=par, t_q2=t_q2, td=t_dnall[par]):
                        W(gq, t_q2, td)
                        gq.dma_start(out=d_qtp[par][g * 256:(g + 1) * 256, :].rearrange("(k p) c -> p k c", p=128),
                                     in_=tqw[:].rearrange("p (k c) -> p k c", k=2)
                                     ).then_inc(dQs.sem, 16); dQs.n += 16
                    if g in gcut:
                        t_qt_g[gcut[g]] = dQs.tok()
                return t_qt_g

            def emit_qxT(ch):
                par = ch % 2
                for rt in range(RT):
                    for j in range(NDB):
                        @blk.tensor
                        def _(t, rt=rt, j=j, par=par, tq=t_qx_p[par]):
                            W(t, tq, sV.tok())
                            sT.bump(t.transpose(ptr2[:], qx[:, rt * DM + j * 128:rt * DM + j * 128 + 128], idn[:]))
                        t_tp = sT.tok()

                        @blk.vector
                        def _(v, rt=rt, j=j, par=par, t_tp=t_tp, tf=t_qxTf[par]):
                            W(v, t_tp, tf)
                            sV.bumpw(v, v.tensor_copy(out=qxT[par][:, j * LC + rt * 128:j * LC + rt * 128 + 128], in_=ptr2[:]))
                return sV.tok()

            def emit_gateup(ch, t_qxT):
                par = ch % 2
                t_pgfree, t_pufree = t_pgu_init
                for i in range(44):
                    ip = i % 2
                    dWp = dW0 if ip == 0 else dW1

                    @blk.sync
                    def _(sync, i=i, ip=ip, dWp=dWp, tc=t_qgu[ip]):
                        W(sync, tc)
                        sync.dma_start(out=qgs[ip][:],
                                       in_=d_QG[:, i * 128:(i + 1) * 128].rearrange("(j p) c -> p j c", p=128)
                                       ).then_inc(dWp.sem, 16); dWp.n += 16
                        sync.dma_start(out=qus[ip][:],
                                       in_=d_QU[:, i * 128:(i + 1) * 128].rearrange("(j p) c -> p j c", p=128)
                                       ).then_inc(dWp.sem, 16); dWp.n += 16
                    t_wg = dWp.tok()

                    @blk.tensor
                    def _(t, i=i, ip=ip, par=par, t_wg=t_wg, t_qxT=t_qxT, tpg=t_pgfree, tpu=t_pufree):
                        W(t, t_wg, t_qxT, tpg, tpu)
                        for j in range(NDB):
                            ig = t.matmul(pg[:], qgs[ip][:, j * 128:(j + 1) * 128],
                                          qxT[par][:, j * LC:(j + 1) * LC], start=(j == 0), stop=(j == NDB - 1))
                        sT.bump(ig)
                        for j in range(NDB):
                            iu = t.matmul(pu[:], qus[ip][:, j * 128:(j + 1) * 128],
                                          qxT[par][:, j * LC:(j + 1) * LC], start=(j == 0), stop=(j == NDB - 1))
                        sT.bump(iu)
                    t_guv = sT.tok()
                    t_qgu[ip] = t_guv

                    @blk.scalar
                    def _(s, i=i, t_guv=t_guv, tsp=dT0.tok()):
                        W(s, t_guv, sV.tok(), tsp)
                        sA.bumpw(s, s.activation(out=t0t[:], in_=pg[:], func=AF.Silu, bias=0.0, scale=1.0))
                    t_sg2 = sA.tok()
                    t_pgfree = t_sg2

                    @blk.vector
                    def _(v, i=i, t_sg2=t_sg2):
                        W(v, t_sg2)
                        sV.bumpw(v, v.tensor_tensor(out=t0t[:], in0=t0t[:], in1=pu[:], op=ALU.mult))
                    t_mul = sV.tok()
                    t_pufree = t_mul

                    @blk.scalar
                    def _(s, i=i, t_mul=t_mul):
                        W(s, t_mul)
                        sA.bumpw(s, s.activation(out=tb[:], in_=t0t[:], func=AF.Abs, bias=0.0, scale=1.0))
                    t_ab = sA.tok()

                    @blk.vector
                    def _(v, i=i, t_ab=t_ab):
                        W(v, t_ab)
                        sV.bumpw(v, v.tensor_tensor(out=rmax_t[:], in0=rmax_t[:], in1=tb[:], op=ALU.max))

                    @blk.sync
                    def _(sync, i=i, par=par, t_mul=t_mul):
                        W(sync, t_mul)
                        sync.dma_start(out=d_t0p[par][i * 128:(i + 1) * 128, :], in_=t0t[:]).then_inc(dT0.sem, 16); dT0.n += 16
                t_t0all_p[par] = dT0.tok()
                t_qxTf[par] = sT.tok()
                t_pgu_init[0] = t_pgfree
                t_pgu_init[1] = t_pufree

            def emit_tree_scales(ch):
                par = ch % 2
                npar = 128
                t_round = sV.tok()
                while npar > 1:
                    half = npar // 2

                    @blk.sync
                    def _(sync, half=half, npar=npar, t_round=t_round):
                        W(sync, t_round)
                        sync.dma_start(out=tqf[0:half, 0:512], in_=rmax_t[half:npar, 0:512]
                                       ).then_inc(dBr.sem, 16); dBr.n += 16
                    t_mv = dBr.tok()

                    @blk.vector
                    def _(v, half=half, t_mv=t_mv):
                        W(v, t_mv)
                        sV.bumpw(v, v.tensor_tensor(out=rmax_t[0:half, 0:512], in0=rmax_t[0:half, 0:512],
                                                in1=tqf[0:half, 0:512], op=ALU.max))
                    t_round = sV.tok()
                    npar = half

                @blk.sync
                def _(sync, ch=ch, t_round=t_round):
                    W(sync, t_round)
                    sync.dma_start(out=d_tamax[ch:ch + 1, :], in_=rmax_t[0:1, 0:512]).then_inc(dSp.sem, 16); dSp.n += 16
                t_amsp = dSp.tok()

                @blk.sync
                def _(sync, ch=ch, par=par, t_amsp=t_amsp):
                    W(sync, t_amsp, sV.tok())
                    for kk in range(2):
                        sync.dma_start(out=sc_bc[par][:, kk * 512:(kk + 1) * 512],
                                       in_=d_tamax[ch:ch + 1, :].partition_broadcast(128)
                                       ).then_inc(dBr.sem, 16); dBr.n += 16
                t_ambc = dBr.tok()

                @blk.vector
                def _(v, par=par, t_ambc=t_ambc):
                    W(v, t_ambc)
                    sV.bumpw(v, v.tensor_scalar(out=dq_bc[par][:], in0=sc_bc[par][:], scalar1=1.0 / 448.0, scalar2=None, op0=ALU.mult))
                    sV.bumpw(v, v.reciprocal(out=sc_bc[par][:], in_=sc_bc[par][:]))
                    sV.bumpw(v, v.tensor_scalar(out=sc_bc[par][:], in0=sc_bc[par][:], scalar1=448.0, scalar2=None, op0=ALU.mult))
                t_tsc_p[par] = sV.tok()

            def emit_down(ch, t_qt_g):
                nonlocal t_evprev, t_outprev
                par = ch % 2
                t_dnld = sT.tok()
                for dmc in range(4):
                    t_dn_ev = t_evprev if dmc == 0 else sV.tok()
                    for ig in range(4):
                        @blk.sync
                        def _(sync, dmc=dmc, ig=ig, par=par, t_dnld=t_dnld):
                            W(sync, t_qt_g[ig], t_dnld)
                            sync.dma_start(out=qts[:, 0:11 * 512],
                                           in_=d_qtp[par][ig * 1408:(ig + 1) * 1408, :].rearrange("(b p) c -> p b c", p=128)
                                           ).then_inc(dW0.sem, 16); dW0.n += 16
                            sync.dma_start(out=wds[:, 0:11 * 512],
                                           in_=d_QD[ig * 1408:(ig + 1) * 1408, dmc * 512:(dmc + 1) * 512]
                                           .rearrange("(b p) c -> p b c", p=128)
                                           ).then_inc(dW0.sem, 16); dW0.n += 16
                        t_wd = dW0.tok()

                        @blk.tensor
                        def _(t, dmc=dmc, ig=ig, t_wd=t_wd, t_dn_ev=t_dn_ev):
                            W(t, t_wd, t_dn_ev)
                            for ib in range(11):
                                for rt in range(RT):
                                    ins = t.matmul(pdn[rt][:],
                                                   qts[:, ib * 512 + rt * 128:ib * 512 + rt * 128 + 128],
                                                   wds[:, ib * 512:(ib + 1) * 512],
                                                   start=(ig == 0 and ib == 0), stop=(ig == 3 and ib == 10))
                            sT.bump(ins)
                        t_dnld = sT.tok()
                    t_dn = sT.tok()

                    for rt in range(RT):
                        op = (dmc * RT + rt) % 2

                        @blk.vector
                        def _(v, dmc=dmc, rt=rt, par=par, op=op, t_dn=t_dn, tprev=ost_tok[op]):
                            W(v, t_dn, t_x2l_p[par], tprev, t_modT)
                            sV.bumpw(v, v.tensor_tensor(out=tqf[:, 0:512], in0=pdn[rt][:],
                                                    in1=gbT[:, dmc * 512:(dmc + 1) * 512], op=ALU.mult))
                            sV.bumpw(v, v.tensor_tensor(out=ost[op][:],
                                                    in0=tqf[:, 0:512],
                                                    in1=x2tc[par][:, rt * DM + dmc * 512:rt * DM + dmc * 512 + 512],
                                                    op=ALU.add))
                        t_pev = sV.tok()

                        @blk.sync
                        def _(sync, ch=ch, dmc=dmc, rt=rt, op=op, t_pev=t_pev):
                            W(sync, t_pev)
                            sync.dma_start(out=d_out[ch * LC + rt * 128:ch * LC + (rt + 1) * 128,
                                                     dmc * 512:(dmc + 1) * 512],
                                           in_=ost[op][:]).then_inc(dOu.sem, 16); dOu.n += 16
                        ost_tok[op] = dOu.tok()
                t_outprev = dOu.tok()
                t_evprev = sV.tok()
                t_x2free[par] = sV.tok()
                t_dnall[par] = dW0.tok()

            for ch in range(NCH):
                emit_ln2qx(ch)
                t_qxT = emit_qxT(ch)
                if ch > 0:
                    tqg_prev = emit_tquant(ch - 1)
                emit_gateup(ch, t_qxT)
                emit_tree_scales(ch)
                if ch > 0:
                    emit_down(ch - 1, tqg_prev)
            tqg_prev = emit_tquant(NCH - 1)
            emit_down(NCH - 1, tqg_prev)

            @blk.sync
            def _(sync):
                W(sync, dOu.tok(), sV.tok(), sA.tok(), sT.tok())

    return nc


def _vec_tiles(v, width=32):
    return np.ascontiguousarray(v.reshape(width, 128).T)


def kernel(**inputs):
    x = np.asarray(inputs["L0_attn_hidden_states"], np.float32)
    W_in = np.asarray(inputs["L0_attn_W_in"], np.float32)
    conv_w = np.asarray(inputs["L0_attn_conv_w"], np.float32)
    conv_b = np.asarray(inputs["L0_attn_conv_b"], np.float32)
    W_x = np.asarray(inputs["L0_attn_W_x"], np.float32)
    W_dt = np.asarray(inputs["L0_attn_W_dt"], np.float32)
    b_dt = np.asarray(inputs["L0_attn_b_dt"], np.float32)
    A_log = np.asarray(inputs["L0_attn_A_log"], np.float32)
    Dv = np.asarray(inputs["L0_attn_D"], np.float32)
    W_out = np.asarray(inputs["L0_attn_W_out"], np.float32)
    b_out = np.asarray(inputs["L0_attn_b_out"], np.float32)
    w_gate = np.asarray(inputs["L0_ffn_w_gate"], np.float32)
    w_up = np.asarray(inputs["L0_ffn_w_up"], np.float32)
    w_down = np.asarray(inputs["L0_ffn_w_down"], np.float32)
    up_idx = np.asarray(inputs["L0_ffn_up_reorder_index"])
    dn_idx = np.asarray(inputs["L0_ffn_down_reorder_index"])
    cvec = np.asarray(inputs["L0_c"], np.float32)
    wmod = np.asarray(inputs["L0_wmod"], np.float32)

    assert up_idx.shape == (DM,)
    pinv = np.argsort(dn_idx)
    w_down_p = w_down[:, pinv]

    shared = {
        "wmod": np.ascontiguousarray(wmod),
        "WinT": np.ascontiguousarray(W_in.T),
        "convw": np.ascontiguousarray(conv_w.reshape(32, 128, 4).transpose(1, 0, 2).reshape(128, 128)),
        "convb": _vec_tiles(conv_b),
        "bdt": _vec_tiles(b_dt),
        "Alog": np.ascontiguousarray(A_log.reshape(32, 128, 16).transpose(1, 0, 2).reshape(128, 512)),
        "Dt": _vec_tiles(Dv),
        "bout": _vec_tiles(b_out, width=16),
        "WxT": np.ascontiguousarray(W_x.T),
        "WdtT": np.ascontiguousarray(W_dt.T),
        "WoutT": np.ascontiguousarray(W_out.T),
    }
    wgT = np.ascontiguousarray(w_gate.T)
    wuT = np.ascontiguousarray(w_up.T)
    wdPT = np.ascontiguousarray(w_down_p.T)

    nc = build_program()
    in_maps = []
    for c in range(8):
        m = dict(shared)
        m["xT"] = np.ascontiguousarray(x[c].T)
        m["cvecT"] = _vec_tiles(cvec[c], width=16)
        m["wgS"] = np.ascontiguousarray(wgT[c * (DM // 8):(c + 1) * (DM // 8), :])
        m["wuS"] = np.ascontiguousarray(wuT[c * (DM // 8):(c + 1) * (DM // 8), :])
        m["wdS"] = np.ascontiguousarray(wdPT[c * (IFF // 8):(c + 1) * (IFF // 8), :])
        in_maps.append(m)

    res = run_bass_kernel_spmd(nc, in_maps, core_ids=list(range(8)))
    out = np.stack([np.asarray(r["out"], np.float32) for r in res.results], axis=0)
    return out


# version 17
# speedup vs baseline: 1.0076x; 1.0076x over previous
import contextlib

import numpy as np
import ml_dtypes

import concourse.bass as bass
import concourse.mybir as mybir
from concourse.bass_utils import run_bass_kernel_spmd

F32 = mybir.dt.float32
BF16 = mybir.dt.bfloat16
I32 = mybir.dt.int32
AF = mybir.ActivationFunctionType
ALU = mybir.AluOpType
AX = mybir.AxisListType

DM = 2048
DI = 4096
NST = 16
DTR = 128
L = 2048
LC = 512
NCH = L // LC
NDB = DM // 128
NIB = DI // 128
IFF = 5632
NIC = IFF // 512
RT = LC // 128
LN2C = float(np.log(2.0))
IVL2 = float(1.0 / np.log(2.0))
EPS = 1e-5


class Ctr:
    def __init__(self, sem, step=1):
        self.sem = sem
        self.n = 0
        self.step = step

    def bump(self, instr):
        instr.then_inc(self.sem, self.step)
        self.n += self.step
        return (self.sem, self.n)

    def bumpw(self, eng, instr):
        tk = self.bump(instr)
        eng.wait_ge(*tk)
        return tk

    def tok(self):
        return (self.sem, self.n)


def W(eng, *toks):
    for tk in toks:
        if tk is None:
            continue
        sem, v = tk
        if v > 0:
            eng.wait_ge(sem, v)


def emit_quant(blk, sV, sA, *, src, scale_ap, dq_ap, scr1, scr2, scr3, iscr,
               out_bf, cstref, pre_toks=(), scalar_scales=False):
    @blk.vector
    def _(v):
        W(v, *pre_toks)
        if scalar_scales:
            sV.bumpw(v, v.tensor_scalar(out=scr1, in0=src, scalar1=scale_ap, scalar2=None, op0=ALU.mult))
        else:
            sV.bumpw(v, v.tensor_tensor(out=scr1, in0=src, in1=scale_ap, op=ALU.mult))
    t1 = sV.tok()

    @blk.scalar
    def _(s):
        W(s, t1)
        sA.bumpw(s, s.activation(out=scr2, in_=scr1, func=AF.Abs, bias=0.0, scale=1.0))
        sA.bumpw(s, s.activation(out=scr2, in_=scr2, func=AF.Ln, bias=cstref[:, 4:5], scale=1.0))
    t2 = sA.tok()

    @blk.vector
    def _(v):
        W(v, t2)
        sV.bumpw(v, v.tensor_scalar(out=scr2, in0=scr2, scalar1=IVL2, scalar2=None,
                                    op0=ALU.mult))
        sV.bumpw(v, v.tensor_copy(out=iscr, in_=scr2))
        sV.bumpw(v, v.tensor_copy(out=scr3, in_=iscr))
        sV.bumpw(v, v.tensor_tensor(out=scr2, in0=scr2, in1=scr3, op=ALU.subtract))
        sV.bumpw(v, v.tensor_scalar(out=out_bf, in0=scr2, scalar1=0.0, scalar2=None, op0=ALU.is_lt))
        sV.bumpw(v, v.tensor_scalar(out=scr2, in0=scr2, scalar1=1.0, scalar2=None, op0=ALU.is_ge))
        sV.bumpw(v, v.tensor_tensor(out=scr3, in0=scr3, in1=out_bf, op=ALU.subtract))
        sV.bumpw(v, v.tensor_tensor(out=scr3, in0=scr3, in1=scr2, op=ALU.add))
        sV.bumpw(v, v.tensor_copy(out=iscr, in_=scr3))
        sV.bumpw(v, v.tensor_copy(out=scr3, in_=iscr))
        sV.bumpw(v, v.tensor_scalar(out=scr3, in0=scr3, scalar1=-3.0, scalar2=-9.0,
                                    op0=ALU.add, op1=ALU.max))
        sV.bumpw(v, v.tensor_scalar(out=scr3, in0=scr3, scalar1=LN2C, scalar2=None, op0=ALU.mult))
    t3 = sV.tok()

    @blk.scalar
    def _(s):
        W(s, t3)
        sA.bumpw(s, s.activation(out=out_bf, in_=scr3, func=AF.Exp, bias=0.0, scale=cstref[:, 2:3]))
        sA.bumpw(s, s.activation(out=scr3, in_=scr3, func=AF.Exp, bias=0.0, scale=1.0))
    t4 = sA.tok()

    @blk.vector
    def _(v):
        W(v, t4)
        sV.bumpw(v, v.tensor_tensor(out=scr1, in0=scr1, in1=out_bf, op=ALU.mult))
        sV.bumpw(v, v.tensor_copy(out=iscr, in_=scr1))
        sV.bumpw(v, v.tensor_copy(out=scr2, in_=iscr))
        sV.bumpw(v, v.tensor_tensor(out=scr1, in0=scr1, in1=scr2, op=ALU.subtract))
        sV.bumpw(v, v.tensor_scalar(out=out_bf, in0=scr1, scalar1=0.5, scalar2=None, op0=ALU.is_gt))
        sV.bumpw(v, v.tensor_scalar(out=scr1, in0=scr1, scalar1=-0.5, scalar2=None, op0=ALU.is_lt))
        sV.bumpw(v, v.tensor_tensor(out=scr2, in0=scr2, in1=out_bf, op=ALU.add))
        sV.bumpw(v, v.tensor_tensor(out=scr2, in0=scr2, in1=scr1, op=ALU.subtract))
        if scalar_scales:
            sV.bumpw(v, v.tensor_scalar(out=scr3, in0=scr3, scalar1=dq_ap, scalar2=None, op0=ALU.mult))
        else:
            sV.bumpw(v, v.tensor_tensor(out=scr3, in0=scr3, in1=dq_ap, op=ALU.mult))
        sV.bumpw(v, v.tensor_tensor(out=out_bf, in0=scr2, in1=scr3, op=ALU.mult))
    return sV.tok()


def emit_quant_pure(blk, sV, sA, *, src, scale_ap, dq_ap, scr1, scr2, scr3, scr4,
                    out_bf, cstref, pre_toks=(), tt_scales=False):
    @blk.vector
    def _(v):
        W(v, *pre_toks)
        if tt_scales:
            sV.bumpw(v, v.tensor_tensor(out=scr1, in0=src, in1=scale_ap, op=ALU.mult))
        else:
            sV.bumpw(v, v.tensor_scalar(out=scr1, in0=src, scalar1=scale_ap, scalar2=None, op0=ALU.mult))
    t1 = sV.tok()

    @blk.scalar
    def _(s):
        W(s, t1)
        sA.bumpw(s, s.activation(out=scr2, in_=scr1, func=AF.Abs, bias=0.0, scale=1.0))
        sA.bumpw(s, s.activation(out=scr2, in_=scr2, func=AF.Ln, bias=cstref[:, 4:5], scale=1.0))
    t2 = sA.tok()

    @blk.vector
    def _(v):
        W(v, t2)
        sV.bumpw(v, v.tensor_scalar(out=scr2, in0=scr2, scalar1=IVL2, scalar2=-6.99,
                                    op0=ALU.mult, op1=ALU.max))
        sV.bumpw(v, v.tensor_scalar(out=scr3, in0=scr2, scalar1=0.0, scalar2=-7.0,
                                    op0=ALU.mult, op1=ALU.add))
        for dl in (8.0, 4.0, 2.0, 1.0):
            sV.bumpw(v, v.scalar_tensor_tensor(out=scr4, in0=scr2, scalar=-dl, in1=scr3,
                                               op0=ALU.add, op1=ALU.is_ge))
            sV.bumpw(v, v.scalar_tensor_tensor(out=scr3, in0=scr4, scalar=dl, in1=scr3,
                                               op0=ALU.mult, op1=ALU.add))
        sV.bumpw(v, v.tensor_scalar(out=scr3, in0=scr3, scalar1=-3.0, scalar2=-9.0,
                                    op0=ALU.add, op1=ALU.max))
        sV.bumpw(v, v.tensor_scalar(out=scr3, in0=scr3, scalar1=LN2C, scalar2=None, op0=ALU.mult))
    t3 = sV.tok()

    @blk.scalar
    def _(s):
        W(s, t3)
        sA.bumpw(s, s.activation(out=out_bf, in_=scr3, func=AF.Exp, bias=0.0, scale=cstref[:, 2:3]))
        sA.bumpw(s, s.activation(out=scr3, in_=scr3, func=AF.Exp, bias=0.0, scale=1.0))
    t4 = sA.tok()

    @blk.vector
    def _(v):
        W(v, t4)
        sV.bumpw(v, v.tensor_tensor(out=scr1, in0=scr1, in1=out_bf, op=ALU.mult))
        sV.bumpw(v, v.tensor_scalar(out=scr2, in0=scr1, scalar1=0.0, scalar2=-16.0,
                                    op0=ALU.mult, op1=ALU.add))
        for dl in (16.0, 8.0, 4.0, 2.0, 1.0):
            sV.bumpw(v, v.scalar_tensor_tensor(out=scr4, in0=scr1, scalar=-(dl - 0.5), in1=scr2,
                                               op0=ALU.add, op1=ALU.is_ge))
            sV.bumpw(v, v.scalar_tensor_tensor(out=scr2, in0=scr4, scalar=dl, in1=scr2,
                                               op0=ALU.mult, op1=ALU.add))
        sV.bumpw(v, v.scalar_tensor_tensor(out=scr4, in0=scr1, scalar=-0.5, in1=scr2,
                                           op0=ALU.add, op1=ALU.is_ge))
        sV.bumpw(v, v.tensor_tensor(out=scr2, in0=scr2, in1=scr4, op=ALU.add))
        if tt_scales:
            sV.bumpw(v, v.tensor_tensor(out=scr3, in0=scr3, in1=dq_ap, op=ALU.mult))
        else:
            sV.bumpw(v, v.tensor_scalar(out=scr3, in0=scr3, scalar1=dq_ap, scalar2=None, op0=ALU.mult))
        sV.bumpw(v, v.tensor_tensor(out=out_bf, in0=scr2, in1=scr3, op=ALU.mult))
    return sV.tok()


def build_program():
    nc = bass.Bass("TRN2", target_bir_lowering=False, debug=False)

    d_xT = nc.dram_tensor("xT", [DM, L], F32, kind="ExternalInput").ap()
    d_cv = nc.dram_tensor("cvecT", [128, 16], F32, kind="ExternalInput").ap()
    d_wmod = nc.dram_tensor("wmod", [DM, 6 * DM], F32, kind="ExternalInput").ap()
    d_WinT = nc.dram_tensor("WinT", [DM, 2 * DI], F32, kind="ExternalInput").ap()
    d_convw = nc.dram_tensor("convw", [128, 128], F32, kind="ExternalInput").ap()
    d_convb = nc.dram_tensor("convb", [128, 32], F32, kind="ExternalInput").ap()
    d_bdt = nc.dram_tensor("bdt", [128, 32], F32, kind="ExternalInput").ap()
    d_Alog = nc.dram_tensor("Alog", [128, 512], F32, kind="ExternalInput").ap()
    d_Dt = nc.dram_tensor("Dt", [128, 32], F32, kind="ExternalInput").ap()
    d_bout = nc.dram_tensor("bout", [128, 16], F32, kind="ExternalInput").ap()
    d_WxT = nc.dram_tensor("WxT", [DI, 160], F32, kind="ExternalInput").ap()
    d_WdtT = nc.dram_tensor("WdtT", [128, DI], F32, kind="ExternalInput").ap()
    d_WoutT = nc.dram_tensor("WoutT", [DI, DM], F32, kind="ExternalInput").ap()
    d_wgS = nc.dram_tensor("wgS", [DM // 8, IFF], F32, kind="ExternalInput").ap()
    d_wuS = nc.dram_tensor("wuS", [DM // 8, IFF], F32, kind="ExternalInput").ap()
    d_wdS = nc.dram_tensor("wdS", [IFF // 8, DM], F32, kind="ExternalInput").ap()
    d_out = nc.dram_tensor("out", [L, DM], F32, kind="ExternalOutput").ap()

    d_WinB = nc.dram_tensor("WinB", [DM, 2 * DI], BF16, kind="Internal").ap()
    d_WoutB = nc.dram_tensor("WoutB", [DI, DM], BF16, kind="Internal").ap()
    d_WxB = nc.dram_tensor("WxB_d", [128, 32 * 160], BF16, kind="Internal").ap()
    d_WdtB = nc.dram_tensor("WdtB_d", [128, DI], BF16, kind="Internal").ap()
    t_QG = nc.dram_tensor("QG", [DM, IFF], BF16, kind="Internal", addr_space="Shared"); d_QG = t_QG.ap()
    t_QU = nc.dram_tensor("QU", [DM, IFF], BF16, kind="Internal", addr_space="Shared"); d_QU = t_QU.ap()
    t_QD = nc.dram_tensor("QD", [IFF, DM], BF16, kind="Internal", addr_space="Shared"); d_QD = t_QD.ap()
    t_qgS = nc.dram_tensor("qgS", [DM // 8, IFF], BF16, kind="Internal"); d_qgS = t_qgS.ap()
    t_quS = nc.dram_tensor("quS", [DM // 8, IFF], BF16, kind="Internal"); d_quS = t_quS.ap()
    t_qdS = nc.dram_tensor("qdS", [IFF // 8, DM], BF16, kind="Internal"); d_qdS = t_qdS.ap()
    d_modrow = nc.dram_tensor("modrow", [1, 6 * DM], F32, kind="Internal").ap()
    t_amaxg = nc.dram_tensor("amaxg", [1, IFF], F32, kind="Internal", addr_space="Shared"); d_amaxg = t_amaxg.ap()
    t_amaxu = nc.dram_tensor("amaxu", [1, IFF], F32, kind="Internal", addr_space="Shared"); d_amaxu = t_amaxu.ap()
    t_amaxd = nc.dram_tensor("amaxd", [1, DM], F32, kind="Internal", addr_space="Shared"); d_amaxd = t_amaxd.ap()
    t_amgP = nc.dram_tensor("amgP", [1, IFF], F32, kind="Internal"); d_amgP = t_amgP.ap()
    t_amuP = nc.dram_tensor("amuP", [1, IFF], F32, kind="Internal"); d_amuP = t_amuP.ap()
    t_amdP = nc.dram_tensor("amdP", [1, DM], F32, kind="Internal"); d_amdP = t_amdP.ap()
    d_srow = nc.dram_tensor("srow", [NCH, 2 * LC], F32, kind="Internal").ap()
    d_bc = nc.dram_tensor("bcrows", [NCH, 32 * LC], BF16, kind="Internal").ap()
    d_x2 = nc.dram_tensor("x2sp", [L, DM], BF16, kind="Internal").ap()
    d_t0p = [nc.dram_tensor("t0sp0", [IFF, LC], BF16, kind="Internal").ap(),
             nc.dram_tensor("t0sp1", [IFF, LC], BF16, kind="Internal").ap()]
    d_qtp = [nc.dram_tensor("qtsp0", [IFF, LC], BF16, kind="Internal").ap(),
             nc.dram_tensor("qtsp1", [IFF, LC], BF16, kind="Internal").ap()]
    d_tamax = nc.dram_tensor("tamax", [NCH, LC], F32, kind="Internal").ap()

    ident = nc.inline_tensor(np.eye(128, dtype=ml_dtypes.bfloat16), name="identbf")

    es0 = contextlib.ExitStack()
    with es0:
        def SEM(name, step=1):
            return Ctr(es0.enter_context(nc.semaphore(name)), step)

        sV = SEM("sV"); sA = SEM("sA"); sT = SEM("sT"); sG = SEM("sG")
        dIn = SEM("dIn", 16); dW0 = SEM("dW0", 16); dW1 = SEM("dW1", 16)
        dCn = SEM("dCn", 16); dSp = SEM("dSp", 16); dBr = SEM("dBr", 16)
        dT0 = SEM("dT0", 16); dQt = SEM("dQt", 16); dOu = SEM("dOu", 16)
        dTq = SEM("dTq", 16); dQs = SEM("dQs", 16)
        ccRg = es0.enter_context(nc.semaphore("ccRg"))
        ccRu = es0.enter_context(nc.semaphore("ccRu"))
        ccRd = es0.enter_context(nc.semaphore("ccRd"))
        ccGg = es0.enter_context(nc.semaphore("ccGg"))
        ccGu = es0.enter_context(nc.semaphore("ccGu"))
        ccGd = es0.enter_context(nc.semaphore("ccGd"))

        def P0(name, shape, dt):
            return es0.enter_context(nc.sbuf_tensor(name, shape, dt))

        idn = P0("idn", [128, 128], BF16)
        convw = P0("convw_s", [128, 128], F32)
        convb = P0("convb_s", [128, 32], F32)
        bdt = P0("bdt_s", [128, 32], F32)
        Aneg = P0("Aneg", [128, 512], F32)
        Dt = P0("Dt_s", [128, 32], F32)
        bout = P0("bout_s", [128, 16], F32)
        ca1 = P0("ca1", [128, 16], F32)
        sa_t = P0("sa_t", [128, 16], F32)
        gaf = P0("gaf", [128, 16], F32)
        gab = P0("gab", [128, 16], F32)
        cst = P0("cst", [128, 8], F32)

        esA = contextlib.ExitStack()
        with esA:
            def SB(name, shape, dt):
                return esA.enter_context(nc.sbuf_tensor(name, shape, dt))

            wbuf = SB("wbufA", [128, 8192], F32)
            wb2 = SB("wb2A", [128, IFF], F32)
            qsl = SB("qslA", [128, IFF], F32)
            qs3 = SB("qs3A", [128, IFF], F32)
            qint = SB("qintA", [128, IFF], I32)
            qout = SB("qoutA", [128, 8192], BF16)
            sc_b = SB("scbA", [128, IFF], BF16)
            dq_b = SB("dqbA", [128, IFF], BF16)
            siluc = SB("siluc", [128, 16], BF16)
            mrow = SB("mrowA", [1, 512], F32)
            ps1 = esA.enter_context(nc.psum_tensor("psA1", [1, 512], F32))

            blk = esA.enter_context(nc.Block())

            @blk.sync
            def _(sync):
                sync.dma_start(out=convw[:], in_=d_convw).then_inc(dCn.sem, 16); dCn.n += 16
                sync.dma_start(out=convb[:], in_=d_convb).then_inc(dCn.sem, 16); dCn.n += 16
                sync.dma_start(out=bdt[:], in_=d_bdt).then_inc(dCn.sem, 16); dCn.n += 16
                sync.dma_start(out=Aneg[:], in_=d_Alog).then_inc(dCn.sem, 16); dCn.n += 16
                sync.dma_start(out=Dt[:], in_=d_Dt).then_inc(dCn.sem, 16); dCn.n += 16
                sync.dma_start(out=bout[:], in_=d_bout).then_inc(dCn.sem, 16); dCn.n += 16
                sync.dma_start(out=idn[:], in_=ident.ap()).then_inc(dCn.sem, 16); dCn.n += 16
                sync.dma_start(out=wbuf[:, 0:16], in_=d_cv).then_inc(dCn.sem, 16); dCn.n += 16
            t_cn = dCn.tok()

            @blk.vector
            def _(v):
                for col, val in ((0, EPS), (1, -0.5), (2, -1.0), (3, 1.0), (4, 1e-38), (5, 0.5)):
                    sV.bumpw(v, v.memset(cst[:, col:col + 1], val))
            t_cst = sV.tok()

            @blk.scalar
            def _(s):
                W(s, t_cn, t_cst)
                sA.bumpw(s, s.activation(out=Aneg[:], in_=Aneg[:], func=AF.Exp, bias=0.0, scale=1.0))
                sA.bumpw(s, s.activation(out=siluc[:], in_=wbuf[:, 0:16], func=AF.Silu, bias=0.0, scale=1.0))
            t_acts = sA.tok()

            @blk.vector
            def _(v):
                W(v, t_acts)
                sV.bumpw(v, v.tensor_scalar(out=Aneg[:], in0=Aneg[:], scalar1=-1.0, scalar2=None, op0=ALU.mult))

            @blk.sync
            def _(sync):
                W(sync, t_acts)
                sync.dma_start(out=wbuf[:, 0:5120],
                               in_=d_WxT.rearrange("(j p) r -> p j r", p=128)).then_inc(dCn.sem, 16); dCn.n += 16
                sync.dma_start(out=qsl[:, 0:DI], in_=d_WdtT).then_inc(dCn.sem, 16); dCn.n += 16
            t_wx = dCn.tok()

            @blk.vector
            def _(v):
                W(v, t_wx)
                sV.bumpw(v, v.tensor_copy(out=qout[:, 0:5120], in_=wbuf[:, 0:5120]))
                sV.bumpw(v, v.tensor_copy(out=qout[:, 5120:5120 + 2048], in_=qsl[:, 0:2048]))
                sV.bumpw(v, v.tensor_copy(out=sc_b[:, 0:2048], in_=qsl[:, 2048:DI]))
            t_wxc = sV.tok()

            @blk.sync
            def _(sync):
                W(sync, t_wxc)
                sync.dma_start(out=d_WxB, in_=qout[:, 0:5120]).then_inc(dSp.sem, 16); dSp.n += 16
                sync.dma_start(out=d_WdtB[:, 0:2048], in_=qout[:, 5120:5120 + 2048]).then_inc(dSp.sem, 16); dSp.n += 16
                sync.dma_start(out=d_WdtB[:, 2048:DI], in_=sc_b[:, 0:2048]).then_inc(dSp.sem, 16); dSp.n += 16

            for mg in range(24):
                @blk.sync
                def _(sync, mg=mg):
                    W(sync, sV.tok())
                    sync.dma_start(out=wbuf[:],
                                   in_=d_wmod[:, mg * 512:(mg + 1) * 512].rearrange("(j p) c -> p j c", p=128)
                                   ).then_inc(dCn.sem, 16); dCn.n += 16
                t_l = dCn.tok()

                @blk.vector
                def _(v, t_l=t_l):
                    W(v, t_l, sT.tok(), dSp.tok())
                    sV.bumpw(v, v.tensor_copy(out=qout[:], in_=wbuf[:]))
                t_c = sV.tok()

                @blk.tensor
                def _(t, t_c=t_c):
                    W(t, t_c, t_acts, sV.tok())
                    for kb in range(16):
                        ins = t.matmul(ps1[:], siluc[:, kb:kb + 1], qout[:, kb * 512:(kb + 1) * 512],
                                       start=(kb == 0), stop=(kb == 15))
                    sT.bump(ins)
                t_m = sT.tok()

                @blk.vector
                def _(v, t_m=t_m):
                    W(v, t_m, dSp.tok())
                    sV.bumpw(v, v.tensor_copy(out=mrow[:], in_=ps1[:]))
                t_e = sV.tok()

                @blk.sync
                def _(sync, mg=mg, t_e=t_e):
                    W(sync, t_e)
                    sync.dma_start(out=d_modrow[0:1, mg * 512:(mg + 1) * 512], in_=mrow[:]
                                   ).then_inc(dSp.sem, 16); dSp.n += 16
            t_modsp = dSp.tok()

            @blk.sync
            def _(sync):
                W(sync, t_modsp)
                with nc.allow_non_contiguous_dma(reason="tiny one-time per-partition mod vectors"):
                    for dst, off in ((sa_t, 0), (ca1, DM), (gaf, 2 * DM)):
                        sync.dma_start(out=dst[:], in_=d_modrow[0:1, off:off + DM].rearrange("a (j p) -> p a j", p=128)
                                       ).then_inc(dBr.sem, 16); dBr.n += 16
            t_br1 = dBr.tok()

            @blk.vector
            def _(v):
                W(v, t_br1)
                sV.bumpw(v, v.tensor_scalar(out=ca1[:], in0=ca1[:], scalar1=1.0, scalar2=None, op0=ALU.add))
                sV.bumpw(v, v.tensor_scalar(out=wb2[:, 0:16], in0=gaf[:], scalar1=0.0, scalar2=None, op0=ALU.is_ge))
                sV.bumpw(v, v.tensor_scalar(out=wb2[:, 0:16], in0=wb2[:, 0:16], scalar1=2.0, scalar2=-1.0,
                                        op0=ALU.mult, op1=ALU.add))
                sV.bumpw(v, v.tensor_scalar(out=wb2[:, 16:32], in0=gaf[:], scalar1=-1.0, scalar2=None, op0=ALU.mult))
                sV.bumpw(v, v.tensor_tensor(out=wb2[:, 16:32], in0=gaf[:], in1=wb2[:, 16:32], op=ALU.max))
                sV.bumpw(v, v.tensor_scalar(out=wb2[:, 16:32], in0=wb2[:, 16:32], scalar1=0.25, scalar2=None, op0=ALU.max))
                sV.bumpw(v, v.tensor_tensor(out=gaf[:], in0=wb2[:, 0:16], in1=wb2[:, 16:32], op=ALU.mult))
                sV.bumpw(v, v.tensor_tensor(out=gab[:], in0=gaf[:], in1=bout[:], op=ALU.mult))

            def cast_weight(dsrc, ddst, ncols):
                nslab = dsrc.shape[0] // 128
                for slx in range(nslab):
                    @blk.sync
                    def _(sync, slx=slx):
                        W(sync, dSp.tok())
                        sync.dma_start(out=wbuf[:, 0:ncols], in_=dsrc[slx * 128:(slx + 1) * 128, :]
                                       ).then_inc(dW0.sem, 16); dW0.n += 16
                    t_ld = dW0.tok()

                    @blk.vector
                    def _(v, t_ld=t_ld):
                        W(v, t_ld)
                        sV.bumpw(v, v.tensor_copy(out=qout[:, 0:ncols], in_=wbuf[:, 0:ncols]))
                    t_ca = sV.tok()

                    @blk.sync
                    def _(sync, slx=slx, t_ca=t_ca):
                        W(sync, t_ca)
                        sync.dma_start(out=ddst[slx * 128:(slx + 1) * 128, :], in_=qout[:, 0:ncols]
                                       ).then_inc(dSp.sem, 16); dSp.n += 16

            cast_weight(d_WinT, d_WinB, 2 * DI)
            cast_weight(d_WoutT, d_WoutB, DM)

            RG8 = [list(range(8))]

            def shard_slabs(nrows):
                out, r = [], 0
                while r < nrows:
                    p = min(128, nrows - r)
                    out.append((r, p))
                    r += p
                return out

            def shard_amax(dsrc_sh, nrows, ncols, d_amP, t_amP, t_amax, ccR):
                for slx, (r0, p) in enumerate(shard_slabs(nrows)):
                    @blk.sync
                    def _(sync, r0=r0, p=p):
                        W(sync, sV.tok(), sA.tok())
                        sync.dma_start(out=wbuf[0:p, 0:ncols], in_=dsrc_sh[r0:r0 + p, :]
                                       ).then_inc(dW0.sem, 16); dW0.n += 16
                    t_ld = dW0.tok()

                    @blk.scalar
                    def _(s, t_ld=t_ld, p=p):
                        W(s, t_ld, sV.tok())
                        sA.bumpw(s, s.activation(out=wb2[0:p, 0:ncols], in_=wbuf[0:p, 0:ncols], func=AF.Abs,
                                             bias=0.0, scale=1.0))
                    t_ab = sA.tok()

                    @blk.vector
                    def _(v, slx=slx, p=p, t_ab=t_ab):
                        W(v, t_ab)
                        if slx == 0:
                            sV.bumpw(v, v.tensor_copy(out=qsl[0:p, 0:ncols], in_=wb2[0:p, 0:ncols]))
                        else:
                            sV.bumpw(v, v.tensor_tensor(out=qsl[0:p, 0:ncols], in0=qsl[0:p, 0:ncols],
                                                    in1=wb2[0:p, 0:ncols], op=ALU.max))
                npar = min(128, nrows)
                while npar > 1:
                    half = (npar + 1) // 2

                    @blk.sync
                    def _(sync, half=half, npar=npar):
                        W(sync, sV.tok())
                        sync.dma_start(out=wb2[0:npar - half, 0:ncols], in_=qsl[half:npar, 0:ncols]
                                       ).then_inc(dBr.sem, 16); dBr.n += 16
                    t_mv = dBr.tok()

                    @blk.vector
                    def _(v, half=half, npar=npar, t_mv=t_mv):
                        W(v, t_mv)
                        sV.bumpw(v, v.tensor_tensor(out=qsl[0:npar - half, 0:ncols], in0=qsl[0:npar - half, 0:ncols],
                                                in1=wb2[0:npar - half, 0:ncols], op=ALU.max))
                    npar = half

                @blk.sync
                def _(sync):
                    W(sync, sV.tok())
                    sync.dma_start(out=d_amP[0:1, 0:ncols], in_=qsl[0:1, 0:ncols]).then_inc(dSp.sem, 16); dSp.n += 16
                t_am = dSp.tok()

                @blk.gpsimd
                def _(g, t_am=t_am):
                    W(g, t_am)
                    g.collective_compute("AllReduce", ALU.max, replica_groups=RG8,
                                         ins=[t_amP.ap().opt()], outs=[t_amax.ap().opt()]).then_inc(ccR)

            def quant_shard(dsrc_sh, d_qsh, t_qsh, t_QX, nrows, ncols, d_amax, ccR, ccG):
                @blk.sync
                def _(sync):
                    sync.wait_ge(ccR, 1)
                    W(sync, sV.tok(), sA.tok())
                    sync.dma_start(out=wb2[:, 0:ncols],
                                   in_=d_amax[0:1, 0:ncols].partition_broadcast(128)
                                   ).then_inc(dBr.sem, 16); dBr.n += 16
                t_bc = dBr.tok()

                @blk.vector
                def _(v, t_bc=t_bc):
                    W(v, t_bc)
                    sV.bumpw(v, v.tensor_scalar(out=dq_b[:, 0:ncols], in0=wb2[:, 0:ncols],
                                            scalar1=1.0 / 448.0, scalar2=None, op0=ALU.mult))
                    sV.bumpw(v, v.reciprocal(out=wb2[:, 0:ncols], in_=wb2[:, 0:ncols]))
                    sV.bumpw(v, v.tensor_scalar(out=sc_b[:, 0:ncols], in0=wb2[:, 0:ncols],
                                            scalar1=448.0, scalar2=None, op0=ALU.mult))
                t_sc = sV.tok()

                for r0, p in shard_slabs(nrows):
                    @blk.sync
                    def _(sync, r0=r0, p=p):
                        W(sync, sV.tok(), dSp.tok())
                        sync.dma_start(out=wbuf[0:p, 0:ncols], in_=dsrc_sh[r0:r0 + p, :]
                                       ).then_inc(dW1.sem, 16); dW1.n += 16
                    t_ld = dW1.tok()
                    emit_quant(blk, sV, sA,
                               src=wbuf[0:p, 0:ncols], scale_ap=sc_b[0:p, 0:ncols], dq_ap=dq_b[0:p, 0:ncols],
                               scr1=wb2[0:p, 0:ncols], scr2=qsl[0:p, 0:ncols], scr3=qs3[0:p, 0:ncols],
                               iscr=qint[0:p, 0:ncols],
                               out_bf=qout[0:p, 0:ncols], cstref=cst[0:p, :], pre_toks=(t_ld, t_sc))
                    t_q = sV.tok()

                    @blk.sync
                    def _(sync, r0=r0, p=p, t_q=t_q):
                        W(sync, t_q)
                        sync.dma_start(out=d_qsh[r0:r0 + p, :], in_=qout[0:p, 0:ncols]
                                       ).then_inc(dSp.sem, 16); dSp.n += 16
                t_spq = dSp.tok()

                @blk.gpsimd
                def _(g, t_spq=t_spq):
                    W(g, t_spq)
                    g.collective_compute("AllGather", ALU.bypass, replica_groups=RG8,
                                         ins=[t_qsh.ap().opt()], outs=[t_QX.ap().opt()]).then_inc(ccG)

            shard_amax(d_wgS, DM // 8, IFF, d_amgP, t_amgP, t_amaxg, ccRg)
            shard_amax(d_wuS, DM // 8, IFF, d_amuP, t_amuP, t_amaxu, ccRu)
            shard_amax(d_wdS, IFF // 8, DM, d_amdP, t_amdP, t_amaxd, ccRd)
            quant_shard(d_wgS, d_qgS, t_qgS, t_QG, DM // 8, IFF, d_amaxg, ccRg, ccGg)
            quant_shard(d_wuS, d_quS, t_quS, t_QU, DM // 8, IFF, d_amaxu, ccRu, ccGu)
            quant_shard(d_wdS, d_qdS, t_qdS, t_QD, IFF // 8, DM, d_amaxd, ccRd, ccGd)

            @blk.sync
            def _(sync):
                W(sync, dSp.tok(), sV.tok(), sA.tok(), sT.tok(), dBr.tok())

        esB = contextlib.ExitStack()
        with esB:
            def SB(name, shape, dt):
                return esB.enter_context(nc.sbuf_tensor(name, shape, dt))

            xs = SB("xs", [128, NDB * LC], BF16)
            bufA = SB("bufA", [128, NST * LC], BF16)
            bufB = SB("bufB", [128, NST * LC], BF16)
            bufH = SB("bufH", [128, NST * LC], BF16)
            wsl = [SB("wsl0", [128, 2048], BF16), SB("wsl1", [128, 2048], BF16)]
            xcv = SB("xcv0", [128, LC + 3], BF16)
            xc = SB("xc", [128, NIB * LC], BF16)
            sz = SB("szb", [128, NIB * LC], BF16)
            Bb = SB("Bb", [128, NST * LC], BF16)
            Cb = SB("Cb", [128, NST * LC], BF16)
            dtl = SB("dtl", [128, LC], BF16)
            dlt = SB("dlt", [128, LC], F32)
            dl2 = SB("dl2", [128, LC], F32)
            rs2 = SB("rs2", [128, LC], F32)
            du = SB("du", [128, LC], BF16)
            r16 = SB("r16", [128, 16], F32)
            t16 = SB("t16", [128, 16], F32)
            carry = SB("carry", [128, NIB * NST], F32)
            halo = SB("halo", [128, NIB * 3], BF16)
            mrb = SB("mrb", [128, LC], F32)
            rsb = SB("rsb", [128, LC], F32)
            bcsb = SB("bcsb", [32, LC], BF16)
            onesb = SB("onesb", [128, 1], BF16)
            xstg = SB("xstg", [128, 512], F32)
            WxB = SB("WxB", [128, 32 * 160], BF16)
            WdtB = SB("WdtB", [128, DI], BF16)

            pz0 = esB.enter_context(nc.psum_tensor("pz0", [128, 512], F32))
            pz1 = esB.enter_context(nc.psum_tensor("pz1", [128, 512], F32))
            pd1 = esB.enter_context(nc.psum_tensor("pd1", [128, 512], F32))
            pd2 = esB.enter_context(nc.psum_tensor("pd2", [32, 512], F32))
            pss = esB.enter_context(nc.psum_tensor("pss", [1, 512], F32))
            psq = esB.enter_context(nc.psum_tensor("psq", [1, 512], F32))
            ptr = esB.enter_context(nc.psum_tensor("ptr", [128, 128], BF16))

            blk = esB.enter_context(nc.Block())

            @blk.vector
            def _(v):
                sV.bumpw(v, v.memset(onesb[:], 1.0))
                sV.bumpw(v, v.memset(carry[:], 0.0))
                sV.bumpw(v, v.memset(halo[:], 0.0))
            t_init = sV.tok()

            @blk.sync
            def _(sync):
                sync.dma_start(out=WxB[:], in_=d_WxB).then_inc(dCn.sem, 16); dCn.n += 16
                sync.dma_start(out=WdtB[:], in_=d_WdtB).then_inc(dCn.sem, 16); dCn.n += 16
            t_wres = dCn.tok()

            for ch in range(NCH):
                for grp in range(16):
                    @blk.sync
                    def _(sync, ch=ch, grp=grp):
                        W(sync, sV.tok())
                        sync.dma_start(out=xstg[:],
                                       in_=d_xT[grp * 128:(grp + 1) * 128, ch * LC:(ch + 1) * LC]
                                       ).then_inc(dIn.sem, 16); dIn.n += 16
                    t_x = dIn.tok()

                    @blk.vector
                    def _(v, grp=grp, t_x=t_x):
                        W(v, t_x)
                        sV.bumpw(v, v.tensor_copy(out=xs[:, grp * LC:(grp + 1) * LC], in_=xstg[:]))
                t_xs = sV.tok()

                @blk.scalar
                def _(s):
                    W(s, t_xs)
                    sA.bumpw(s, s.activation(out=bufB[:], in_=xs[:], func=AF.Square, bias=0.0, scale=1.0))
                t_sq = sA.tok()

                @blk.tensor
                def _(t):
                    W(t, t_xs, t_sq, t_init, sV.tok())
                    for j in range(NDB):
                        i1 = t.matmul(pss[:], onesb[:], xs[:, j * LC:(j + 1) * LC],
                                      start=(j == 0), stop=(j == NDB - 1))
                    sT.bump(i1)
                    for j in range(NDB):
                        i2 = t.matmul(psq[:], onesb[:], bufB[:, j * LC:(j + 1) * LC],
                                      start=(j == 0), stop=(j == NDB - 1))
                    sT.bump(i2)
                t_st = sT.tok()

                @blk.vector
                def _(v):
                    W(v, t_st, dSp.tok())
                    sV.bumpw(v, v.tensor_scalar(out=rsb[0:1, 0:LC], in0=pss[:], scalar1=1.0 / DM, scalar2=None, op0=ALU.mult))
                    sV.bumpw(v, v.tensor_scalar(out=mrb[0:1, 0:LC], in0=psq[:], scalar1=1.0 / DM, scalar2=None, op0=ALU.mult))
                    sV.bumpw(v, v.tensor_tensor(out=dlt[0:1, 0:LC], in0=rsb[0:1, 0:LC], in1=rsb[0:1, 0:LC], op=ALU.mult))
                    sV.bumpw(v, v.tensor_tensor(out=mrb[0:1, 0:LC], in0=mrb[0:1, 0:LC], in1=dlt[0:1, 0:LC], op=ALU.subtract))
                t_va = sV.tok()

                @blk.scalar
                def _(s):
                    W(s, t_va)
                    sA.bumpw(s, s.activation(out=mrb[0:1, 0:LC], in_=mrb[0:1, 0:LC], func=AF.Ln,
                                         bias=cst[0:1, 0:1], scale=1.0))
                    sA.bumpw(s, s.activation(out=mrb[0:1, 0:LC], in_=mrb[0:1, 0:LC], func=AF.Exp,
                                         bias=0.0, scale=cst[0:1, 1:2]))
                t_rs = sA.tok()

                @blk.vector
                def _(v):
                    W(v, t_rs)
                    sV.bumpw(v, v.tensor_tensor(out=dlt[0:1, 0:LC], in0=rsb[0:1, 0:LC], in1=mrb[0:1, 0:LC], op=ALU.mult))
                t_mr = sV.tok()

                @blk.sync
                def _(sync, ch=ch, t_mr=t_mr):
                    W(sync, t_mr)
                    sync.dma_start(out=d_srow[ch:ch + 1, 0:LC], in_=dlt[0:1, 0:LC]).then_inc(dSp.sem, 16); dSp.n += 16
                    sync.dma_start(out=d_srow[ch:ch + 1, LC:2 * LC], in_=mrb[0:1, 0:LC]).then_inc(dSp.sem, 16); dSp.n += 16
                t_ssp = dSp.tok()

                @blk.sync
                def _(sync, ch=ch, t_ssp=t_ssp):
                    W(sync, t_ssp, sV.tok())
                    sync.dma_start(out=mrb[:], in_=d_srow[ch:ch + 1, 0:LC].partition_broadcast(128)
                                   ).then_inc(dBr.sem, 16); dBr.n += 16
                    sync.dma_start(out=rsb[:], in_=d_srow[ch:ch + 1, LC:2 * LC].partition_broadcast(128)
                                   ).then_inc(dBr.sem, 16); dBr.n += 16
                t_bcs = dBr.tok()

                for j in range(NDB):
                    @blk.vector
                    def _(v, j=j):
                        W(v, t_bcs)
                        sV.bumpw(v, v.tensor_tensor(out=dlt[:], in0=xs[:, j * LC:(j + 1) * LC], in1=rsb[:], op=ALU.mult))
                        sV.bumpw(v, v.tensor_tensor(out=dlt[:], in0=dlt[:], in1=mrb[:], op=ALU.subtract))
                        sV.bumpw(v, v.tensor_scalar(out=bufH[:, j * LC:(j + 1) * LC], in0=dlt[:],
                                                scalar1=ca1[:, j:j + 1], scalar2=sa_t[:, j:j + 1],
                                                op0=ALU.mult, op1=ALU.add))
                t_h = sV.tok()

                t_slab_cons = [sT.tok(), sT.tok()]
                t_bankfree = [sV.tok(), sV.tok()]
                t_mrbfree = sA.tok()
                for te in range(64):
                    par = te % 2
                    dWp = dW0 if par == 0 else dW1
                    pz = pz0 if te % 2 == 0 else pz1

                    @blk.sync
                    def _(sync, te=te, par=par, dWp=dWp, tc=t_slab_cons[par]):
                        W(sync, tc)
                        sync.dma_start(out=wsl[par][:],
                                       in_=d_WinB[:, te * 128:(te + 1) * 128].rearrange("(j p) c -> p j c", p=128)
                                       ).then_inc(dWp.sem, 16); dWp.n += 16
                    t_w = dWp.tok()

                    if True:
                        @blk.tensor
                        def _(t, par=par, t_w=t_w, tf=t_bankfree[te % 2], pz=pz):
                            W(t, t_w, tf, t_h)
                            for j in range(NDB):
                                ins = t.matmul(pz[:], wsl[par][:, j * 128:(j + 1) * 128],
                                               bufH[:, j * LC:(j + 1) * LC],
                                               start=(j == 0), stop=(j == NDB - 1))
                            sT.bump(ins)
                        t_mm = sT.tok()
                        t_slab_cons[par] = t_mm

                        if te < NIB:
                            @blk.vector
                            def _(v, te=te, t_mm=t_mm, pz=pz, tmf=t_mrbfree):
                                W(v, t_mm, tmf)
                                sV.bumpw(v, v.tensor_copy(out=xcv[:, 3:3 + LC], in_=pz[:]))
                                sV.bumpw(v, v.tensor_copy(out=xcv[:, 0:3], in_=halo[:, te * 3:te * 3 + 3]))
                                sV.bumpw(v, v.tensor_copy(out=halo[:, te * 3:te * 3 + 3], in_=xcv[:, LC:LC + 3]))
                                sV.bumpw(v, v.tensor_scalar(out=mrb[:], in0=xcv[:, 0:LC],
                                                        scalar1=convw[:, te * 4:te * 4 + 1], scalar2=None, op0=ALU.mult))
                                for k in range(1, 4):
                                    sV.bumpw(v, v.scalar_tensor_tensor(out=mrb[:], in0=xcv[:, k:k + LC],
                                                                   scalar=convw[:, te * 4 + k:te * 4 + k + 1],
                                                                   in1=mrb[:], op0=ALU.mult, op1=ALU.add))
                            t_cv = sV.tok()
                            t_bankfree[te % 2] = t_cv

                            @blk.scalar
                            def _(s, te=te, t_cv=t_cv):
                                W(s, t_cv)
                                sA.bumpw(s, s.activation(out=xc[:, te * LC:(te + 1) * LC], in_=mrb[:], func=AF.Silu,
                                                     bias=convb[:, te:te + 1], scale=1.0))
                            t_mrbfree = sA.tok()
                        else:
                            tz = te - NIB

                            @blk.scalar
                            def _(s, tz=tz, t_mm=t_mm, pz=pz):
                                W(s, t_mm)
                                sA.bumpw(s, s.activation(out=sz[:, tz * LC:(tz + 1) * LC], in_=pz[:], func=AF.Silu,
                                                     bias=0.0, scale=1.0))
                            t_bankfree[te % 2] = sA.tok()
                t_xcz = sA.tok()

                @blk.tensor
                def _(t):
                    W(t, t_xcz, t_wres, sV.tok())
                    for j in range(NIB):
                        i1 = t.matmul(pd1[:], WxB[:, j * 160:j * 160 + 128], xc[:, j * LC:(j + 1) * LC],
                                      start=(j == 0), stop=(j == NIB - 1))
                    sT.bump(i1)
                    for j in range(NIB):
                        i2 = t.matmul(pd2[:], WxB[:, j * 160 + 128:j * 160 + 160], xc[:, j * LC:(j + 1) * LC],
                                      start=(j == 0), stop=(j == NIB - 1))
                    sT.bump(i2)
                t_xd = sT.tok()

                @blk.vector
                def _(v):
                    W(v, t_xd, dSp.tok())
                    sV.bumpw(v, v.tensor_copy(out=dtl[:], in_=pd1[:]))
                    sV.bumpw(v, v.tensor_copy(out=bcsb[:], in_=pd2[:]))
                t_bcev = sV.tok()

                @blk.sync
                def _(sync, ch=ch, t_bcev=t_bcev):
                    W(sync, t_bcev)
                    sync.dma_start(out=d_bc[ch:ch + 1, :].rearrange("a (p c) -> (a p) c", p=32),
                                   in_=bcsb[:]).then_inc(dSp.sem, 16); dSp.n += 16
                t_bcsp = dSp.tok()

                @blk.sync
                def _(sync, ch=ch, t_bcsp=t_bcsp):
                    W(sync, t_bcsp, sV.tok())
                    sync.dma_start(out=Bb[:], in_=d_bc[ch:ch + 1, 0:NST * LC].partition_broadcast(128)
                                   ).then_inc(dBr.sem, 16); dBr.n += 16
                    sync.dma_start(out=Cb[:], in_=d_bc[ch:ch + 1, NST * LC:2 * NST * LC].partition_broadcast(128)
                                   ).then_inc(dBr.sem, 16); dBr.n += 16
                t_bcb = dBr.tok()

                dltP = [dlt, dl2]
                rsbP = [rsb, rs2]
                pzP = [pz0, pz1]
                t_b9v = sV.tok(); t_b9a = sA.tok(); t_b9sp = dSp.tok()
                tsc = [t_b9v, t_b9v]
                tpd = [t_b9a, t_b9a]
                tdafree = [t_b9a, t_b9a]
                tmm = [None, None]
                tsp = [None, None]
                tdl = [None, None]

                def emit_mm(j):
                    par = j % 2

                    @blk.tensor
                    def _(t, j=j, par=par, tw=tpd[par]):
                        W(t, tw, t_bcev, t_xcz)
                        sT.bump(t.matmul(pzP[par][:], WdtB[:, j * 128:(j + 1) * 128], dtl[:], start=True, stop=True))
                    tmm[par] = sT.tok()

                def emit_sp(j, tduj=None):
                    par = j % 2

                    @blk.scalar
                    def _(s, j=j, par=par, tw=tmm[par], tfree=tdl[par], tdaf=tdafree[par], tduj=tduj):
                        W(s, tw, tfree, tdaf, tduj, t_b9v)
                        sA.bumpw(s, s.activation(out=rsbP[par][:], in_=pzP[par][:], func=AF.Abs,
                                             bias=bdt[:, j:j + 1], scale=1.0))
                        sA.bumpw(s, s.activation(out=rsbP[par][:], in_=rsbP[par][:], func=AF.Exp,
                                             bias=0.0, scale=cst[:, 2:3]))
                        sA.bumpw(s, s.activation(out=rsbP[par][:], in_=rsbP[par][:], func=AF.Ln,
                                             bias=cst[:, 3:4], scale=1.0))
                        sA.bumpw(s, s.activation(out=dltP[par][:], in_=pzP[par][:], func=AF.Relu,
                                             bias=bdt[:, j:j + 1], scale=1.0))
                    tsp[par] = sA.tok()
                    tpd[par] = tsp[par]

                def emit_dlt(j):
                    par = j % 2

                    @blk.vector
                    def _(v, par=par, tw=tsp[par]):
                        W(v, tw)
                        sV.bumpw(v, v.tensor_tensor(out=dltP[par][:], in0=dltP[par][:], in1=rsbP[par][:], op=ALU.add))
                    tdl[par] = sV.tok()

                emit_mm(0); emit_sp(0); emit_dlt(0)
                emit_mm(1); emit_sp(1); emit_dlt(1)

                for j in range(NIB):
                    par = j % 2

                    @blk.vector
                    def _(v, j=j, par=par, tw=tdl[par]):
                        W(v, tw, t_xcz)
                        sV.bumpw(v, v.tensor_tensor(out=du[:], in0=dltP[par][:],
                                                in1=xc[:, j * LC:(j + 1) * LC], op=ALU.mult))
                    tdu_j = sV.tok()

                    for h in (0, 1):
                        o = h * 8 * LC

                        @blk.scalar
                        def _(s, j=j, par=par, o=o, tw=tsc[h], tdlj=tdl[par]):
                            W(s, tdlj, tw, t_b9sp)
                            for n in range(8):
                                k = j * 16 + (o // LC) + n
                                ins = s.activation(out=bufA[:, o + n * LC:o + (n + 1) * LC], in_=dltP[par][:],
                                                   func=AF.Exp, bias=0.0, scale=Aneg[:, k:k + 1])
                            sA.bumpw(s, ins)
                        tda_h = sA.tok()
                        if h == 1:
                            tdafree[par] = tda_h

                        @blk.vector
                        def _(v, j=j, h=h, o=o, tda_h=tda_h):
                            W(v, t_bcb)
                            for n in range(8):
                                ins = v.tensor_tensor(out=bufB[:, o + n * LC:o + (n + 1) * LC], in0=du[:],
                                                      in1=Bb[:, o + n * LC:o + (n + 1) * LC], op=ALU.mult)
                            sV.bumpw(v, ins)
                            W(v, tda_h)
                            dA3 = bufA[:, o:o + 8 * LC].rearrange("p (n l) -> p n l", n=8)
                            dB3 = bufB[:, o:o + 8 * LC].rearrange("p (n l) -> p n l", n=8)
                            hs = h * 8
                            sV.bumpw(v, v.tensor_copy(out=r16[:, hs:hs + 8],
                                                  in_=dA3[:, :, 0:1].rearrange("p n o -> p (n o)")))
                            sV.bumpw(v, v.memset(dA3[:, :, 0:1], 0.0))
                            sV.bumpw(v, v.tensor_tensor(out=t16[:, hs:hs + 8], in0=r16[:, hs:hs + 8],
                                                    in1=carry[:, j * NST + hs:j * NST + hs + 8], op=ALU.mult))
                            sV.bumpw(v, v.tensor_tensor(out=dB3[:, :, 0:1], in0=dB3[:, :, 0:1],
                                                    in1=t16[:, hs:hs + 8].rearrange("p (n o) -> p n o", o=1),
                                                    op=ALU.add))
                            sV.bumpw(v, v.tensor_tensor_scan(out=bufH[:, o:o + 8 * LC],
                                                         data0=bufA[:, o:o + 8 * LC],
                                                         data1=bufB[:, o:o + 8 * LC],
                                                         initial=0.0, op0=ALU.mult, op1=ALU.add))
                            st3 = bufH[:, o:o + 8 * LC].rearrange("p (n l) -> p n l", n=8)
                            sV.bumpw(v, v.tensor_copy(out=carry[:, j * NST + hs:j * NST + hs + 8],
                                                  in_=st3[:, :, LC - 1:LC].rearrange("p n o -> p (n o)")))
                        tsc[h] = sV.tok()

                    if j + 2 < NIB:
                        emit_mm(j + 2)
                        emit_sp(j + 2, tdu_j)
                        emit_dlt(j + 2)

                    @blk.vector
                    def _(v, j=j):
                        W(v, t_bcb, t_xcz)
                        sV.bumpw(v, v.tensor_tensor(out=bufH[:], in0=bufH[:], in1=Cb[:], op=ALU.mult))
                        sV.bumpw(v, v.tensor_tensor(out=bufH[:, 0:8 * LC], in0=bufH[:, 0:8 * LC],
                                                in1=bufH[:, 8 * LC:16 * LC], op=ALU.add))
                        sV.bumpw(v, v.tensor_tensor(out=bufH[:, 0:4 * LC], in0=bufH[:, 0:4 * LC],
                                                in1=bufH[:, 4 * LC:8 * LC], op=ALU.add))
                        sV.bumpw(v, v.tensor_tensor(out=bufH[:, 0:2 * LC], in0=bufH[:, 0:2 * LC],
                                                in1=bufH[:, 2 * LC:4 * LC], op=ALU.add))
                        sV.bumpw(v, v.tensor_tensor(out=bufH[:, 0:LC], in0=bufH[:, 0:LC],
                                                in1=bufH[:, LC:2 * LC], op=ALU.add))
                        sV.bumpw(v, v.scalar_tensor_tensor(out=xstg[:], in0=xc[:, j * LC:(j + 1) * LC],
                                                       scalar=Dt[:, j:j + 1], in1=bufH[:, 0:LC],
                                                       op0=ALU.mult, op1=ALU.add))
                        sV.bumpw(v, v.tensor_tensor(out=xc[:, j * LC:(j + 1) * LC], in0=xstg[:],
                                                in1=sz[:, j * LC:(j + 1) * LC], op=ALU.mult))
                t_scan = sV.tok()
                t_aend = sA.tok()

                t_wcons = [t_scan, t_scan]
                t_ofree = [t_scan, t_scan]
                for dm in range(NDB):
                    pz = pz0 if dm % 2 == 0 else pz1
                    t_ws = []
                    for hh in range(2):
                        par = (2 * dm + hh) % 2
                        dWp = dW0 if par == 0 else dW1

                        @blk.sync
                        def _(sync, dm=dm, hh=hh, par=par, dWp=dWp, tc=t_wcons[par]):
                            W(sync, tc)
                            sync.dma_start(out=wsl[par][:],
                                           in_=d_WoutB[hh * 2048:(hh + 1) * 2048, dm * 128:(dm + 1) * 128]
                                           .rearrange("(j p) c -> p j c", p=128)
                                           ).then_inc(dWp.sem, 16); dWp.n += 16
                        t_ws.append(dWp.tok())

                    @blk.tensor
                    def _(t, dm=dm, t_ws=t_ws, tf=t_ofree[dm % 2], pz=pz):
                        W(t, t_ws[0], t_ws[1], tf, t_scan, t_aend)
                        for hh in range(2):
                            par = (2 * dm + hh) % 2
                            for j in range(16):
                                jj = hh * 16 + j
                                ins = t.matmul(pz[:], wsl[par][:, j * 128:(j + 1) * 128],
                                               xc[:, jj * LC:(jj + 1) * LC],
                                               start=(hh == 0 and j == 0), stop=(hh == 1 and j == 15))
                        sT.bump(ins)
                    t_mm = sT.tok()
                    t_wcons[0] = t_mm
                    t_wcons[1] = t_mm

                    @blk.vector
                    def _(v, dm=dm, t_mm=t_mm, pz=pz):
                        W(v, t_mm)
                        sV.bumpw(v, v.tensor_scalar(out=mrb[:], in0=pz[:],
                                                scalar1=gaf[:, dm:dm + 1], scalar2=gab[:, dm:dm + 1],
                                                op0=ALU.mult, op1=ALU.add))
                        sV.bumpw(v, v.tensor_tensor(out=sz[:, dm * LC:(dm + 1) * LC], in0=mrb[:],
                                                in1=xs[:, dm * LC:(dm + 1) * LC], op=ALU.add))
                    t_ofree[dm % 2] = sV.tok()
                t_x2 = sV.tok()

                for dm in range(NDB):
                    for rt in range(RT):
                        @blk.tensor
                        def _(t, dm=dm, rt=rt):
                            W(t, t_x2, sV.tok())
                            sT.bump(t.transpose(ptr[:], sz[:, dm * LC + rt * 128:dm * LC + rt * 128 + 128], idn[:]))
                        t_tp = sT.tok()

                        @blk.vector
                        def _(v, dm=dm, rt=rt, t_tp=t_tp):
                            W(v, t_tp)
                            sV.bumpw(v, v.tensor_copy(out=bufA[:, rt * DM + dm * 128:rt * DM + dm * 128 + 128], in_=ptr[:]))
                t_x2t = sV.tok()

                @blk.sync
                def _(sync, ch=ch, t_x2t=t_x2t):
                    W(sync, t_x2t)
                    sync.dma_start(out=d_x2[ch * LC:(ch + 1) * LC, :].rearrange("(rt p) c -> p rt c", p=128),
                                   in_=bufA[:]).then_inc(dSp.sem, 16); dSp.n += 16

            @blk.sync
            def _(sync):
                W(sync, dSp.tok(), sV.tok(), sA.tok(), sT.tok(), dBr.tok())

        esC = contextlib.ExitStack()
        with esC:
            def SB(name, shape, dt):
                return esC.enter_context(nc.sbuf_tensor(name, shape, dt))

            x2tc = [SB("x2tc0", [128, RT * DM], BF16), SB("x2tc1", [128, RT * DM], BF16)]
            qx = SB("qx", [128, RT * DM], BF16)
            qxT = [SB("qxT0", [128, NDB * LC], BF16), SB("qxT1", [128, NDB * LC], BF16)]
            qgs = [SB("qgs0", [128, NDB * 128], BF16), SB("qgs1", [128, NDB * 128], BF16)]
            qus = [SB("qus0", [128, NDB * 128], BF16), SB("qus1", [128, NDB * 128], BF16)]
            qts = SB("qts", [128, 11 * 512], BF16)
            wds = SB("wds", [128, 11 * 512], BF16)
            h2 = SB("h2", [128, DM], F32)
            c1s = SB("c1s", [128, DM], F32)
            c2s = SB("c2s", [128, DM], F32)
            t0w = SB("t0w", [128, 1024], BF16)
            tqw = SB("tqw", [128, 1024], BF16)
            tqf = SB("tqf", [128, 1024], F32)
            tqf2 = SB("tqf2", [128, 1024], F32)
            tqf3 = SB("tqf3", [128, 1024], F32)
            tqf4 = SB("tqf4", [128, 1024], F32)
            tb = SB("tb", [128, 512], BF16)
            rmax_t = SB("rmax_t", [128, 512], F32)
            sc_bc = [SB("sc_bc0", [128, 1024], F32), SB("sc_bc1", [128, 1024], F32)]
            dq_bc = [SB("dq_bc0", [128, 1024], F32), SB("dq_bc1", [128, 1024], F32)]
            ost = [SB("ost0", [128, 512], F32), SB("ost1", [128, 512], F32)]
            rmax = SB("rmaxC", [128, 8], F32)
            sxr = SB("sxr", [128, RT], F32)
            dqr = SB("dqr", [128, RT], F32)
            t0t = SB("t0t", [128, 512], BF16)
            cb1T = SB("cb1T", [128, DM], BF16)
            sbT = SB("sbT", [128, DM], BF16)
            gbT = SB("gbT", [128, DM], BF16)

            pg = esC.enter_context(nc.psum_tensor("pgC", [128, 512], F32))
            pu = esC.enter_context(nc.psum_tensor("puC", [128, 512], F32))
            pdn = [esC.enter_context(nc.psum_tensor("pdC%d" % i, [128, 512], F32)) for i in range(4)]
            ptr2 = esC.enter_context(nc.psum_tensor("ptrC", [128, 128], BF16))

            blk = esC.enter_context(nc.Block())

            @blk.sync
            def _(sync):
                sync.dma_start(out=h2[:], in_=d_modrow[0:1, 3 * DM:4 * DM].partition_broadcast(128)
                               ).then_inc(dBr.sem, 16); dBr.n += 16
                sync.dma_start(out=c1s[:], in_=d_modrow[0:1, 4 * DM:5 * DM].partition_broadcast(128)
                               ).then_inc(dBr.sem, 16); dBr.n += 16
                sync.dma_start(out=c2s[:], in_=d_modrow[0:1, 5 * DM:6 * DM].partition_broadcast(128)
                               ).then_inc(dBr.sem, 16); dBr.n += 16
            t_mbr = dBr.tok()

            @blk.vector
            def _(v):
                W(v, t_mbr)
                sV.bumpw(v, v.tensor_copy(out=sbT[:], in_=h2[:]))
                sV.bumpw(v, v.tensor_scalar(out=cb1T[:], in0=c1s[:], scalar1=1.0, scalar2=None, op0=ALU.add))
                sV.bumpw(v, v.tensor_scalar(out=h2[:], in0=c2s[:], scalar1=0.0, scalar2=None, op0=ALU.is_ge))
                sV.bumpw(v, v.tensor_scalar(out=h2[:], in0=h2[:], scalar1=2.0, scalar2=-1.0, op0=ALU.mult, op1=ALU.add))
                sV.bumpw(v, v.tensor_scalar(out=c1s[:], in0=c2s[:], scalar1=-1.0, scalar2=None, op0=ALU.mult))
                sV.bumpw(v, v.tensor_tensor(out=c1s[:], in0=c2s[:], in1=c1s[:], op=ALU.max))
                sV.bumpw(v, v.tensor_scalar(out=c1s[:], in0=c1s[:], scalar1=0.25, scalar2=None, op0=ALU.max))
                sV.bumpw(v, v.tensor_tensor(out=gbT[:], in0=h2[:], in1=c1s[:], op=ALU.mult))
            t_modT = sV.tok()


            t_outprev = dOu.tok()
            ost_tok = [dOu.tok(), dOu.tok()]
            t_evprev = sV.tok()
            t_qgu = [sV.tok(), sV.tok()]
            t_x2free = [sV.tok(), sV.tok()]
            t_qxTf = [sT.tok(), sT.tok()]
            t_dnall = [dW0.tok(), dW0.tok()]
            t_tsc_p = [None, None]
            t_t0all_p = [None, None]
            t_pgu_init = [sA.tok(), sV.tok()]
            t_rs2_last = [sA.tok()]
            t_qx_p = [None, None]
            t_x2l_p = [None, None]

            def emit_ln2qx(ch):
                par = ch % 2

                @blk.sync
                def _(sync, ch=ch, par=par, tx=t_x2free[par]):
                    W(sync, tx, sV.tok())
                    if ch == 0:
                        sync.wait_ge(ccGg, 1)
                        sync.wait_ge(ccGu, 1)
                        sync.wait_ge(ccGd, 1)
                    sync.dma_start(out=x2tc[par][:],
                                   in_=d_x2[ch * LC:(ch + 1) * LC, :].rearrange("(rt p) c -> p rt c", p=128)
                                   ).then_inc(dIn.sem, 16); dIn.n += 16
                t_x2l = dIn.tok()
                t_x2l_p[par] = t_x2l

                for rt in range(RT):
                    @blk.vector
                    def _(v, rt=rt, par=par, t_x2l=t_x2l, tln=t_rs2_last[0]):
                        W(v, t_x2l, t_modT, tln)
                        xrt = x2tc[par][:, rt * DM:(rt + 1) * DM]
                        sV.bumpw(v, v.tensor_reduce(out=rmax[:, 0:1], in_=xrt, axis=AX.X, op=ALU.add))
                        sV.bumpw(v, v.tensor_scalar(out=rmax[:, 0:1], in0=rmax[:, 0:1], scalar1=1.0 / DM, scalar2=None, op0=ALU.mult))
                    t_mean = sV.tok()

                    @blk.scalar
                    def _(s, rt=rt, par=par, t_mean=t_mean):
                        W(s, t_mean)
                        sA.bumpw(s, s.activation(out=c1s[:], in_=x2tc[par][:, rt * DM:(rt + 1) * DM], func=AF.Square,
                                             bias=0.0, scale=1.0))
                    t_sq2 = sA.tok()

                    @blk.vector
                    def _(v, rt=rt, t_sq2=t_sq2):
                        W(v, t_sq2)
                        sV.bumpw(v, v.tensor_reduce(out=rmax[:, 1:2], in_=c1s[:], axis=AX.X, op=ALU.add))
                        sV.bumpw(v, v.tensor_scalar(out=rmax[:, 1:2], in0=rmax[:, 1:2], scalar1=1.0 / DM, scalar2=None, op0=ALU.mult))
                        sV.bumpw(v, v.tensor_tensor(out=rmax[:, 2:3], in0=rmax[:, 0:1], in1=rmax[:, 0:1], op=ALU.mult))
                        sV.bumpw(v, v.tensor_tensor(out=rmax[:, 1:2], in0=rmax[:, 1:2], in1=rmax[:, 2:3], op=ALU.subtract))
                    t_va2 = sV.tok()

                    @blk.scalar
                    def _(s, rt=rt, t_va2=t_va2):
                        W(s, t_va2)
                        sA.bumpw(s, s.activation(out=rmax[:, 1:2], in_=rmax[:, 1:2], func=AF.Ln, bias=cst[:, 0:1], scale=1.0))
                        sA.bumpw(s, s.activation(out=rmax[:, 1:2], in_=rmax[:, 1:2], func=AF.Exp, bias=0.0, scale=cst[:, 1:2]))
                    t_rs2 = sA.tok()
                    t_rs2_last[0] = t_rs2

                    @blk.vector
                    def _(v, rt=rt, par=par, t_rs2=t_rs2):
                        W(v, t_rs2)
                        xrt = x2tc[par][:, rt * DM:(rt + 1) * DM]
                        sV.bumpw(v, v.tensor_scalar(out=h2[:], in0=xrt, scalar1=rmax[:, 0:1], scalar2=rmax[:, 1:2],
                                                op0=ALU.subtract, op1=ALU.mult))
                        sV.bumpw(v, v.tensor_tensor(out=h2[:], in0=h2[:], in1=cb1T[:], op=ALU.mult))
                        sV.bumpw(v, v.tensor_tensor(out=h2[:], in0=h2[:], in1=sbT[:], op=ALU.add))
                        sV.bumpw(v, v.tensor_reduce(out=sxr[:, rt:rt + 1], in_=h2[:], axis=AX.X, op=ALU.max,
                                                apply_absolute_value=True))
                        sV.bumpw(v, v.reciprocal(out=dqr[:, rt:rt + 1], in_=sxr[:, rt:rt + 1]))
                        sV.bumpw(v, v.tensor_scalar(out=dqr[:, rt:rt + 1], in0=dqr[:, rt:rt + 1],
                                                scalar1=448.0, scalar2=None, op0=ALU.mult))
                        sV.bumpw(v, v.tensor_scalar(out=sxr[:, rt:rt + 1], in0=sxr[:, rt:rt + 1],
                                                scalar1=1.0 / 448.0, scalar2=None, op0=ALU.mult))
                    t_h2 = sV.tok()

                    for hw in range(2):
                        emit_quant_pure(blk, sV, sA,
                                   src=h2[:, hw * 1024:(hw + 1) * 1024],
                                   scale_ap=dqr[:, rt:rt + 1], dq_ap=sxr[:, rt:rt + 1],
                                   scr1=tqf[:], scr2=tqf2[:], scr3=tqf3[:], scr4=tqf4[:],
                                   out_bf=qx[:, rt * DM + hw * 1024:rt * DM + (hw + 1) * 1024],
                                   cstref=cst, pre_toks=(t_h2,))
                t_qx_p[par] = sV.tok()

                @blk.vector
                def _(v):
                    W(v, dBr.tok())
                    sV.bumpw(v, v.memset(rmax_t[:], 0.0))

            def emit_tquant(ch):
                par = ch % 2
                t_qt_g = [None] * 4
                gcut = {5: 0, 10: 1, 16: 2, 21: 3}
                for g in range(22):
                    @blk.gpsimd
                    def _(gq, g=g, par=par):
                        W(gq, t_t0all_p[par], sV.tok(), sA.tok(), dQs.tok())
                        gq.dma_start(out=t0w[:].rearrange("p (k c) -> p k c", k=2),
                                     in_=d_t0p[par][g * 256:(g + 1) * 256, :].rearrange("(k p) c -> p k c", p=128)
                                     ).then_inc(dTq.sem, 16); dTq.n += 16
                    t_tl = dTq.tok()
                    emit_quant_pure(blk, sV, sA,
                               src=t0w[:], scale_ap=sc_bc[par][:], dq_ap=dq_bc[par][:],
                               scr1=tqf[:], scr2=tqf2[:], scr3=tqf3[:], scr4=tqf4[:],
                               out_bf=tqw[:], cstref=cst, pre_toks=(t_tl, t_tsc_p[par]), tt_scales=True)
                    t_q2 = sV.tok()

                    @blk.gpsimd
                    def _(gq, g=g, par=par, t_q2=t_q2, td=t_dnall[par]):
                        W(gq, t_q2, td)
                        gq.dma_start(out=d_qtp[par][g * 256:(g + 1) * 256, :].rearrange("(k p) c -> p k c", p=128),
                                     in_=tqw[:].rearrange("p (k c) -> p k c", k=2)
                                     ).then_inc(dQs.sem, 16); dQs.n += 16
                    if g in gcut:
                        t_qt_g[gcut[g]] = dQs.tok()
                return t_qt_g

            def emit_qxT(ch):
                par = ch % 2
                for rt in range(RT):
                    for j in range(NDB):
                        @blk.tensor
                        def _(t, rt=rt, j=j, par=par, tq=t_qx_p[par]):
                            W(t, tq, sV.tok())
                            sT.bump(t.transpose(ptr2[:], qx[:, rt * DM + j * 128:rt * DM + j * 128 + 128], idn[:]))
                        t_tp = sT.tok()

                        @blk.vector
                        def _(v, rt=rt, j=j, par=par, t_tp=t_tp, tf=t_qxTf[par]):
                            W(v, t_tp, tf)
                            sV.bumpw(v, v.tensor_copy(out=qxT[par][:, j * LC + rt * 128:j * LC + rt * 128 + 128], in_=ptr2[:]))
                return sV.tok()

            def emit_gateup(ch, t_qxT):
                par = ch % 2
                t_pgfree, t_pufree = t_pgu_init
                for i in range(44):
                    ip = i % 2
                    dWp = dW0 if ip == 0 else dW1

                    @blk.sync
                    def _(sync, i=i, ip=ip, dWp=dWp, tc=t_qgu[ip]):
                        W(sync, tc)
                        sync.dma_start(out=qgs[ip][:],
                                       in_=d_QG[:, i * 128:(i + 1) * 128].rearrange("(j p) c -> p j c", p=128)
                                       ).then_inc(dWp.sem, 16); dWp.n += 16
                        sync.dma_start(out=qus[ip][:],
                                       in_=d_QU[:, i * 128:(i + 1) * 128].rearrange("(j p) c -> p j c", p=128)
                                       ).then_inc(dWp.sem, 16); dWp.n += 16
                    t_wg = dWp.tok()

                    @blk.tensor
                    def _(t, i=i, ip=ip, par=par, t_wg=t_wg, t_qxT=t_qxT, tpg=t_pgfree, tpu=t_pufree):
                        W(t, t_wg, t_qxT, tpg, tpu)
                        for j in range(NDB):
                            ig = t.matmul(pg[:], qgs[ip][:, j * 128:(j + 1) * 128],
                                          qxT[par][:, j * LC:(j + 1) * LC], start=(j == 0), stop=(j == NDB - 1))
                        sT.bump(ig)
                        for j in range(NDB):
                            iu = t.matmul(pu[:], qus[ip][:, j * 128:(j + 1) * 128],
                                          qxT[par][:, j * LC:(j + 1) * LC], start=(j == 0), stop=(j == NDB - 1))
                        sT.bump(iu)
                    t_guv = sT.tok()
                    t_qgu[ip] = t_guv

                    @blk.scalar
                    def _(s, i=i, t_guv=t_guv, tsp=dT0.tok(), tpu=t_pufree):
                        W(s, t_guv, tpu, tsp)
                        sA.bumpw(s, s.activation(out=t0t[:], in_=pg[:], func=AF.Silu, bias=0.0, scale=1.0))
                    t_sg2 = sA.tok()
                    t_pgfree = t_sg2

                    @blk.vector
                    def _(v, i=i, t_sg2=t_sg2):
                        W(v, t_sg2)
                        sV.bumpw(v, v.tensor_tensor(out=t0t[:], in0=t0t[:], in1=pu[:], op=ALU.mult))
                    t_mul = sV.tok()
                    t_pufree = t_mul

                    @blk.scalar
                    def _(s, i=i, t_mul=t_mul):
                        W(s, t_mul)
                        sA.bumpw(s, s.activation(out=tb[:], in_=t0t[:], func=AF.Abs, bias=0.0, scale=1.0))
                    t_ab = sA.tok()

                    @blk.vector
                    def _(v, i=i, t_ab=t_ab):
                        W(v, t_ab)
                        sV.bumpw(v, v.tensor_tensor(out=rmax_t[:], in0=rmax_t[:], in1=tb[:], op=ALU.max))

                    @blk.sync
                    def _(sync, i=i, par=par, t_mul=t_mul):
                        W(sync, t_mul)
                        sync.dma_start(out=d_t0p[par][i * 128:(i + 1) * 128, :], in_=t0t[:]).then_inc(dT0.sem, 16); dT0.n += 16
                t_t0all_p[par] = dT0.tok()
                t_qxTf[par] = sT.tok()
                t_pgu_init[0] = t_pgfree
                t_pgu_init[1] = t_pufree

            def emit_tree_scales(ch):
                par = ch % 2
                npar = 128
                t_round = sV.tok()
                while npar > 1:
                    half = npar // 2

                    @blk.sync
                    def _(sync, half=half, npar=npar, t_round=t_round):
                        W(sync, t_round)
                        sync.dma_start(out=tqf[0:half, 0:512], in_=rmax_t[half:npar, 0:512]
                                       ).then_inc(dBr.sem, 16); dBr.n += 16
                    t_mv = dBr.tok()

                    @blk.vector
                    def _(v, half=half, t_mv=t_mv):
                        W(v, t_mv)
                        sV.bumpw(v, v.tensor_tensor(out=rmax_t[0:half, 0:512], in0=rmax_t[0:half, 0:512],
                                                in1=tqf[0:half, 0:512], op=ALU.max))
                    t_round = sV.tok()
                    npar = half

                @blk.sync
                def _(sync, ch=ch, t_round=t_round):
                    W(sync, t_round)
                    sync.dma_start(out=d_tamax[ch:ch + 1, :], in_=rmax_t[0:1, 0:512]).then_inc(dSp.sem, 16); dSp.n += 16
                t_amsp = dSp.tok()

                @blk.sync
                def _(sync, ch=ch, par=par, t_amsp=t_amsp):
                    W(sync, t_amsp, sV.tok())
                    for kk in range(2):
                        sync.dma_start(out=sc_bc[par][:, kk * 512:(kk + 1) * 512],
                                       in_=d_tamax[ch:ch + 1, :].partition_broadcast(128)
                                       ).then_inc(dBr.sem, 16); dBr.n += 16
                t_ambc = dBr.tok()

                @blk.vector
                def _(v, par=par, t_ambc=t_ambc):
                    W(v, t_ambc)
                    sV.bumpw(v, v.tensor_scalar(out=dq_bc[par][:], in0=sc_bc[par][:], scalar1=1.0 / 448.0, scalar2=None, op0=ALU.mult))
                    sV.bumpw(v, v.reciprocal(out=sc_bc[par][:], in_=sc_bc[par][:]))
                    sV.bumpw(v, v.tensor_scalar(out=sc_bc[par][:], in0=sc_bc[par][:], scalar1=448.0, scalar2=None, op0=ALU.mult))
                t_tsc_p[par] = sV.tok()

            def emit_down(ch, t_qt_g):
                nonlocal t_evprev, t_outprev
                par = ch % 2
                t_dnld = sT.tok()
                for dmc in range(4):
                    t_dn_ev = t_evprev if dmc == 0 else sV.tok()
                    for ig in range(4):
                        @blk.sync
                        def _(sync, dmc=dmc, ig=ig, par=par, t_dnld=t_dnld):
                            W(sync, t_qt_g[ig], t_dnld)
                            sync.dma_start(out=qts[:, 0:11 * 512],
                                           in_=d_qtp[par][ig * 1408:(ig + 1) * 1408, :].rearrange("(b p) c -> p b c", p=128)
                                           ).then_inc(dW0.sem, 16); dW0.n += 16
                            sync.dma_start(out=wds[:, 0:11 * 512],
                                           in_=d_QD[ig * 1408:(ig + 1) * 1408, dmc * 512:(dmc + 1) * 512]
                                           .rearrange("(b p) c -> p b c", p=128)
                                           ).then_inc(dW0.sem, 16); dW0.n += 16
                        t_wd = dW0.tok()

                        @blk.tensor
                        def _(t, dmc=dmc, ig=ig, t_wd=t_wd, t_dn_ev=t_dn_ev):
                            W(t, t_wd, t_dn_ev)
                            for ib in range(11):
                                for rt in range(RT):
                                    ins = t.matmul(pdn[rt][:],
                                                   qts[:, ib * 512 + rt * 128:ib * 512 + rt * 128 + 128],
                                                   wds[:, ib * 512:(ib + 1) * 512],
                                                   start=(ig == 0 and ib == 0), stop=(ig == 3 and ib == 10))
                            sT.bump(ins)
                        t_dnld = sT.tok()
                    t_dn = sT.tok()

                    for rt in range(RT):
                        op = (dmc * RT + rt) % 2

                        @blk.vector
                        def _(v, dmc=dmc, rt=rt, par=par, op=op, t_dn=t_dn, tprev=ost_tok[op]):
                            W(v, t_dn, t_x2l_p[par], tprev, t_modT)
                            sV.bumpw(v, v.tensor_tensor(out=tqf[:, 0:512], in0=pdn[rt][:],
                                                    in1=gbT[:, dmc * 512:(dmc + 1) * 512], op=ALU.mult))
                            sV.bumpw(v, v.tensor_tensor(out=ost[op][:],
                                                    in0=tqf[:, 0:512],
                                                    in1=x2tc[par][:, rt * DM + dmc * 512:rt * DM + dmc * 512 + 512],
                                                    op=ALU.add))
                        t_pev = sV.tok()

                        @blk.sync
                        def _(sync, ch=ch, dmc=dmc, rt=rt, op=op, t_pev=t_pev):
                            W(sync, t_pev)
                            sync.dma_start(out=d_out[ch * LC + rt * 128:ch * LC + (rt + 1) * 128,
                                                     dmc * 512:(dmc + 1) * 512],
                                           in_=ost[op][:]).then_inc(dOu.sem, 16); dOu.n += 16
                        ost_tok[op] = dOu.tok()
                t_outprev = dOu.tok()
                t_evprev = sV.tok()
                t_x2free[par] = sV.tok()
                t_dnall[par] = dW0.tok()

            for ch in range(NCH):
                emit_ln2qx(ch)
                t_qxT = emit_qxT(ch)
                if ch > 0:
                    tqg_prev = emit_tquant(ch - 1)
                emit_gateup(ch, t_qxT)
                emit_tree_scales(ch)
                if ch > 0:
                    emit_down(ch - 1, tqg_prev)
            tqg_prev = emit_tquant(NCH - 1)
            emit_down(NCH - 1, tqg_prev)

            @blk.sync
            def _(sync):
                W(sync, dOu.tok(), sV.tok(), sA.tok(), sT.tok())

    return nc


def _vec_tiles(v, width=32):
    return np.ascontiguousarray(v.reshape(width, 128).T)


def kernel(**inputs):
    x = np.asarray(inputs["L0_attn_hidden_states"], np.float32)
    W_in = np.asarray(inputs["L0_attn_W_in"], np.float32)
    conv_w = np.asarray(inputs["L0_attn_conv_w"], np.float32)
    conv_b = np.asarray(inputs["L0_attn_conv_b"], np.float32)
    W_x = np.asarray(inputs["L0_attn_W_x"], np.float32)
    W_dt = np.asarray(inputs["L0_attn_W_dt"], np.float32)
    b_dt = np.asarray(inputs["L0_attn_b_dt"], np.float32)
    A_log = np.asarray(inputs["L0_attn_A_log"], np.float32)
    Dv = np.asarray(inputs["L0_attn_D"], np.float32)
    W_out = np.asarray(inputs["L0_attn_W_out"], np.float32)
    b_out = np.asarray(inputs["L0_attn_b_out"], np.float32)
    w_gate = np.asarray(inputs["L0_ffn_w_gate"], np.float32)
    w_up = np.asarray(inputs["L0_ffn_w_up"], np.float32)
    w_down = np.asarray(inputs["L0_ffn_w_down"], np.float32)
    up_idx = np.asarray(inputs["L0_ffn_up_reorder_index"])
    dn_idx = np.asarray(inputs["L0_ffn_down_reorder_index"])
    cvec = np.asarray(inputs["L0_c"], np.float32)
    wmod = np.asarray(inputs["L0_wmod"], np.float32)

    assert up_idx.shape == (DM,)
    pinv = np.argsort(dn_idx)
    w_down_p = w_down[:, pinv]

    shared = {
        "wmod": np.ascontiguousarray(wmod),
        "WinT": np.ascontiguousarray(W_in.T),
        "convw": np.ascontiguousarray(conv_w.reshape(32, 128, 4).transpose(1, 0, 2).reshape(128, 128)),
        "convb": _vec_tiles(conv_b),
        "bdt": _vec_tiles(b_dt),
        "Alog": np.ascontiguousarray(A_log.reshape(32, 128, 16).transpose(1, 0, 2).reshape(128, 512)),
        "Dt": _vec_tiles(Dv),
        "bout": _vec_tiles(b_out, width=16),
        "WxT": np.ascontiguousarray(W_x.T),
        "WdtT": np.ascontiguousarray(W_dt.T),
        "WoutT": np.ascontiguousarray(W_out.T),
    }
    wgT = np.ascontiguousarray(w_gate.T)
    wuT = np.ascontiguousarray(w_up.T)
    wdPT = np.ascontiguousarray(w_down_p.T)

    nc = build_program()
    in_maps = []
    for c in range(8):
        m = dict(shared)
        m["xT"] = np.ascontiguousarray(x[c].T)
        m["cvecT"] = _vec_tiles(cvec[c], width=16)
        m["wgS"] = np.ascontiguousarray(wgT[c * (DM // 8):(c + 1) * (DM // 8), :])
        m["wuS"] = np.ascontiguousarray(wuT[c * (DM // 8):(c + 1) * (DM // 8), :])
        m["wdS"] = np.ascontiguousarray(wdPT[c * (IFF // 8):(c + 1) * (IFF // 8), :])
        in_maps.append(m)

    res = run_bass_kernel_spmd(nc, in_maps, core_ids=list(range(8)))
    out = np.stack([np.asarray(r["out"], np.float32) for r in res.results], axis=0)
    return out


# version 18 (final)
# speedup vs baseline: 1.0558x; 1.0478x over previous
import contextlib

import numpy as np
import ml_dtypes

import concourse.bass as bass
import concourse.mybir as mybir
from concourse.bass_utils import run_bass_kernel_spmd

F32 = mybir.dt.float32
BF16 = mybir.dt.bfloat16
I32 = mybir.dt.int32
AF = mybir.ActivationFunctionType
ALU = mybir.AluOpType
AX = mybir.AxisListType

DM = 2048
DI = 4096
NST = 16
DTR = 128
L = 2048
LC = 512
NCH = L // LC
NDB = DM // 128
NIB = DI // 128
IFF = 5632
NIC = IFF // 512
RT = LC // 128
LN2C = float(np.log(2.0))
IVL2 = float(1.0 / np.log(2.0))
EPS = 1e-5


class Ctr:
    def __init__(self, sem, step=1):
        self.sem = sem
        self.n = 0
        self.step = step

    def bump(self, instr):
        instr.then_inc(self.sem, self.step)
        self.n += self.step
        return (self.sem, self.n)

    def bumpw(self, eng, instr):
        tk = self.bump(instr)
        eng.wait_ge(*tk)
        return tk

    def tok(self):
        return (self.sem, self.n)


def W(eng, *toks):
    for tk in toks:
        if tk is None:
            continue
        sem, v = tk
        if v > 0:
            eng.wait_ge(sem, v)


def emit_quant(blk, sV, sA, *, src, scale_ap, dq_ap, scr1, scr2, scr3, iscr,
               out_bf, cstref, pre_toks=(), scalar_scales=False):
    @blk.vector
    def _(v):
        W(v, *pre_toks)
        if scalar_scales:
            sV.bumpw(v, v.tensor_scalar(out=scr1, in0=src, scalar1=scale_ap, scalar2=None, op0=ALU.mult))
        else:
            sV.bumpw(v, v.tensor_tensor(out=scr1, in0=src, in1=scale_ap, op=ALU.mult))
    t1 = sV.tok()

    @blk.scalar
    def _(s):
        W(s, t1)
        sA.bumpw(s, s.activation(out=scr2, in_=scr1, func=AF.Abs, bias=0.0, scale=1.0))
        sA.bumpw(s, s.activation(out=scr2, in_=scr2, func=AF.Ln, bias=cstref[:, 4:5], scale=1.0))
    t2 = sA.tok()

    @blk.vector
    def _(v):
        W(v, t2)
        sV.bumpw(v, v.tensor_scalar(out=scr2, in0=scr2, scalar1=IVL2, scalar2=None,
                                    op0=ALU.mult))
        sV.bumpw(v, v.tensor_copy(out=iscr, in_=scr2))
        sV.bumpw(v, v.tensor_copy(out=scr3, in_=iscr))
        sV.bumpw(v, v.tensor_tensor(out=scr2, in0=scr2, in1=scr3, op=ALU.subtract))
        sV.bumpw(v, v.tensor_scalar(out=out_bf, in0=scr2, scalar1=0.0, scalar2=None, op0=ALU.is_lt))
        sV.bumpw(v, v.tensor_scalar(out=scr2, in0=scr2, scalar1=1.0, scalar2=None, op0=ALU.is_ge))
        sV.bumpw(v, v.tensor_tensor(out=scr3, in0=scr3, in1=out_bf, op=ALU.subtract))
        sV.bumpw(v, v.tensor_tensor(out=scr3, in0=scr3, in1=scr2, op=ALU.add))
        sV.bumpw(v, v.tensor_copy(out=iscr, in_=scr3))
        sV.bumpw(v, v.tensor_copy(out=scr3, in_=iscr))
        sV.bumpw(v, v.tensor_scalar(out=scr3, in0=scr3, scalar1=-3.0, scalar2=-9.0,
                                    op0=ALU.add, op1=ALU.max))
        sV.bumpw(v, v.tensor_scalar(out=scr3, in0=scr3, scalar1=LN2C, scalar2=None, op0=ALU.mult))
    t3 = sV.tok()

    @blk.scalar
    def _(s):
        W(s, t3)
        sA.bumpw(s, s.activation(out=out_bf, in_=scr3, func=AF.Exp, bias=0.0, scale=cstref[:, 2:3]))
        sA.bumpw(s, s.activation(out=scr3, in_=scr3, func=AF.Exp, bias=0.0, scale=1.0))
    t4 = sA.tok()

    @blk.vector
    def _(v):
        W(v, t4)
        sV.bumpw(v, v.tensor_tensor(out=scr1, in0=scr1, in1=out_bf, op=ALU.mult))
        sV.bumpw(v, v.tensor_copy(out=iscr, in_=scr1))
        sV.bumpw(v, v.tensor_copy(out=scr2, in_=iscr))
        sV.bumpw(v, v.tensor_tensor(out=scr1, in0=scr1, in1=scr2, op=ALU.subtract))
        sV.bumpw(v, v.tensor_scalar(out=out_bf, in0=scr1, scalar1=0.5, scalar2=None, op0=ALU.is_gt))
        sV.bumpw(v, v.tensor_scalar(out=scr1, in0=scr1, scalar1=-0.5, scalar2=None, op0=ALU.is_lt))
        sV.bumpw(v, v.tensor_tensor(out=scr2, in0=scr2, in1=out_bf, op=ALU.add))
        sV.bumpw(v, v.tensor_tensor(out=scr2, in0=scr2, in1=scr1, op=ALU.subtract))
        if scalar_scales:
            sV.bumpw(v, v.tensor_scalar(out=scr3, in0=scr3, scalar1=dq_ap, scalar2=None, op0=ALU.mult))
        else:
            sV.bumpw(v, v.tensor_tensor(out=scr3, in0=scr3, in1=dq_ap, op=ALU.mult))
        sV.bumpw(v, v.tensor_tensor(out=out_bf, in0=scr2, in1=scr3, op=ALU.mult))
    return sV.tok()


def emit_quant_pure(blk, sV, sA, *, src, scale_ap, dq_ap, scr1, scr2, scr3, scr4,
                    out_bf, cstref, pre_toks=()):
    @blk.vector
    def _(v):
        W(v, *pre_toks)
        sV.bumpw(v, v.tensor_scalar(out=scr1, in0=src, scalar1=scale_ap, scalar2=None, op0=ALU.mult))
    t1 = sV.tok()

    @blk.scalar
    def _(s):
        W(s, t1)
        sA.bumpw(s, s.activation(out=scr2, in_=scr1, func=AF.Abs, bias=0.0, scale=1.0))
        sA.bumpw(s, s.activation(out=scr2, in_=scr2, func=AF.Ln, bias=cstref[:, 4:5], scale=1.0))
    t2 = sA.tok()

    @blk.vector
    def _(v):
        W(v, t2)
        sV.bumpw(v, v.tensor_scalar(out=scr2, in0=scr2, scalar1=IVL2, scalar2=-6.99,
                                    op0=ALU.mult, op1=ALU.max))
        sV.bumpw(v, v.tensor_scalar(out=scr3, in0=scr2, scalar1=0.0, scalar2=-7.0,
                                    op0=ALU.mult, op1=ALU.add))
        for dl in (8.0, 4.0, 2.0, 1.0):
            sV.bumpw(v, v.scalar_tensor_tensor(out=scr4, in0=scr2, scalar=-dl, in1=scr3,
                                               op0=ALU.add, op1=ALU.is_ge))
            sV.bumpw(v, v.scalar_tensor_tensor(out=scr3, in0=scr4, scalar=dl, in1=scr3,
                                               op0=ALU.mult, op1=ALU.add))
        sV.bumpw(v, v.tensor_scalar(out=scr3, in0=scr3, scalar1=-3.0, scalar2=-9.0,
                                    op0=ALU.add, op1=ALU.max))
        sV.bumpw(v, v.tensor_scalar(out=scr3, in0=scr3, scalar1=LN2C, scalar2=None, op0=ALU.mult))
    t3 = sV.tok()

    @blk.scalar
    def _(s):
        W(s, t3)
        sA.bumpw(s, s.activation(out=out_bf, in_=scr3, func=AF.Exp, bias=0.0, scale=cstref[:, 2:3]))
        sA.bumpw(s, s.activation(out=scr3, in_=scr3, func=AF.Exp, bias=0.0, scale=1.0))
    t4 = sA.tok()

    @blk.vector
    def _(v):
        W(v, t4)
        sV.bumpw(v, v.tensor_tensor(out=scr1, in0=scr1, in1=out_bf, op=ALU.mult))
        sV.bumpw(v, v.tensor_scalar(out=scr2, in0=scr1, scalar1=0.0, scalar2=-16.0,
                                    op0=ALU.mult, op1=ALU.add))
        for dl in (16.0, 8.0, 4.0, 2.0, 1.0):
            sV.bumpw(v, v.scalar_tensor_tensor(out=scr4, in0=scr1, scalar=-(dl - 0.5), in1=scr2,
                                               op0=ALU.add, op1=ALU.is_ge))
            sV.bumpw(v, v.scalar_tensor_tensor(out=scr2, in0=scr4, scalar=dl, in1=scr2,
                                               op0=ALU.mult, op1=ALU.add))
        sV.bumpw(v, v.scalar_tensor_tensor(out=scr4, in0=scr1, scalar=-0.5, in1=scr2,
                                           op0=ALU.add, op1=ALU.is_ge))
        sV.bumpw(v, v.tensor_tensor(out=scr2, in0=scr2, in1=scr4, op=ALU.add))
        sV.bumpw(v, v.tensor_scalar(out=scr3, in0=scr3, scalar1=dq_ap, scalar2=None, op0=ALU.mult))
        sV.bumpw(v, v.tensor_tensor(out=out_bf, in0=scr2, in1=scr3, op=ALU.mult))
    return sV.tok()


def build_program():
    nc = bass.Bass("TRN2", target_bir_lowering=False, debug=False)

    d_xT = nc.dram_tensor("xT", [DM, L], F32, kind="ExternalInput").ap()
    d_cv = nc.dram_tensor("cvecT", [128, 16], F32, kind="ExternalInput").ap()
    d_wmod = nc.dram_tensor("wmod", [DM, 6 * DM], F32, kind="ExternalInput").ap()
    d_WinS = nc.dram_tensor("WinS", [DM // 8, 2 * DI], F32, kind="ExternalInput").ap()
    d_convw = nc.dram_tensor("convw", [128, 128], F32, kind="ExternalInput").ap()
    d_convb = nc.dram_tensor("convb", [128, 32], F32, kind="ExternalInput").ap()
    d_bdt = nc.dram_tensor("bdt", [128, 32], F32, kind="ExternalInput").ap()
    d_Alog = nc.dram_tensor("Alog", [128, 512], F32, kind="ExternalInput").ap()
    d_Dt = nc.dram_tensor("Dt", [128, 32], F32, kind="ExternalInput").ap()
    d_bout = nc.dram_tensor("bout", [128, 16], F32, kind="ExternalInput").ap()
    d_WxT = nc.dram_tensor("WxT", [DI, 160], F32, kind="ExternalInput").ap()
    d_WdtT = nc.dram_tensor("WdtT", [128, DI], F32, kind="ExternalInput").ap()
    d_WoutS = nc.dram_tensor("WoutS", [DI // 8, DM], F32, kind="ExternalInput").ap()
    d_wgS = nc.dram_tensor("wgS", [DM // 8, IFF], F32, kind="ExternalInput").ap()
    d_wuS = nc.dram_tensor("wuS", [DM // 8, IFF], F32, kind="ExternalInput").ap()
    d_wdS = nc.dram_tensor("wdS", [IFF // 8, DM], F32, kind="ExternalInput").ap()
    d_out = nc.dram_tensor("out", [L, DM], F32, kind="ExternalOutput").ap()

    t_WinB = nc.dram_tensor("WinB", [DM, 2 * DI], BF16, kind="Internal", addr_space="Shared"); d_WinB = t_WinB.ap()
    t_WoutB = nc.dram_tensor("WoutB", [DI, DM], BF16, kind="Internal", addr_space="Shared"); d_WoutB = t_WoutB.ap()
    t_wiC = nc.dram_tensor("wiC", [DM // 8, 2 * DI], BF16, kind="Internal"); d_wiC = t_wiC.ap()
    t_woC = nc.dram_tensor("woC", [DI // 8, DM], BF16, kind="Internal"); d_woC = t_woC.ap()
    d_WxB = nc.dram_tensor("WxB_d", [128, 32 * 160], BF16, kind="Internal").ap()
    d_WdtB = nc.dram_tensor("WdtB_d", [128, DI], BF16, kind="Internal").ap()
    t_QG = nc.dram_tensor("QG", [DM, IFF], BF16, kind="Internal", addr_space="Shared"); d_QG = t_QG.ap()
    t_QU = nc.dram_tensor("QU", [DM, IFF], BF16, kind="Internal", addr_space="Shared"); d_QU = t_QU.ap()
    t_QD = nc.dram_tensor("QD", [IFF, DM], BF16, kind="Internal", addr_space="Shared"); d_QD = t_QD.ap()
    t_qgS = nc.dram_tensor("qgS", [DM // 8, IFF], BF16, kind="Internal"); d_qgS = t_qgS.ap()
    t_quS = nc.dram_tensor("quS", [DM // 8, IFF], BF16, kind="Internal"); d_quS = t_quS.ap()
    t_qdS = nc.dram_tensor("qdS", [IFF // 8, DM], BF16, kind="Internal"); d_qdS = t_qdS.ap()
    d_modrow = nc.dram_tensor("modrow", [1, 6 * DM], F32, kind="Internal").ap()
    t_amaxg = nc.dram_tensor("amaxg", [1, IFF], F32, kind="Internal", addr_space="Shared"); d_amaxg = t_amaxg.ap()
    t_amaxu = nc.dram_tensor("amaxu", [1, IFF], F32, kind="Internal", addr_space="Shared"); d_amaxu = t_amaxu.ap()
    t_amaxd = nc.dram_tensor("amaxd", [1, DM], F32, kind="Internal", addr_space="Shared"); d_amaxd = t_amaxd.ap()
    t_amgP = nc.dram_tensor("amgP", [1, IFF], F32, kind="Internal"); d_amgP = t_amgP.ap()
    t_amuP = nc.dram_tensor("amuP", [1, IFF], F32, kind="Internal"); d_amuP = t_amuP.ap()
    t_amdP = nc.dram_tensor("amdP", [1, DM], F32, kind="Internal"); d_amdP = t_amdP.ap()
    d_srow = nc.dram_tensor("srow", [NCH, 2 * LC], F32, kind="Internal").ap()
    d_bc = nc.dram_tensor("bcrows", [NCH, 32 * LC], BF16, kind="Internal").ap()
    d_x2 = nc.dram_tensor("x2sp", [L, DM], BF16, kind="Internal").ap()
    d_t0 = nc.dram_tensor("t0sp", [LC, IFF], BF16, kind="Internal").ap()
    d_qt = nc.dram_tensor("qtsp", [IFF, LC], BF16, kind="Internal").ap()

    ident = nc.inline_tensor(np.eye(128, dtype=ml_dtypes.bfloat16), name="identbf")

    es0 = contextlib.ExitStack()
    with es0:
        def SEM(name, step=1):
            return Ctr(es0.enter_context(nc.semaphore(name)), step)

        sV = SEM("sV"); sA = SEM("sA"); sT = SEM("sT"); sG = SEM("sG")
        dIn = SEM("dIn", 16); dW0 = SEM("dW0", 16); dW1 = SEM("dW1", 16)
        dCn = SEM("dCn", 16); dSp = SEM("dSp", 16); dBr = SEM("dBr", 16)
        dT0 = SEM("dT0", 16); dQt = SEM("dQt", 16); dOu = SEM("dOu", 16)
        ccGw = es0.enter_context(nc.semaphore("ccGw"))
        ccGo = es0.enter_context(nc.semaphore("ccGo"))
        ccRg = es0.enter_context(nc.semaphore("ccRg"))
        ccRu = es0.enter_context(nc.semaphore("ccRu"))
        ccRd = es0.enter_context(nc.semaphore("ccRd"))
        ccGg = es0.enter_context(nc.semaphore("ccGg"))
        ccGu = es0.enter_context(nc.semaphore("ccGu"))
        ccGd = es0.enter_context(nc.semaphore("ccGd"))

        def P0(name, shape, dt):
            return es0.enter_context(nc.sbuf_tensor(name, shape, dt))

        idn = P0("idn", [128, 128], BF16)
        convw = P0("convw_s", [128, 128], F32)
        convb = P0("convb_s", [128, 32], F32)
        bdt = P0("bdt_s", [128, 32], F32)
        Aneg = P0("Aneg", [128, 512], F32)
        Dt = P0("Dt_s", [128, 32], F32)
        bout = P0("bout_s", [128, 16], F32)
        ca1 = P0("ca1", [128, 16], F32)
        sa_t = P0("sa_t", [128, 16], F32)
        gaf = P0("gaf", [128, 16], F32)
        gab = P0("gab", [128, 16], F32)
        cst = P0("cst", [128, 8], F32)

        esA = contextlib.ExitStack()
        with esA:
            def SB(name, shape, dt):
                return esA.enter_context(nc.sbuf_tensor(name, shape, dt))

            wbuf = SB("wbufA", [128, 8192], F32)
            wb2 = SB("wb2A", [128, IFF], F32)
            qsl = SB("qslA", [128, IFF], F32)
            qs3 = SB("qs3A", [128, IFF], F32)
            qint = SB("qintA", [128, IFF], I32)
            qout = SB("qoutA", [128, 8192], BF16)
            sc_b = SB("scbA", [128, IFF], BF16)
            dq_b = SB("dqbA", [128, IFF], BF16)
            siluc = SB("siluc", [128, 16], BF16)
            mrow = SB("mrowA", [1, 512], F32)
            ps1 = esA.enter_context(nc.psum_tensor("psA1", [1, 512], F32))

            blk = esA.enter_context(nc.Block())

            @blk.sync
            def _(sync):
                sync.dma_start(out=convw[:], in_=d_convw).then_inc(dCn.sem, 16); dCn.n += 16
                sync.dma_start(out=convb[:], in_=d_convb).then_inc(dCn.sem, 16); dCn.n += 16
                sync.dma_start(out=bdt[:], in_=d_bdt).then_inc(dCn.sem, 16); dCn.n += 16
                sync.dma_start(out=Aneg[:], in_=d_Alog).then_inc(dCn.sem, 16); dCn.n += 16
                sync.dma_start(out=Dt[:], in_=d_Dt).then_inc(dCn.sem, 16); dCn.n += 16
                sync.dma_start(out=bout[:], in_=d_bout).then_inc(dCn.sem, 16); dCn.n += 16
                sync.dma_start(out=idn[:], in_=ident.ap()).then_inc(dCn.sem, 16); dCn.n += 16
                sync.dma_start(out=wbuf[:, 0:16], in_=d_cv).then_inc(dCn.sem, 16); dCn.n += 16
            t_cn = dCn.tok()

            @blk.vector
            def _(v):
                for col, val in ((0, EPS), (1, -0.5), (2, -1.0), (3, 1.0), (4, 1e-38), (5, 0.5)):
                    sV.bumpw(v, v.memset(cst[:, col:col + 1], val))
            t_cst = sV.tok()

            @blk.scalar
            def _(s):
                W(s, t_cn, t_cst)
                sA.bumpw(s, s.activation(out=Aneg[:], in_=Aneg[:], func=AF.Exp, bias=0.0, scale=1.0))
                sA.bumpw(s, s.activation(out=siluc[:], in_=wbuf[:, 0:16], func=AF.Silu, bias=0.0, scale=1.0))
            t_acts = sA.tok()

            @blk.vector
            def _(v):
                W(v, t_acts)
                sV.bumpw(v, v.tensor_scalar(out=Aneg[:], in0=Aneg[:], scalar1=-1.0, scalar2=None, op0=ALU.mult))

            @blk.sync
            def _(sync):
                W(sync, t_acts)
                sync.dma_start(out=wbuf[:, 0:5120],
                               in_=d_WxT.rearrange("(j p) r -> p j r", p=128)).then_inc(dCn.sem, 16); dCn.n += 16
                sync.dma_start(out=qsl[:, 0:DI], in_=d_WdtT).then_inc(dCn.sem, 16); dCn.n += 16
            t_wx = dCn.tok()

            @blk.vector
            def _(v):
                W(v, t_wx)
                sV.bumpw(v, v.tensor_copy(out=qout[:, 0:5120], in_=wbuf[:, 0:5120]))
                sV.bumpw(v, v.tensor_copy(out=qout[:, 5120:5120 + 2048], in_=qsl[:, 0:2048]))
                sV.bumpw(v, v.tensor_copy(out=sc_b[:, 0:2048], in_=qsl[:, 2048:DI]))
            t_wxc = sV.tok()

            @blk.sync
            def _(sync):
                W(sync, t_wxc)
                sync.dma_start(out=d_WxB, in_=qout[:, 0:5120]).then_inc(dSp.sem, 16); dSp.n += 16
                sync.dma_start(out=d_WdtB[:, 0:2048], in_=qout[:, 5120:5120 + 2048]).then_inc(dSp.sem, 16); dSp.n += 16
                sync.dma_start(out=d_WdtB[:, 2048:DI], in_=sc_b[:, 0:2048]).then_inc(dSp.sem, 16); dSp.n += 16

            for mg in range(24):
                @blk.sync
                def _(sync, mg=mg):
                    W(sync, sV.tok())
                    sync.dma_start(out=wbuf[:],
                                   in_=d_wmod[:, mg * 512:(mg + 1) * 512].rearrange("(j p) c -> p j c", p=128)
                                   ).then_inc(dCn.sem, 16); dCn.n += 16
                t_l = dCn.tok()

                @blk.vector
                def _(v, t_l=t_l):
                    W(v, t_l, sT.tok(), dSp.tok())
                    sV.bumpw(v, v.tensor_copy(out=qout[:], in_=wbuf[:]))
                t_c = sV.tok()

                @blk.tensor
                def _(t, t_c=t_c):
                    W(t, t_c, t_acts, sV.tok())
                    for kb in range(16):
                        ins = t.matmul(ps1[:], siluc[:, kb:kb + 1], qout[:, kb * 512:(kb + 1) * 512],
                                       start=(kb == 0), stop=(kb == 15))
                    sT.bump(ins)
                t_m = sT.tok()

                @blk.vector
                def _(v, t_m=t_m):
                    W(v, t_m, dSp.tok())
                    sV.bumpw(v, v.tensor_copy(out=mrow[:], in_=ps1[:]))
                t_e = sV.tok()

                @blk.sync
                def _(sync, mg=mg, t_e=t_e):
                    W(sync, t_e)
                    sync.dma_start(out=d_modrow[0:1, mg * 512:(mg + 1) * 512], in_=mrow[:]
                                   ).then_inc(dSp.sem, 16); dSp.n += 16
            t_modsp = dSp.tok()

            @blk.sync
            def _(sync):
                W(sync, t_modsp)
                with nc.allow_non_contiguous_dma(reason="tiny one-time per-partition mod vectors"):
                    for dst, off in ((sa_t, 0), (ca1, DM), (gaf, 2 * DM)):
                        sync.dma_start(out=dst[:], in_=d_modrow[0:1, off:off + DM].rearrange("a (j p) -> p a j", p=128)
                                       ).then_inc(dBr.sem, 16); dBr.n += 16
            t_br1 = dBr.tok()

            @blk.vector
            def _(v):
                W(v, t_br1)
                sV.bumpw(v, v.tensor_scalar(out=ca1[:], in0=ca1[:], scalar1=1.0, scalar2=None, op0=ALU.add))
                sV.bumpw(v, v.tensor_scalar(out=wb2[:, 0:16], in0=gaf[:], scalar1=0.0, scalar2=None, op0=ALU.is_ge))
                sV.bumpw(v, v.tensor_scalar(out=wb2[:, 0:16], in0=wb2[:, 0:16], scalar1=2.0, scalar2=-1.0,
                                        op0=ALU.mult, op1=ALU.add))
                sV.bumpw(v, v.tensor_scalar(out=wb2[:, 16:32], in0=gaf[:], scalar1=-1.0, scalar2=None, op0=ALU.mult))
                sV.bumpw(v, v.tensor_tensor(out=wb2[:, 16:32], in0=gaf[:], in1=wb2[:, 16:32], op=ALU.max))
                sV.bumpw(v, v.tensor_scalar(out=wb2[:, 16:32], in0=wb2[:, 16:32], scalar1=0.25, scalar2=None, op0=ALU.max))
                sV.bumpw(v, v.tensor_tensor(out=gaf[:], in0=wb2[:, 0:16], in1=wb2[:, 16:32], op=ALU.mult))
                sV.bumpw(v, v.tensor_tensor(out=gab[:], in0=gaf[:], in1=bout[:], op=ALU.mult))

            def cast_weight(dsrc, ddst, ncols):
                nslab = dsrc.shape[0] // 128
                for slx in range(nslab):
                    @blk.sync
                    def _(sync, slx=slx):
                        W(sync, dSp.tok())
                        sync.dma_start(out=wbuf[:, 0:ncols], in_=dsrc[slx * 128:(slx + 1) * 128, :]
                                       ).then_inc(dW0.sem, 16); dW0.n += 16
                    t_ld = dW0.tok()

                    @blk.vector
                    def _(v, t_ld=t_ld):
                        W(v, t_ld)
                        sV.bumpw(v, v.tensor_copy(out=qout[:, 0:ncols], in_=wbuf[:, 0:ncols]))
                    t_ca = sV.tok()

                    @blk.sync
                    def _(sync, slx=slx, t_ca=t_ca):
                        W(sync, t_ca)
                        sync.dma_start(out=ddst[slx * 128:(slx + 1) * 128, :], in_=qout[:, 0:ncols]
                                       ).then_inc(dSp.sem, 16); dSp.n += 16

            def cast_shard(dsrc_sh, d_bnc, t_bnc, t_full, nrows, ncols, ccX):
                for slx in range(nrows // 128):
                    @blk.sync
                    def _(sync, slx=slx):
                        W(sync, dSp.tok(), sV.tok())
                        sync.dma_start(out=wbuf[:, 0:ncols], in_=dsrc_sh[slx * 128:(slx + 1) * 128, :]
                                       ).then_inc(dW0.sem, 16); dW0.n += 16
                    t_ld = dW0.tok()

                    @blk.vector
                    def _(v, t_ld=t_ld):
                        W(v, t_ld)
                        sV.bumpw(v, v.tensor_copy(out=qout[:, 0:ncols], in_=wbuf[:, 0:ncols]))
                    t_ca = sV.tok()

                    @blk.sync
                    def _(sync, slx=slx, t_ca=t_ca):
                        W(sync, t_ca)
                        sync.dma_start(out=d_bnc[slx * 128:(slx + 1) * 128, :], in_=qout[:, 0:ncols]
                                       ).then_inc(dSp.sem, 16); dSp.n += 16
                t_csp = dSp.tok()

                @blk.gpsimd
                def _(g, t_csp=t_csp):
                    W(g, t_csp)
                    g.collective_compute("AllGather", ALU.bypass, replica_groups=RG8,
                                         ins=[t_bnc.ap().opt()], outs=[t_full.ap().opt()]).then_inc(ccX)

            RG8 = [list(range(8))]

            RG8 = [list(range(8))]

            def shard_slabs(nrows):
                out, r = [], 0
                while r < nrows:
                    p = min(128, nrows - r)
                    out.append((r, p))
                    r += p
                return out

            def shard_amax(dsrc_sh, nrows, ncols, d_amP, t_amP, t_amax, ccR):
                for slx, (r0, p) in enumerate(shard_slabs(nrows)):
                    @blk.sync
                    def _(sync, r0=r0, p=p):
                        W(sync, sV.tok(), sA.tok())
                        sync.dma_start(out=wbuf[0:p, 0:ncols], in_=dsrc_sh[r0:r0 + p, :]
                                       ).then_inc(dW0.sem, 16); dW0.n += 16
                    t_ld = dW0.tok()

                    @blk.scalar
                    def _(s, t_ld=t_ld, p=p):
                        W(s, t_ld, sV.tok())
                        sA.bumpw(s, s.activation(out=wb2[0:p, 0:ncols], in_=wbuf[0:p, 0:ncols], func=AF.Abs,
                                             bias=0.0, scale=1.0))
                    t_ab = sA.tok()

                    @blk.vector
                    def _(v, slx=slx, p=p, t_ab=t_ab):
                        W(v, t_ab)
                        if slx == 0:
                            sV.bumpw(v, v.tensor_copy(out=qsl[0:p, 0:ncols], in_=wb2[0:p, 0:ncols]))
                        else:
                            sV.bumpw(v, v.tensor_tensor(out=qsl[0:p, 0:ncols], in0=qsl[0:p, 0:ncols],
                                                    in1=wb2[0:p, 0:ncols], op=ALU.max))
                npar = min(128, nrows)
                while npar > 1:
                    half = (npar + 1) // 2

                    @blk.sync
                    def _(sync, half=half, npar=npar):
                        W(sync, sV.tok())
                        sync.dma_start(out=wb2[0:npar - half, 0:ncols], in_=qsl[half:npar, 0:ncols]
                                       ).then_inc(dBr.sem, 16); dBr.n += 16
                    t_mv = dBr.tok()

                    @blk.vector
                    def _(v, half=half, npar=npar, t_mv=t_mv):
                        W(v, t_mv)
                        sV.bumpw(v, v.tensor_tensor(out=qsl[0:npar - half, 0:ncols], in0=qsl[0:npar - half, 0:ncols],
                                                in1=wb2[0:npar - half, 0:ncols], op=ALU.max))
                    npar = half

                @blk.sync
                def _(sync):
                    W(sync, sV.tok())
                    sync.dma_start(out=d_amP[0:1, 0:ncols], in_=qsl[0:1, 0:ncols]).then_inc(dSp.sem, 16); dSp.n += 16
                t_am = dSp.tok()

                @blk.gpsimd
                def _(g, t_am=t_am):
                    W(g, t_am)
                    g.collective_compute("AllReduce", ALU.max, replica_groups=RG8,
                                         ins=[t_amP.ap().opt()], outs=[t_amax.ap().opt()]).then_inc(ccR)

            def quant_shard(dsrc_sh, d_qsh, t_qsh, t_QX, nrows, ncols, d_amax, ccR, ccG):
                @blk.sync
                def _(sync):
                    sync.wait_ge(ccR, 1)
                    W(sync, sV.tok(), sA.tok())
                    sync.dma_start(out=wb2[:, 0:ncols],
                                   in_=d_amax[0:1, 0:ncols].partition_broadcast(128)
                                   ).then_inc(dBr.sem, 16); dBr.n += 16
                t_bc = dBr.tok()

                @blk.vector
                def _(v, t_bc=t_bc):
                    W(v, t_bc)
                    sV.bumpw(v, v.tensor_scalar(out=dq_b[:, 0:ncols], in0=wb2[:, 0:ncols],
                                            scalar1=1.0 / 448.0, scalar2=None, op0=ALU.mult))
                    sV.bumpw(v, v.reciprocal(out=wb2[:, 0:ncols], in_=wb2[:, 0:ncols]))
                    sV.bumpw(v, v.tensor_scalar(out=sc_b[:, 0:ncols], in0=wb2[:, 0:ncols],
                                            scalar1=448.0, scalar2=None, op0=ALU.mult))
                t_sc = sV.tok()

                for r0, p in shard_slabs(nrows):
                    @blk.sync
                    def _(sync, r0=r0, p=p):
                        W(sync, sV.tok(), dSp.tok())
                        sync.dma_start(out=wbuf[0:p, 0:ncols], in_=dsrc_sh[r0:r0 + p, :]
                                       ).then_inc(dW1.sem, 16); dW1.n += 16
                    t_ld = dW1.tok()
                    emit_quant(blk, sV, sA,
                               src=wbuf[0:p, 0:ncols], scale_ap=sc_b[0:p, 0:ncols], dq_ap=dq_b[0:p, 0:ncols],
                               scr1=wb2[0:p, 0:ncols], scr2=qsl[0:p, 0:ncols], scr3=qs3[0:p, 0:ncols],
                               iscr=qint[0:p, 0:ncols],
                               out_bf=qout[0:p, 0:ncols], cstref=cst[0:p, :], pre_toks=(t_ld, t_sc))
                    t_q = sV.tok()

                    @blk.sync
                    def _(sync, r0=r0, p=p, t_q=t_q):
                        W(sync, t_q)
                        sync.dma_start(out=d_qsh[r0:r0 + p, :], in_=qout[0:p, 0:ncols]
                                       ).then_inc(dSp.sem, 16); dSp.n += 16
                t_spq = dSp.tok()

                @blk.gpsimd
                def _(g, t_spq=t_spq):
                    W(g, t_spq)
                    g.collective_compute("AllGather", ALU.bypass, replica_groups=RG8,
                                         ins=[t_qsh.ap().opt()], outs=[t_QX.ap().opt()]).then_inc(ccG)

            shard_amax(d_wgS, DM // 8, IFF, d_amgP, t_amgP, t_amaxg, ccRg)
            shard_amax(d_wuS, DM // 8, IFF, d_amuP, t_amuP, t_amaxu, ccRu)
            shard_amax(d_wdS, IFF // 8, DM, d_amdP, t_amdP, t_amaxd, ccRd)
            cast_shard(d_WinS, d_wiC, t_wiC, t_WinB, DM // 8, 2 * DI, ccGw)
            cast_shard(d_WoutS, d_woC, t_woC, t_WoutB, DI // 8, DM, ccGo)
            quant_shard(d_wgS, d_qgS, t_qgS, t_QG, DM // 8, IFF, d_amaxg, ccRg, ccGg)
            quant_shard(d_wuS, d_quS, t_quS, t_QU, DM // 8, IFF, d_amaxu, ccRu, ccGu)
            quant_shard(d_wdS, d_qdS, t_qdS, t_QD, IFF // 8, DM, d_amaxd, ccRd, ccGd)

            @blk.sync
            def _(sync):
                W(sync, dSp.tok(), sV.tok(), sA.tok(), sT.tok(), dBr.tok())

        esB = contextlib.ExitStack()
        with esB:
            def SB(name, shape, dt):
                return esB.enter_context(nc.sbuf_tensor(name, shape, dt))

            xs = SB("xs", [128, NDB * LC], BF16)
            bufA = SB("bufA", [128, NST * LC], BF16)
            bufB = SB("bufB", [128, NST * LC], BF16)
            bufH = SB("bufH", [128, NST * LC], BF16)
            wsl = [SB("wsl0", [128, 2048], BF16), SB("wsl1", [128, 2048], BF16)]
            xcv = SB("xcv0", [128, LC + 3], BF16)
            xc = SB("xc", [128, NIB * LC], BF16)
            sz = SB("szb", [128, NIB * LC], BF16)
            Bb = SB("Bb", [128, NST * LC], BF16)
            Cb = SB("Cb", [128, NST * LC], BF16)
            dtl = SB("dtl", [128, LC], BF16)
            dlt = SB("dlt", [128, LC], F32)
            dl2 = SB("dl2", [128, LC], F32)
            rs2 = SB("rs2", [128, LC], F32)
            du = SB("du", [128, LC], BF16)
            r16 = SB("r16", [128, 16], F32)
            t16 = SB("t16", [128, 16], F32)
            carry = SB("carry", [128, NIB * NST], F32)
            halo = SB("halo", [128, NIB * 3], BF16)
            mrb = SB("mrb", [128, LC], F32)
            rsb = SB("rsb", [128, LC], F32)
            bcsb = SB("bcsb", [32, LC], BF16)
            onesb = SB("onesb", [128, 1], BF16)
            xstg = SB("xstg", [128, 512], F32)
            WxB = SB("WxB", [128, 32 * 160], BF16)
            WdtB = SB("WdtB", [128, DI], BF16)

            pz0 = esB.enter_context(nc.psum_tensor("pz0", [128, 512], F32))
            pz1 = esB.enter_context(nc.psum_tensor("pz1", [128, 512], F32))
            pd1 = esB.enter_context(nc.psum_tensor("pd1", [128, 512], F32))
            pd2 = esB.enter_context(nc.psum_tensor("pd2", [32, 512], F32))
            pss = esB.enter_context(nc.psum_tensor("pss", [1, 512], F32))
            psq = esB.enter_context(nc.psum_tensor("psq", [1, 512], F32))
            ptr = esB.enter_context(nc.psum_tensor("ptr", [128, 128], BF16))

            blk = esB.enter_context(nc.Block())

            @blk.vector
            def _(v):
                sV.bumpw(v, v.memset(onesb[:], 1.0))
                sV.bumpw(v, v.memset(carry[:], 0.0))
                sV.bumpw(v, v.memset(halo[:], 0.0))
            t_init = sV.tok()

            @blk.sync
            def _(sync):
                sync.wait_ge(ccGw, 1)
                sync.wait_ge(ccGo, 1)
                sync.dma_start(out=WxB[:], in_=d_WxB).then_inc(dCn.sem, 16); dCn.n += 16
                sync.dma_start(out=WdtB[:], in_=d_WdtB).then_inc(dCn.sem, 16); dCn.n += 16
            t_wres = dCn.tok()

            for ch in range(NCH):
                for grp in range(16):
                    @blk.sync
                    def _(sync, ch=ch, grp=grp):
                        W(sync, sV.tok())
                        sync.dma_start(out=xstg[:],
                                       in_=d_xT[grp * 128:(grp + 1) * 128, ch * LC:(ch + 1) * LC]
                                       ).then_inc(dIn.sem, 16); dIn.n += 16
                    t_x = dIn.tok()

                    @blk.vector
                    def _(v, grp=grp, t_x=t_x):
                        W(v, t_x)
                        sV.bumpw(v, v.tensor_copy(out=xs[:, grp * LC:(grp + 1) * LC], in_=xstg[:]))
                t_xs = sV.tok()

                @blk.scalar
                def _(s):
                    W(s, t_xs)
                    sA.bumpw(s, s.activation(out=bufB[:], in_=xs[:], func=AF.Square, bias=0.0, scale=1.0))
                t_sq = sA.tok()

                @blk.tensor
                def _(t):
                    W(t, t_xs, t_sq, t_init, sV.tok())
                    for j in range(NDB):
                        i1 = t.matmul(pss[:], onesb[:], xs[:, j * LC:(j + 1) * LC],
                                      start=(j == 0), stop=(j == NDB - 1))
                    sT.bump(i1)
                    for j in range(NDB):
                        i2 = t.matmul(psq[:], onesb[:], bufB[:, j * LC:(j + 1) * LC],
                                      start=(j == 0), stop=(j == NDB - 1))
                    sT.bump(i2)
                t_st = sT.tok()

                @blk.vector
                def _(v):
                    W(v, t_st, dSp.tok())
                    sV.bumpw(v, v.tensor_scalar(out=rsb[0:1, 0:LC], in0=pss[:], scalar1=1.0 / DM, scalar2=None, op0=ALU.mult))
                    sV.bumpw(v, v.tensor_scalar(out=mrb[0:1, 0:LC], in0=psq[:], scalar1=1.0 / DM, scalar2=None, op0=ALU.mult))
                    sV.bumpw(v, v.tensor_tensor(out=dlt[0:1, 0:LC], in0=rsb[0:1, 0:LC], in1=rsb[0:1, 0:LC], op=ALU.mult))
                    sV.bumpw(v, v.tensor_tensor(out=mrb[0:1, 0:LC], in0=mrb[0:1, 0:LC], in1=dlt[0:1, 0:LC], op=ALU.subtract))
                t_va = sV.tok()

                @blk.scalar
                def _(s):
                    W(s, t_va)
                    sA.bumpw(s, s.activation(out=mrb[0:1, 0:LC], in_=mrb[0:1, 0:LC], func=AF.Ln,
                                         bias=cst[0:1, 0:1], scale=1.0))
                    sA.bumpw(s, s.activation(out=mrb[0:1, 0:LC], in_=mrb[0:1, 0:LC], func=AF.Exp,
                                         bias=0.0, scale=cst[0:1, 1:2]))
                t_rs = sA.tok()

                @blk.vector
                def _(v):
                    W(v, t_rs)
                    sV.bumpw(v, v.tensor_tensor(out=dlt[0:1, 0:LC], in0=rsb[0:1, 0:LC], in1=mrb[0:1, 0:LC], op=ALU.mult))
                t_mr = sV.tok()

                @blk.sync
                def _(sync, ch=ch, t_mr=t_mr):
                    W(sync, t_mr)
                    sync.dma_start(out=d_srow[ch:ch + 1, 0:LC], in_=dlt[0:1, 0:LC]).then_inc(dSp.sem, 16); dSp.n += 16
                    sync.dma_start(out=d_srow[ch:ch + 1, LC:2 * LC], in_=mrb[0:1, 0:LC]).then_inc(dSp.sem, 16); dSp.n += 16
                t_ssp = dSp.tok()

                @blk.sync
                def _(sync, ch=ch, t_ssp=t_ssp):
                    W(sync, t_ssp, sV.tok())
                    sync.dma_start(out=mrb[:], in_=d_srow[ch:ch + 1, 0:LC].partition_broadcast(128)
                                   ).then_inc(dBr.sem, 16); dBr.n += 16
                    sync.dma_start(out=rsb[:], in_=d_srow[ch:ch + 1, LC:2 * LC].partition_broadcast(128)
                                   ).then_inc(dBr.sem, 16); dBr.n += 16
                t_bcs = dBr.tok()

                for j in range(NDB):
                    @blk.vector
                    def _(v, j=j):
                        W(v, t_bcs)
                        sV.bumpw(v, v.tensor_tensor(out=dlt[:], in0=xs[:, j * LC:(j + 1) * LC], in1=rsb[:], op=ALU.mult))
                        sV.bumpw(v, v.tensor_tensor(out=dlt[:], in0=dlt[:], in1=mrb[:], op=ALU.subtract))
                        sV.bumpw(v, v.tensor_scalar(out=bufH[:, j * LC:(j + 1) * LC], in0=dlt[:],
                                                scalar1=ca1[:, j:j + 1], scalar2=sa_t[:, j:j + 1],
                                                op0=ALU.mult, op1=ALU.add))
                t_h = sV.tok()

                t_slab_cons = [sT.tok(), sT.tok()]
                t_bankfree = [sV.tok(), sV.tok()]
                t_mrbfree = sA.tok()
                for te in range(64):
                    par = te % 2
                    dWp = dW0 if par == 0 else dW1
                    pz = pz0 if te % 2 == 0 else pz1

                    @blk.sync
                    def _(sync, te=te, par=par, dWp=dWp, tc=t_slab_cons[par]):
                        W(sync, tc)
                        sync.dma_start(out=wsl[par][:],
                                       in_=d_WinB[:, te * 128:(te + 1) * 128].rearrange("(j p) c -> p j c", p=128)
                                       ).then_inc(dWp.sem, 16); dWp.n += 16
                    t_w = dWp.tok()

                    if True:
                        @blk.tensor
                        def _(t, par=par, t_w=t_w, tf=t_bankfree[te % 2], pz=pz):
                            W(t, t_w, tf, t_h)
                            for j in range(NDB):
                                ins = t.matmul(pz[:], wsl[par][:, j * 128:(j + 1) * 128],
                                               bufH[:, j * LC:(j + 1) * LC],
                                               start=(j == 0), stop=(j == NDB - 1))
                            sT.bump(ins)
                        t_mm = sT.tok()
                        t_slab_cons[par] = t_mm

                        if te < NIB:
                            @blk.vector
                            def _(v, te=te, t_mm=t_mm, pz=pz, tmf=t_mrbfree):
                                W(v, t_mm, tmf)
                                sV.bumpw(v, v.tensor_copy(out=xcv[:, 3:3 + LC], in_=pz[:]))
                                sV.bumpw(v, v.tensor_copy(out=xcv[:, 0:3], in_=halo[:, te * 3:te * 3 + 3]))
                                sV.bumpw(v, v.tensor_copy(out=halo[:, te * 3:te * 3 + 3], in_=xcv[:, LC:LC + 3]))
                                sV.bumpw(v, v.tensor_scalar(out=mrb[:], in0=xcv[:, 0:LC],
                                                        scalar1=convw[:, te * 4:te * 4 + 1], scalar2=None, op0=ALU.mult))
                                for k in range(1, 4):
                                    sV.bumpw(v, v.scalar_tensor_tensor(out=mrb[:], in0=xcv[:, k:k + LC],
                                                                   scalar=convw[:, te * 4 + k:te * 4 + k + 1],
                                                                   in1=mrb[:], op0=ALU.mult, op1=ALU.add))
                            t_cv = sV.tok()
                            t_bankfree[te % 2] = t_cv

                            @blk.scalar
                            def _(s, te=te, t_cv=t_cv):
                                W(s, t_cv)
                                sA.bumpw(s, s.activation(out=xc[:, te * LC:(te + 1) * LC], in_=mrb[:], func=AF.Silu,
                                                     bias=convb[:, te:te + 1], scale=1.0))
                            t_mrbfree = sA.tok()
                        else:
                            tz = te - NIB

                            @blk.scalar
                            def _(s, tz=tz, t_mm=t_mm, pz=pz):
                                W(s, t_mm)
                                sA.bumpw(s, s.activation(out=sz[:, tz * LC:(tz + 1) * LC], in_=pz[:], func=AF.Silu,
                                                     bias=0.0, scale=1.0))
                            t_bankfree[te % 2] = sA.tok()
                t_xcz = sA.tok()

                @blk.tensor
                def _(t):
                    W(t, t_xcz, t_wres, sV.tok())
                    for j in range(NIB):
                        i1 = t.matmul(pd1[:], WxB[:, j * 160:j * 160 + 128], xc[:, j * LC:(j + 1) * LC],
                                      start=(j == 0), stop=(j == NIB - 1))
                    sT.bump(i1)
                    for j in range(NIB):
                        i2 = t.matmul(pd2[:], WxB[:, j * 160 + 128:j * 160 + 160], xc[:, j * LC:(j + 1) * LC],
                                      start=(j == 0), stop=(j == NIB - 1))
                    sT.bump(i2)
                t_xd = sT.tok()

                @blk.vector
                def _(v):
                    W(v, t_xd, dSp.tok())
                    sV.bumpw(v, v.tensor_copy(out=dtl[:], in_=pd1[:]))
                    sV.bumpw(v, v.tensor_copy(out=bcsb[:], in_=pd2[:]))
                t_bcev = sV.tok()

                @blk.sync
                def _(sync, ch=ch, t_bcev=t_bcev):
                    W(sync, t_bcev)
                    sync.dma_start(out=d_bc[ch:ch + 1, :].rearrange("a (p c) -> (a p) c", p=32),
                                   in_=bcsb[:]).then_inc(dSp.sem, 16); dSp.n += 16
                t_bcsp = dSp.tok()

                @blk.sync
                def _(sync, ch=ch, t_bcsp=t_bcsp):
                    W(sync, t_bcsp, sV.tok())
                    sync.dma_start(out=Bb[:], in_=d_bc[ch:ch + 1, 0:NST * LC].partition_broadcast(128)
                                   ).then_inc(dBr.sem, 16); dBr.n += 16
                    sync.dma_start(out=Cb[:], in_=d_bc[ch:ch + 1, NST * LC:2 * NST * LC].partition_broadcast(128)
                                   ).then_inc(dBr.sem, 16); dBr.n += 16
                t_bcb = dBr.tok()

                dltP = [dlt, dl2]
                rsbP = [rsb, rs2]
                pzP = [pz0, pz1]
                t_b9v = sV.tok(); t_b9a = sA.tok(); t_b9sp = dSp.tok()
                tsc = [t_b9v, t_b9v]
                tpd = [t_b9a, t_b9a]
                tdafree = [t_b9a, t_b9a]
                tmm = [None, None]
                tsp = [None, None]
                tdl = [None, None]

                def emit_mm(j):
                    par = j % 2

                    @blk.tensor
                    def _(t, j=j, par=par, tw=tpd[par]):
                        W(t, tw, t_bcev, t_xcz)
                        sT.bump(t.matmul(pzP[par][:], WdtB[:, j * 128:(j + 1) * 128], dtl[:], start=True, stop=True))
                    tmm[par] = sT.tok()

                def emit_sp(j, tduj=None):
                    par = j % 2

                    @blk.scalar
                    def _(s, j=j, par=par, tw=tmm[par], tfree=tdl[par], tdaf=tdafree[par], tduj=tduj):
                        W(s, tw, tfree, tdaf, tduj, t_b9v)
                        sA.bumpw(s, s.activation(out=rsbP[par][:], in_=pzP[par][:], func=AF.Abs,
                                             bias=bdt[:, j:j + 1], scale=1.0))
                        sA.bumpw(s, s.activation(out=rsbP[par][:], in_=rsbP[par][:], func=AF.Exp,
                                             bias=0.0, scale=cst[:, 2:3]))
                        sA.bumpw(s, s.activation(out=rsbP[par][:], in_=rsbP[par][:], func=AF.Ln,
                                             bias=cst[:, 3:4], scale=1.0))
                        sA.bumpw(s, s.activation(out=dltP[par][:], in_=pzP[par][:], func=AF.Relu,
                                             bias=bdt[:, j:j + 1], scale=1.0))
                    tsp[par] = sA.tok()
                    tpd[par] = tsp[par]

                def emit_dlt(j):
                    par = j % 2

                    @blk.vector
                    def _(v, par=par, tw=tsp[par]):
                        W(v, tw)
                        sV.bumpw(v, v.tensor_tensor(out=dltP[par][:], in0=dltP[par][:], in1=rsbP[par][:], op=ALU.add))
                    tdl[par] = sV.tok()

                emit_mm(0); emit_sp(0); emit_dlt(0)
                emit_mm(1); emit_sp(1); emit_dlt(1)

                for j in range(NIB):
                    par = j % 2

                    @blk.vector
                    def _(v, j=j, par=par, tw=tdl[par]):
                        W(v, tw, t_xcz)
                        sV.bumpw(v, v.tensor_tensor(out=du[:], in0=dltP[par][:],
                                                in1=xc[:, j * LC:(j + 1) * LC], op=ALU.mult))
                    tdu_j = sV.tok()

                    for h in (0, 1):
                        o = h * 8 * LC

                        @blk.scalar
                        def _(s, j=j, par=par, o=o, tw=tsc[h], tdlj=tdl[par]):
                            W(s, tdlj, tw, t_b9sp)
                            for n in range(8):
                                k = j * 16 + (o // LC) + n
                                ins = s.activation(out=bufA[:, o + n * LC:o + (n + 1) * LC], in_=dltP[par][:],
                                                   func=AF.Exp, bias=0.0, scale=Aneg[:, k:k + 1])
                            sA.bumpw(s, ins)
                        tda_h = sA.tok()
                        if h == 1:
                            tdafree[par] = tda_h

                        @blk.vector
                        def _(v, j=j, h=h, o=o, tda_h=tda_h):
                            W(v, t_bcb)
                            for n in range(8):
                                ins = v.tensor_tensor(out=bufB[:, o + n * LC:o + (n + 1) * LC], in0=du[:],
                                                      in1=Bb[:, o + n * LC:o + (n + 1) * LC], op=ALU.mult)
                            sV.bumpw(v, ins)
                            W(v, tda_h)
                            dA3 = bufA[:, o:o + 8 * LC].rearrange("p (n l) -> p n l", n=8)
                            dB3 = bufB[:, o:o + 8 * LC].rearrange("p (n l) -> p n l", n=8)
                            hs = h * 8
                            sV.bumpw(v, v.tensor_copy(out=r16[:, hs:hs + 8],
                                                  in_=dA3[:, :, 0:1].rearrange("p n o -> p (n o)")))
                            sV.bumpw(v, v.memset(dA3[:, :, 0:1], 0.0))
                            sV.bumpw(v, v.tensor_tensor(out=t16[:, hs:hs + 8], in0=r16[:, hs:hs + 8],
                                                    in1=carry[:, j * NST + hs:j * NST + hs + 8], op=ALU.mult))
                            sV.bumpw(v, v.tensor_tensor(out=dB3[:, :, 0:1], in0=dB3[:, :, 0:1],
                                                    in1=t16[:, hs:hs + 8].rearrange("p (n o) -> p n o", o=1),
                                                    op=ALU.add))
                            sV.bumpw(v, v.tensor_tensor_scan(out=bufH[:, o:o + 8 * LC],
                                                         data0=bufA[:, o:o + 8 * LC],
                                                         data1=bufB[:, o:o + 8 * LC],
                                                         initial=0.0, op0=ALU.mult, op1=ALU.add))
                            st3 = bufH[:, o:o + 8 * LC].rearrange("p (n l) -> p n l", n=8)
                            sV.bumpw(v, v.tensor_copy(out=carry[:, j * NST + hs:j * NST + hs + 8],
                                                  in_=st3[:, :, LC - 1:LC].rearrange("p n o -> p (n o)")))
                        tsc[h] = sV.tok()

                    if j + 2 < NIB:
                        emit_mm(j + 2)
                        emit_sp(j + 2, tdu_j)
                        emit_dlt(j + 2)

                    @blk.vector
                    def _(v, j=j):
                        W(v, t_bcb, t_xcz)
                        sV.bumpw(v, v.tensor_tensor(out=bufH[:], in0=bufH[:], in1=Cb[:], op=ALU.mult))
                        sV.bumpw(v, v.tensor_tensor(out=bufH[:, 0:8 * LC], in0=bufH[:, 0:8 * LC],
                                                in1=bufH[:, 8 * LC:16 * LC], op=ALU.add))
                        sV.bumpw(v, v.tensor_tensor(out=bufH[:, 0:4 * LC], in0=bufH[:, 0:4 * LC],
                                                in1=bufH[:, 4 * LC:8 * LC], op=ALU.add))
                        sV.bumpw(v, v.tensor_tensor(out=bufH[:, 0:2 * LC], in0=bufH[:, 0:2 * LC],
                                                in1=bufH[:, 2 * LC:4 * LC], op=ALU.add))
                        sV.bumpw(v, v.tensor_tensor(out=bufH[:, 0:LC], in0=bufH[:, 0:LC],
                                                in1=bufH[:, LC:2 * LC], op=ALU.add))
                        sV.bumpw(v, v.scalar_tensor_tensor(out=xstg[:], in0=xc[:, j * LC:(j + 1) * LC],
                                                       scalar=Dt[:, j:j + 1], in1=bufH[:, 0:LC],
                                                       op0=ALU.mult, op1=ALU.add))
                        sV.bumpw(v, v.tensor_tensor(out=xc[:, j * LC:(j + 1) * LC], in0=xstg[:],
                                                in1=sz[:, j * LC:(j + 1) * LC], op=ALU.mult))
                t_scan = sV.tok()
                t_aend = sA.tok()

                t_wcons = [t_scan, t_scan]
                t_ofree = [t_scan, t_scan]
                for dm in range(NDB):
                    pz = pz0 if dm % 2 == 0 else pz1
                    t_ws = []
                    for hh in range(2):
                        par = (2 * dm + hh) % 2
                        dWp = dW0 if par == 0 else dW1

                        @blk.sync
                        def _(sync, dm=dm, hh=hh, par=par, dWp=dWp, tc=t_wcons[par]):
                            W(sync, tc)
                            sync.dma_start(out=wsl[par][:],
                                           in_=d_WoutB[hh * 2048:(hh + 1) * 2048, dm * 128:(dm + 1) * 128]
                                           .rearrange("(j p) c -> p j c", p=128)
                                           ).then_inc(dWp.sem, 16); dWp.n += 16
                        t_ws.append(dWp.tok())

                    @blk.tensor
                    def _(t, dm=dm, t_ws=t_ws, tf=t_ofree[dm % 2], pz=pz):
                        W(t, t_ws[0], t_ws[1], tf, t_scan, t_aend)
                        for hh in range(2):
                            par = (2 * dm + hh) % 2
                            for j in range(16):
                                jj = hh * 16 + j
                                ins = t.matmul(pz[:], wsl[par][:, j * 128:(j + 1) * 128],
                                               xc[:, jj * LC:(jj + 1) * LC],
                                               start=(hh == 0 and j == 0), stop=(hh == 1 and j == 15))
                        sT.bump(ins)
                    t_mm = sT.tok()
                    t_wcons[0] = t_mm
                    t_wcons[1] = t_mm

                    @blk.vector
                    def _(v, dm=dm, t_mm=t_mm, pz=pz):
                        W(v, t_mm)
                        sV.bumpw(v, v.tensor_scalar(out=mrb[:], in0=pz[:],
                                                scalar1=gaf[:, dm:dm + 1], scalar2=gab[:, dm:dm + 1],
                                                op0=ALU.mult, op1=ALU.add))
                        sV.bumpw(v, v.tensor_tensor(out=sz[:, dm * LC:(dm + 1) * LC], in0=mrb[:],
                                                in1=xs[:, dm * LC:(dm + 1) * LC], op=ALU.add))
                    t_ofree[dm % 2] = sV.tok()
                t_x2 = sV.tok()

                for dm in range(NDB):
                    for rt in range(RT):
                        @blk.tensor
                        def _(t, dm=dm, rt=rt):
                            W(t, t_x2, sV.tok())
                            sT.bump(t.transpose(ptr[:], sz[:, dm * LC + rt * 128:dm * LC + rt * 128 + 128], idn[:]))
                        t_tp = sT.tok()

                        @blk.vector
                        def _(v, dm=dm, rt=rt, t_tp=t_tp):
                            W(v, t_tp)
                            sV.bumpw(v, v.tensor_copy(out=bufA[:, rt * DM + dm * 128:rt * DM + dm * 128 + 128], in_=ptr[:]))
                t_x2t = sV.tok()

                @blk.sync
                def _(sync, ch=ch, t_x2t=t_x2t):
                    W(sync, t_x2t)
                    sync.dma_start(out=d_x2[ch * LC:(ch + 1) * LC, :].rearrange("(rt p) c -> p rt c", p=128),
                                   in_=bufA[:]).then_inc(dSp.sem, 16); dSp.n += 16

            @blk.sync
            def _(sync):
                W(sync, dSp.tok(), sV.tok(), sA.tok(), sT.tok(), dBr.tok())

        esC = contextlib.ExitStack()
        with esC:
            def SB(name, shape, dt):
                return esC.enter_context(nc.sbuf_tensor(name, shape, dt))

            x2tc = SB("x2tc", [128, RT * DM], BF16)
            qx = SB("qx", [128, RT * DM], BF16)
            qxT = SB("qxT", [128, NDB * LC], BF16)
            gsl = [SB("gsl0", [128, NDB * 512], BF16), SB("gsl1", [128, NDB * 512], BF16)]
            usl = [SB("usl0", [128, NDB * 512], BF16), SB("usl1", [128, NDB * 512], BF16)]
            qts = gsl[1]
            wds = usl[1]
            h2 = SB("h2", [128, DM], F32)
            c1s = SB("c1s", [128, DM], F32)
            c2s = SB("c2s", [128, DM], F32)
            c3s = SB("c3s", [128, DM], F32)
            c4s = SB("c4s", [128, DM], F32)
            t0w = SB("t0w", [128, 1408], BF16)
            tqw = SB("tqw", [128, 1408], BF16)
            tqf = SB("tqf", [128, 1408], F32)
            tqf2 = SB("tqf2", [128, 1408], F32)
            tqf3 = SB("tqf3", [128, 1408], F32)
            tqf4 = SB("tqf4", [128, 1408], F32)
            ost = [SB("ost0", [128, 512], F32), SB("ost1", [128, 512], F32)]
            rmax = SB("rmaxC", [128, 8], F32)
            sxr = SB("sxr", [128, RT], F32)
            dqr = SB("dqr", [128, RT], F32)
            t0t = SB("t0t", [128, 512], BF16)
            cb1T = SB("cb1T", [128, DM], BF16)
            sbT = SB("sbT", [128, DM], BF16)
            gbT = SB("gbT", [128, DM], BF16)

            pg = esC.enter_context(nc.psum_tensor("pgC", [128, 512], F32))
            pu = esC.enter_context(nc.psum_tensor("puC", [128, 512], F32))
            pdn = [esC.enter_context(nc.psum_tensor("pdC%d" % i, [128, 512], F32)) for i in range(4)]
            ptr2 = esC.enter_context(nc.psum_tensor("ptrC", [128, 128], BF16))

            blk = esC.enter_context(nc.Block())

            @blk.sync
            def _(sync):
                sync.dma_start(out=h2[:], in_=d_modrow[0:1, 3 * DM:4 * DM].partition_broadcast(128)
                               ).then_inc(dBr.sem, 16); dBr.n += 16
                sync.dma_start(out=c1s[:], in_=d_modrow[0:1, 4 * DM:5 * DM].partition_broadcast(128)
                               ).then_inc(dBr.sem, 16); dBr.n += 16
                sync.dma_start(out=c2s[:], in_=d_modrow[0:1, 5 * DM:6 * DM].partition_broadcast(128)
                               ).then_inc(dBr.sem, 16); dBr.n += 16
            t_mbr = dBr.tok()

            @blk.vector
            def _(v):
                W(v, t_mbr)
                sV.bumpw(v, v.tensor_copy(out=sbT[:], in_=h2[:]))
                sV.bumpw(v, v.tensor_scalar(out=cb1T[:], in0=c1s[:], scalar1=1.0, scalar2=None, op0=ALU.add))
                sV.bumpw(v, v.tensor_scalar(out=h2[:], in0=c2s[:], scalar1=0.0, scalar2=None, op0=ALU.is_ge))
                sV.bumpw(v, v.tensor_scalar(out=h2[:], in0=h2[:], scalar1=2.0, scalar2=-1.0, op0=ALU.mult, op1=ALU.add))
                sV.bumpw(v, v.tensor_scalar(out=c1s[:], in0=c2s[:], scalar1=-1.0, scalar2=None, op0=ALU.mult))
                sV.bumpw(v, v.tensor_tensor(out=c1s[:], in0=c2s[:], in1=c1s[:], op=ALU.max))
                sV.bumpw(v, v.tensor_scalar(out=c1s[:], in0=c1s[:], scalar1=0.25, scalar2=None, op0=ALU.max))
                sV.bumpw(v, v.tensor_tensor(out=gbT[:], in0=h2[:], in1=c1s[:], op=ALU.mult))
            t_modT = sV.tok()


            t_outprev = dOu.tok()
            ost_tok = [dOu.tok(), dOu.tok()]
            for ch in range(NCH):
                @blk.sync
                def _(sync, ch=ch):
                    W(sync, sV.tok(), sT.tok())
                    if ch == 0:
                        sync.wait_ge(ccGg, 1)
                        sync.wait_ge(ccGu, 1)
                        sync.wait_ge(ccGd, 1)
                    sync.dma_start(out=x2tc[:],
                                   in_=d_x2[ch * LC:(ch + 1) * LC, :].rearrange("(rt p) c -> p rt c", p=128)
                                   ).then_inc(dIn.sem, 16); dIn.n += 16
                t_x2l = dIn.tok()

                for rt in range(RT):
                    @blk.vector
                    def _(v, rt=rt, t_x2l=t_x2l):
                        W(v, t_x2l, t_modT, sA.tok())
                        xrt = x2tc[:, rt * DM:(rt + 1) * DM]
                        sV.bumpw(v, v.tensor_reduce(out=rmax[:, 0:1], in_=xrt, axis=AX.X, op=ALU.add))
                        sV.bumpw(v, v.tensor_scalar(out=rmax[:, 0:1], in0=rmax[:, 0:1], scalar1=1.0 / DM, scalar2=None, op0=ALU.mult))
                    t_mean = sV.tok()

                    @blk.scalar
                    def _(s, rt=rt, t_mean=t_mean):
                        W(s, t_mean)
                        sA.bumpw(s, s.activation(out=c1s[:], in_=x2tc[:, rt * DM:(rt + 1) * DM], func=AF.Square,
                                             bias=0.0, scale=1.0))
                    t_sq2 = sA.tok()

                    @blk.vector
                    def _(v, rt=rt, t_sq2=t_sq2):
                        W(v, t_sq2)
                        sV.bumpw(v, v.tensor_reduce(out=rmax[:, 1:2], in_=c1s[:], axis=AX.X, op=ALU.add))
                        sV.bumpw(v, v.tensor_scalar(out=rmax[:, 1:2], in0=rmax[:, 1:2], scalar1=1.0 / DM, scalar2=None, op0=ALU.mult))
                        sV.bumpw(v, v.tensor_tensor(out=rmax[:, 2:3], in0=rmax[:, 0:1], in1=rmax[:, 0:1], op=ALU.mult))
                        sV.bumpw(v, v.tensor_tensor(out=rmax[:, 1:2], in0=rmax[:, 1:2], in1=rmax[:, 2:3], op=ALU.subtract))
                    t_va2 = sV.tok()

                    @blk.scalar
                    def _(s, rt=rt, t_va2=t_va2):
                        W(s, t_va2)
                        sA.bumpw(s, s.activation(out=rmax[:, 1:2], in_=rmax[:, 1:2], func=AF.Ln, bias=cst[:, 0:1], scale=1.0))
                        sA.bumpw(s, s.activation(out=rmax[:, 1:2], in_=rmax[:, 1:2], func=AF.Exp, bias=0.0, scale=cst[:, 1:2]))
                    t_rs2 = sA.tok()

                    @blk.vector
                    def _(v, rt=rt, t_rs2=t_rs2):
                        W(v, t_rs2)
                        xrt = x2tc[:, rt * DM:(rt + 1) * DM]
                        sV.bumpw(v, v.tensor_scalar(out=h2[:], in0=xrt, scalar1=rmax[:, 0:1], scalar2=rmax[:, 1:2],
                                                op0=ALU.subtract, op1=ALU.mult))
                        sV.bumpw(v, v.tensor_tensor(out=h2[:], in0=h2[:], in1=cb1T[:], op=ALU.mult))
                        sV.bumpw(v, v.tensor_tensor(out=h2[:], in0=h2[:], in1=sbT[:], op=ALU.add))
                        sV.bumpw(v, v.tensor_reduce(out=sxr[:, rt:rt + 1], in_=h2[:], axis=AX.X, op=ALU.max,
                                                apply_absolute_value=True))
                        sV.bumpw(v, v.reciprocal(out=dqr[:, rt:rt + 1], in_=sxr[:, rt:rt + 1]))
                        sV.bumpw(v, v.tensor_scalar(out=dqr[:, rt:rt + 1], in0=dqr[:, rt:rt + 1],
                                                scalar1=448.0, scalar2=None, op0=ALU.mult))
                        sV.bumpw(v, v.tensor_scalar(out=sxr[:, rt:rt + 1], in0=sxr[:, rt:rt + 1],
                                                scalar1=1.0 / 448.0, scalar2=None, op0=ALU.mult))
                    t_h2 = sV.tok()

                    emit_quant_pure(blk, sV, sA,
                               src=h2[:], scale_ap=dqr[:, rt:rt + 1], dq_ap=sxr[:, rt:rt + 1],
                               scr1=c1s[:], scr2=c2s[:], scr3=c3s[:], scr4=c4s[:],
                               out_bf=qx[:, rt * DM:(rt + 1) * DM],
                               cstref=cst, pre_toks=(t_h2,))
                t_qx = sV.tok()

                for rt in range(RT):
                    for j in range(NDB):
                        @blk.tensor
                        def _(t, rt=rt, j=j):
                            W(t, t_qx, sV.tok())
                            sT.bump(t.transpose(ptr2[:], qx[:, rt * DM + j * 128:rt * DM + j * 128 + 128], idn[:]))
                        t_tp = sT.tok()

                        @blk.vector
                        def _(v, rt=rt, j=j, t_tp=t_tp):
                            W(v, t_tp)
                            sV.bumpw(v, v.tensor_copy(out=qxT[:, j * LC + rt * 128:j * LC + rt * 128 + 128], in_=ptr2[:]))
                t_qxT = sV.tok()

                @blk.vector
                def _(v):
                    W(v, t_qxT)
                    sV.bumpw(v, v.memset(rmax[:, 0:RT], 0.0))
                t_rm0 = sV.tok()

                t_slabC = [sV.tok(), sV.tok()]
                for ic in range(NIC):
                    par = ic % 2
                    dWp = dW0 if par == 0 else dW1

                    @blk.sync
                    def _(sync, ic=ic, par=par, dWp=dWp, tc=t_slabC[par]):
                        W(sync, tc)
                        sync.dma_start(out=gsl[par][:],
                                       in_=d_QG[:, ic * 512:(ic + 1) * 512].rearrange("(j p) c -> p j c", p=128)
                                       ).then_inc(dWp.sem, 16); dWp.n += 16
                        sync.dma_start(out=usl[par][:],
                                       in_=d_QU[:, ic * 512:(ic + 1) * 512].rearrange("(j p) c -> p j c", p=128)
                                       ).then_inc(dWp.sem, 16); dWp.n += 16
                    t_wg = dWp.tok()

                    for rt in range(RT):
                        @blk.tensor
                        def _(t, ic=ic, rt=rt, par=par, t_wg=t_wg):
                            W(t, t_wg, t_qxT, sV.tok(), sA.tok())
                            for j in range(NDB):
                                ig = t.matmul(pg[:], qxT[:, j * LC + rt * 128:j * LC + rt * 128 + 128],
                                              gsl[par][:, j * 512:(j + 1) * 512], start=(j == 0), stop=(j == NDB - 1))
                            sT.bump(ig)
                            for j in range(NDB):
                                iu = t.matmul(pu[:], qxT[:, j * LC + rt * 128:j * LC + rt * 128 + 128],
                                              usl[par][:, j * 512:(j + 1) * 512], start=(j == 0), stop=(j == NDB - 1))
                            sT.bump(iu)
                        t_guv = sT.tok()
                        if rt == RT - 1:
                            t_slabC[par] = t_guv

                        @blk.scalar
                        def _(s, ic=ic, rt=rt, t_guv=t_guv):
                            W(s, t_guv, sV.tok(), dT0.tok())
                            sA.bumpw(s, s.activation(out=t0t[:], in_=pg[:], func=AF.Silu, bias=0.0, scale=1.0))
                        t_sg2 = sA.tok()

                        @blk.vector
                        def _(v, ic=ic, rt=rt, t_sg2=t_sg2):
                            W(v, t_sg2)
                            sV.bumpw(v, v.tensor_tensor(out=t0t[:], in0=t0t[:], in1=pu[:], op=ALU.mult))
                            sV.bumpw(v, v.tensor_reduce(out=rmax[:, 4:5], in_=t0t[:], axis=AX.X, op=ALU.max,
                                                    apply_absolute_value=True))
                            sV.bumpw(v, v.tensor_tensor(out=rmax[:, rt:rt + 1], in0=rmax[:, rt:rt + 1],
                                                    in1=rmax[:, 4:5], op=ALU.max))
                        t_t0 = sV.tok()

                        @blk.sync
                        def _(sync, ic=ic, rt=rt, t_t0=t_t0):
                            W(sync, t_t0)
                            sync.dma_start(out=d_t0[rt * 128:(rt + 1) * 128, ic * 512:(ic + 1) * 512],
                                           in_=t0t[:]).then_inc(dT0.sem, 16); dT0.n += 16
                t_t0all = dT0.tok()

                @blk.vector
                def _(v):
                    W(v, sV.tok())
                    sV.bumpw(v, v.reciprocal(out=dqr[:], in_=rmax[:, 0:RT]))
                    sV.bumpw(v, v.tensor_scalar(out=dqr[:], in0=dqr[:], scalar1=448.0, scalar2=None, op0=ALU.mult))
                    sV.bumpw(v, v.tensor_scalar(out=sxr[:], in0=rmax[:, 0:RT], scalar1=1.0 / 448.0, scalar2=None, op0=ALU.mult))
                t_tsc = sV.tok()

                t_guall = sT.tok()
                t_dnld = t_guall
                qtt = gsl[0]
                t_qt_g = []
                for qg_ in range(4):
                    for rt in range(RT):
                        @blk.sync
                        def _(sync, rt=rt, qg_=qg_):
                            W(sync, t_t0all, sV.tok(), sA.tok(), dQt.tok())
                            sync.dma_start(out=t0w[:], in_=d_t0[rt * 128:(rt + 1) * 128, qg_ * 1408:(qg_ + 1) * 1408]
                                           ).then_inc(dIn.sem, 16); dIn.n += 16
                        t_tl = dIn.tok()
                        emit_quant_pure(blk, sV, sA,
                                   src=t0w[:], scale_ap=dqr[:, rt:rt + 1], dq_ap=sxr[:, rt:rt + 1],
                                   scr1=tqf[:], scr2=tqf2[:], scr3=tqf3[:], scr4=tqf4[:],
                                   out_bf=tqw[:], cstref=cst, pre_toks=(t_tl, t_tsc))
                        t_q2 = sV.tok()

                        for k in range(11):
                            @blk.tensor
                            def _(t, k=k, t_q2=t_q2):
                                W(t, t_q2, sV.tok())
                                sT.bump(t.transpose(ptr2[:], tqw[:, k * 128:(k + 1) * 128], idn[:]))
                            t_tp2 = sT.tok()

                            @blk.vector
                            def _(v, k=k, t_tp2=t_tp2):
                                W(v, t_tp2)
                                sV.bumpw(v, v.tensor_copy(out=qtt[:, k * 128:(k + 1) * 128], in_=ptr2[:]))
                        t_as = sV.tok()

                        @blk.sync
                        def _(sync, rt=rt, qg_=qg_, t_as=t_as):
                            W(sync, t_as)
                            sync.dma_start(out=d_qt[qg_ * 1408:(qg_ + 1) * 1408, rt * 128:(rt + 1) * 128]
                                           .rearrange("(k p) c -> p k c", p=128),
                                           in_=qtt[:, 0:1408]).then_inc(dQt.sem, 16); dQt.n += 16
                    t_qt_g.append(dQt.tok())
                t_qtall = dQt.tok()

                for dmc in range(4):
                    t_dn_ev = sV.tok()
                    _unused = t_outprev
                    for ig in range(4):
                        @blk.sync
                        def _(sync, dmc=dmc, ig=ig, t_dnld=t_dnld):
                            W(sync, t_qt_g[ig], t_dnld)
                            sync.dma_start(out=qts[:, 0:11 * 512],
                                           in_=d_qt[ig * 1408:(ig + 1) * 1408, :].rearrange("(b p) c -> p b c", p=128)
                                           ).then_inc(dW0.sem, 16); dW0.n += 16
                            sync.dma_start(out=wds[:, 0:11 * 512],
                                           in_=d_QD[ig * 1408:(ig + 1) * 1408, dmc * 512:(dmc + 1) * 512]
                                           .rearrange("(b p) c -> p b c", p=128)
                                           ).then_inc(dW0.sem, 16); dW0.n += 16
                        t_wd = dW0.tok()

                        @blk.tensor
                        def _(t, dmc=dmc, ig=ig, t_wd=t_wd, t_dn_ev=t_dn_ev):
                            W(t, t_wd, t_dn_ev)
                            for ib in range(11):
                                for rt in range(RT):
                                    ins = t.matmul(pdn[rt][:],
                                                   qts[:, ib * 512 + rt * 128:ib * 512 + rt * 128 + 128],
                                                   wds[:, ib * 512:(ib + 1) * 512],
                                                   start=(ig == 0 and ib == 0), stop=(ig == 3 and ib == 10))
                            sT.bump(ins)
                        t_dnld = sT.tok()
                    t_dn = sT.tok()

                    for rt in range(RT):
                        par = (dmc * RT + rt) % 2

                        @blk.vector
                        def _(v, dmc=dmc, rt=rt, par=par, t_dn=t_dn, t_x2l=t_x2l, tprev=ost_tok[par]):
                            W(v, t_dn, t_x2l, tprev, t_modT)
                            sV.bumpw(v, v.tensor_tensor(out=tqf[:, 0:512], in0=pdn[rt][:],
                                                    in1=gbT[:, dmc * 512:(dmc + 1) * 512], op=ALU.mult))
                            sV.bumpw(v, v.tensor_tensor(out=ost[par][:],
                                                    in0=tqf[:, 0:512],
                                                    in1=x2tc[:, rt * DM + dmc * 512:rt * DM + dmc * 512 + 512],
                                                    op=ALU.add))
                        t_pev = sV.tok()

                        @blk.sync
                        def _(sync, ch=ch, dmc=dmc, rt=rt, par=par, t_pev=t_pev):
                            W(sync, t_pev)
                            sync.dma_start(out=d_out[ch * LC + rt * 128:ch * LC + (rt + 1) * 128,
                                                     dmc * 512:(dmc + 1) * 512],
                                           in_=ost[par][:]).then_inc(dOu.sem, 16); dOu.n += 16
                        ost_tok[par] = dOu.tok()
                t_outprev = dOu.tok()
                t_slabC[0] = t_qtall
                t_slabC[1] = sT.tok()

            @blk.sync
            def _(sync):
                W(sync, dOu.tok(), sV.tok(), sA.tok(), sT.tok())

    return nc


def _vec_tiles(v, width=32):
    return np.ascontiguousarray(v.reshape(width, 128).T)


def kernel(**inputs):
    x = np.asarray(inputs["L0_attn_hidden_states"], np.float32)
    W_in = np.asarray(inputs["L0_attn_W_in"], np.float32)
    conv_w = np.asarray(inputs["L0_attn_conv_w"], np.float32)
    conv_b = np.asarray(inputs["L0_attn_conv_b"], np.float32)
    W_x = np.asarray(inputs["L0_attn_W_x"], np.float32)
    W_dt = np.asarray(inputs["L0_attn_W_dt"], np.float32)
    b_dt = np.asarray(inputs["L0_attn_b_dt"], np.float32)
    A_log = np.asarray(inputs["L0_attn_A_log"], np.float32)
    Dv = np.asarray(inputs["L0_attn_D"], np.float32)
    W_out = np.asarray(inputs["L0_attn_W_out"], np.float32)
    b_out = np.asarray(inputs["L0_attn_b_out"], np.float32)
    w_gate = np.asarray(inputs["L0_ffn_w_gate"], np.float32)
    w_up = np.asarray(inputs["L0_ffn_w_up"], np.float32)
    w_down = np.asarray(inputs["L0_ffn_w_down"], np.float32)
    up_idx = np.asarray(inputs["L0_ffn_up_reorder_index"])
    dn_idx = np.asarray(inputs["L0_ffn_down_reorder_index"])
    cvec = np.asarray(inputs["L0_c"], np.float32)
    wmod = np.asarray(inputs["L0_wmod"], np.float32)

    assert up_idx.shape == (DM,)
    pinv = np.argsort(dn_idx)
    w_down_p = w_down[:, pinv]

    shared = {
        "wmod": np.ascontiguousarray(wmod),
        "convw": np.ascontiguousarray(conv_w.reshape(32, 128, 4).transpose(1, 0, 2).reshape(128, 128)),
        "convb": _vec_tiles(conv_b),
        "bdt": _vec_tiles(b_dt),
        "Alog": np.ascontiguousarray(A_log.reshape(32, 128, 16).transpose(1, 0, 2).reshape(128, 512)),
        "Dt": _vec_tiles(Dv),
        "bout": _vec_tiles(b_out, width=16),
        "WxT": np.ascontiguousarray(W_x.T),
        "WdtT": np.ascontiguousarray(W_dt.T),
    }
    WinT = np.ascontiguousarray(W_in.T)
    WoutT = np.ascontiguousarray(W_out.T)
    wgT = np.ascontiguousarray(w_gate.T)
    wuT = np.ascontiguousarray(w_up.T)
    wdPT = np.ascontiguousarray(w_down_p.T)

    nc = build_program()
    in_maps = []
    for c in range(8):
        m = dict(shared)
        m["xT"] = np.ascontiguousarray(x[c].T)
        m["cvecT"] = _vec_tiles(cvec[c], width=16)
        m["WinS"] = np.ascontiguousarray(WinT[c * (DM // 8):(c + 1) * (DM // 8), :])
        m["WoutS"] = np.ascontiguousarray(WoutT[c * (DI // 8):(c + 1) * (DI // 8), :])
        m["wgS"] = np.ascontiguousarray(wgT[c * (DM // 8):(c + 1) * (DM // 8), :])
        m["wuS"] = np.ascontiguousarray(wuT[c * (DM // 8):(c + 1) * (DM // 8), :])
        m["wdS"] = np.ascontiguousarray(wdPT[c * (IFF // 8):(c + 1) * (IFF // 8), :])
        in_maps.append(m)

    res = run_bass_kernel_spmd(nc, in_maps, core_ids=list(range(8)))
    out = np.stack([np.asarray(r["out"], np.float32) for r in res.results], axis=0)
    return out
